# version 07
# baseline (speedup 1.0000x reference)
import numpy as np
from collections import defaultdict
from contextlib import ExitStack

import concourse.bass as bass
import concourse.mybir as mybir
from concourse.bass_utils import run_bass_kernel_spmd

import concourse.bass_utils as _bu
if not getattr(_bu, "_ldw_patched", False):
    _orig_run_command = _bu.run_command

    def _run_command_ldw(cmd, *a, **k):
        try:
            cmd = ["--enable-ldw-opt=true" if c == "--enable-ldw-opt=false" else c for c in cmd]
        except TypeError:
            pass
        return _orig_run_command(cmd, *a, **k)

    _bu.run_command = _run_command_ldw
    _bu._ldw_patched = True

F32 = mybir.dt.float32
BF16 = mybir.dt.bfloat16
AF = mybir.ActivationFunctionType
ALU = mybir.AluOpType
AX = mybir.AxisListType

NCORE = 8
T = 1024
HFN = 2
HF = 512
DC = 16
IC = 11
NE = 16
SGN = 4
EPS = 1e-5
ENGINES = ("sync", "tensor", "scalar", "vector", "gpsimd")


class Rec:

    def __init__(self):
        self.q = {e: [] for e in ENGINES}
        self.cnt = defaultdict(int)
        self.waited = defaultdict(int)
        self.released = {}
        self.n = defaultdict(int)

    def op(self, eng, attr, *a, inc=None, incby=1, **k):
        rec = [attr, a, k, None]
        self.q[eng].append(rec)
        self.n[eng] += 1
        if inc is not None:
            self.cnt[inc] += incby
            rec[3] = (inc, incby)
            return self.cnt[inc]
        return None

    def wait(self, eng, sem, val):
        if val is None or val <= 0:
            return
        if self.waited[(eng, sem)] >= val:
            return
        self.waited[(eng, sem)] = val
        self.q[eng].append(["wait_ge", (("SEM", sem), val), {}, None])
        self.n[eng] += 1

    def dma(self, eng, out, in_, sem):
        v = self.op(eng, "dma_start", inc=sem, incby=16, out=out, in_=in_)
        return (sem, v)

    def waitv(self, eng, sv):
        self.wait(eng, sv[0], sv[1])

    def consume(self, key, sem, val):
        self.released.setdefault(key, []).append((sem, val))

    def acquire(self, eng, key):
        for sem, val in self.released.pop(key, []):
            self.wait(eng, sem, val)

    def replay(self, eng_obj, eng_name, sems):
        for attr, a, k, then in self.q[eng_name]:
            a = [sems[x[1]] if isinstance(x, tuple) and len(x) == 2 and x[0] == "SEM" else x
                 for x in a]
            r = getattr(eng_obj, attr)(*a, **k)
            if then is not None:
                r.then_inc(sems[then[0]], then[1])


def build_program(upto="full", taps=()):
    nc = bass.Bass("TRN2", target_bir_lowering=False, debug=False)

    d_xfm = nc.dram_tensor("xfm", [2048, T], F32, kind="ExternalInput").ap()
    d_wdt = nc.dram_tensor("wdt", [DC, 128, 2048], F32, kind="ExternalInput").ap()
    d_wb = nc.dram_tensor("wb", [DC, 128, 2048], F32, kind="ExternalInput").ap()
    d_wc = nc.dram_tensor("wc", [DC, 128, 2048], F32, kind="ExternalInput").ap()
    d_gw = nc.dram_tensor("gw", [NE, IC, 128, 2048], F32, kind="ExternalInput").ap()
    d_uw = nc.dram_tensor("uw", [NE, IC, 128, 2048], F32, kind="ExternalInput").ap()
    d_dw = nc.dram_tensor("dw", [NE, DC, 128, 1408], F32, kind="ExternalInput").ap()
    d_sgw = nc.dram_tensor("sgw", [SGN, IC, 128, 2048], F32, kind="ExternalInput").ap()
    d_suw = nc.dram_tensor("suw", [SGN, IC, 128, 2048], F32, kind="ExternalInput").ap()
    d_sdw = nc.dram_tensor("sdw", [DC, 128, 5632], F32, kind="ExternalInput").ap()
    d_rw = nc.dram_tensor("rw", [128, 256], F32, kind="ExternalInput").ap()
    d_sew = nc.dram_tensor("sew", [128, 16], F32, kind="ExternalInput").ap()
    d_bdt = nc.dram_tensor("bdt", [128, 16], F32, kind="ExternalInput").ap()
    d_alog = nc.dram_tensor("alog", [128, 16], F32, kind="ExternalInput").ap()
    d_dskip = nc.dram_tensor("dskip", [128, 16], F32, kind="ExternalInput").ap()
    d_sel = nc.dram_tensor("sel", [128, 8], F32, kind="ExternalInput").ap()
    d_idn = nc.dram_tensor("idn", [128, 128], F32, kind="ExternalInput").ap()
    d_ones1 = nc.dram_tensor("ones1", [1, 128], F32, kind="ExternalInput").ap()
    d_onesc = nc.dram_tensor("onesc", [128, 1], F32, kind="ExternalInput").ap()
    d_cst = nc.dram_tensor("cst", [128, 4], F32, kind="ExternalInput").ap()

    d_ccin = nc.dram_tensor("cc_in", [128, 32], F32, kind="Internal").ap()
    d_ccout = nc.dram_tensor("cc_out", [128 * NCORE, 32], F32, kind="Internal").ap()
    d_r1d = nc.dram_tensor("r1d", [128, DC * T], F32, kind="Internal").ap()
    d_y = nc.dram_tensor("y", [2048, T], F32, kind="ExternalOutput").ap()

    tap_specs = {}

    with ExitStack() as ctx:
        sb = lambda name, shape, dt: ctx.enter_context(nc.sbuf_tensor(name, shape, dt))
        xb_t = sb("xb", [128, DC * T], BF16)
        U1_t = sb("U1", [128, DC * T], F32)
        U2_t = sb("U2", [128, DC * T], F32)
        wdtb_t = sb("wdtb", [128, 2 * 2048], BF16)
        wbb_t = sb("wbb", [128, 2 * 2048], BF16)
        xs_t = sb("xs", [128, 2 * T], F32)
        sil_t = sb("sil", [128, T], F32)
        rows8_t = sb("rows8", [1, 2048], F32)
        srow_t = sb("srow", [1, T], F32)
        probs_t = sb("probs", [128, 128], F32)
        pr2_t = sb("pr2", [128, 128], F32)
        wtm_t = sb("wtm", [128, 128], F32)
        m8_t = sb("m8", [128, 8], F32)
        msk_t = sb("msk", [128, 128], F32)
        ssum8_t = sb("ssum8", [128, 8], F32)
        rcp8_t = sb("rcp8", [128, 8], F32)
        m8_8_t = sb("m8_8", [128, 64], F32)
        nmx_t = sb("nmx", [128, 1], F32)
        ssum_t = sb("ssum", [128, 1], F32)
        rcp_t = sb("rcp", [128, 1], F32)
        hP_t = sb("hP", [128, 32], F32)
        carr_t = sb("carr", [128, 256], F32)
        cvec_t = sb("cvec", [128, 16], F32)
        ct1_t = sb("ct1", [128, 16], F32)
        bdt_t = sb("bdt_s", [128, 16], F32)
        alog_t = sb("alog_s", [128, 16], F32)
        dskip_t = sb("dskip_s", [128, 16], F32)
        A_t = sb("A_s", [128, 16], F32)
        ds1_t = sb("ds1_s", [128, 16], F32)
        sel_t = sb("sel_s", [128, 8], F32)
        idn_t = sb("idn_s", [128, 128], F32)
        ones1_t = sb("ones1_s", [1, 128], F32)
        onesc_t = sb("onesc_s", [128, 1], F32)
        onescb_t = sb("onescb_s", [128, 1], BF16)
        cst_t = sb("cst_s", [128, 4], F32)
        rwb_t = sb("rwb", [128, 256], BF16)
        seb_t = sb("seb", [128, 16], BF16)

        pb = [ctx.enter_context(nc.psum_tensor(f"pb{i}", [128, 512], F32)) for i in range(8)]

        semnames = ("SM", "XB", "RW", "WDT0", "WDT1", "WBB0", "WBB1", "XS0", "XS1",
                    "GW0", "GW1", "UW0", "UW1", "DW0", "DW1", "SP", "WR0", "WR1",
                    "CCS", "Y", "CC", "PE", "ACT", "DVE", "PL")
        sems = {n: ctx.enter_context(nc.semaphore(n)) for n in semnames}
        block = ctx.enter_context(nc.Block())

        xb = xb_t[:, :]
        hb = U1_t.bitcast(BF16)
        accv = U1_t
        U2 = U2_t
        heb = U2_t.bitcast(BF16)
        slot = lambda s: U2[:, s * T:(s + 1) * T]
        w_bc = U2[:, 7040:8064]
        s_bc = U2[:, 8064:9088]
        tscr = U2[:, 9088:11136]
        zeros = sil_t[:, 0:T]
        lg_s = xs_t[0:16, 0:T]
        wfm = xs_t[0:16, T:2 * T]

        r = Rec()

        def TAP(name, ap):
            if name in taps:
                tap_specs[name] = ap

        r.dma("sync", bdt_t[:, :], d_bdt, "SM")
        r.dma("sync", alog_t[:, :], d_alog, "SM")
        r.dma("sync", dskip_t[:, :], d_dskip, "SM")
        r.dma("sync", sel_t[:, :], d_sel, "SM")
        r.dma("sync", idn_t[:, :], d_idn, "SM")
        r.dma("sync", ones1_t[:, :], d_ones1, "SM")
        r.dma("sync", onesc_t[:, :], d_onesc, "SM")
        v_smalls = r.dma("sync", cst_t[:, :], d_cst, "SM")

        for j in range(DC):
            v_xb = r.dma("gpsimd", xb_t[:, j * T:(j + 1) * T], d_xfm[j * 128:(j + 1) * 128, :], "XB")
        r.dma("gpsimd", onescb_t[:, :], d_onesc, "RW")
        r.dma("gpsimd", rwb_t[:, :], d_rw, "RW")
        v_seb = r.dma("gpsimd", seb_t[:, :], d_sew, "RW")

        r.waitv("scalar", v_smalls)
        v_atmp = r.op("scalar", "activation", A_t[:, :], alog_t[:, :], AF.Exp, inc="ACT")
        r.wait("vector", "ACT", v_atmp)
        v_A = r.op("vector", "tensor_scalar_mul", A_t[:, :], A_t[:, :], -1.0, inc="DVE")
        r.waitv("vector", v_smalls)
        v_ds1 = r.op("vector", "tensor_scalar_add", ds1_t[:, :], dskip_t[:, :], 1.0, inc="DVE")
        v_zero = r.op("vector", "memset", zeros, 0.0, inc="DVE")

        wload = {}
        for j0 in range(2):
            wload[("wdt", j0)] = r.dma("gpsimd", wdtb_t[:, (j0 % 2) * 2048:(j0 % 2 + 1) * 2048], d_wdt[j0], f"WDT{j0 % 2}")
            wload[("wb", j0)] = r.dma("gpsimd", wbb_t[:, (j0 % 2) * 2048:(j0 % 2 + 1) * 2048], d_wb[j0], f"WBB{j0 % 2}")

        v_scan_last = None
        for j in range(DC):
            p = j % 2
            wdt_tile = lambda c, p=p: wdtb_t[:, p * 2048 + c * 128: p * 2048 + (c + 1) * 128]
            wb_tile = lambda c, p=p: wbb_t[:, p * 2048 + c * 128: p * 2048 + (c + 1) * 128]
            s_dt, s_ab, s_l, s_r, s_a, s_e, s_hl, s_pq = (slot(0 + p), slot(2 + p), slot(4 + p),
                                                          slot(6 + p), slot(8 + p), slot(10 + p),
                                                          slot(12 + p), slot(14 + p))
            r.acquire("sync", f"xs{p}")
            v_xs = r.dma("sync", xs_t[:, p * T:(p + 1) * T], d_xfm[j * 128:(j + 1) * 128, :], f"XS{p}")

            r.waitv("tensor", v_xb)
            r.waitv("tensor", wload[("wdt", j)])
            r.waitv("tensor", wload[("wb", j)])
            v_dt_g, v_bp_g = {}, {}
            for h in range(HFN):
                r.acquire("tensor", f"pb{p * 4 + h}")
                for c in range(DC):
                    r.op("tensor", "matmul", pb[p * 4 + h][:, :], wdt_tile(c),
                         xb[:, c * T + h * HF: c * T + (h + 1) * HF],
                         start=(c == 0), stop=(c == DC - 1), inc="PE" if c == DC - 1 else None)
                v_dt_g[h] = r.cnt["PE"]
            for h in range(HFN):
                r.acquire("tensor", f"pb{p * 4 + 2 + h}")
                for c in range(DC):
                    r.op("tensor", "matmul", pb[p * 4 + 2 + h][:, :], wb_tile(c),
                         xb[:, c * T + h * HF: c * T + (h + 1) * HF],
                         start=(c == 0), stop=(c == DC - 1), inc="PE" if c == DC - 1 else None)
                v_bp_g[h] = r.cnt["PE"]
            r.consume(f"wdtb{p}", "PE", v_dt_g[1])
            r.consume(f"wbb{p}", "PE", v_bp_g[1])
            if j + 2 < DC:
                r.acquire("gpsimd", f"wdtb{p}")
                wload[("wdt", j + 2)] = r.dma("gpsimd", wdtb_t[:, p * 2048:(p + 1) * 2048], d_wdt[j + 2], f"WDT{p}")
                r.acquire("gpsimd", f"wbb{p}")
                wload[("wb", j + 2)] = r.dma("gpsimd", wbb_t[:, p * 2048:(p + 1) * 2048], d_wb[j + 2], f"WBB{p}")

            v_sp = {}
            for h in range(HFN):
                bank = pb[p * 4 + h]
                hs = slice(h * HF, (h + 1) * HF)
                r.acquire("scalar", f"sp_lr{p}_{h}")
                r.acquire("scalar", f"sp_ae{p}_{h}")
                r.wait("scalar", "PE", v_dt_g[h])
                r.op("scalar", "activation", s_r[:, hs], bank[:, :], AF.Relu, bias=bdt_t[:, j:j + 1])
                v_abs = r.op("scalar", "activation", s_a[:, hs], bank[:, :], AF.Abs,
                             bias=bdt_t[:, j:j + 1], inc="ACT")
                r.consume(f"pb{p * 4 + h}", "ACT", v_abs)
                r.op("scalar", "activation", s_e[:, hs], s_a[:, hs], AF.Exp, scale=-1.0)
                v_sp[h] = r.op("scalar", "activation", s_l[:, hs], s_e[:, hs], AF.Ln, bias=1.0, inc="ACT")

            v_dt, v_u, v_abar = {}, {}, {}
            for h in range(HFN):
                hs = slice(h * HF, (h + 1) * HF)
                r.wait("vector", "ACT", v_sp[h])
                v_dt[h] = r.op("vector", "tensor_add", s_dt[:, hs], s_l[:, hs], s_r[:, hs], inc="DVE")
                r.wait("scalar", "DVE", v_dt[h])
                r.wait("scalar", "DVE", v_A)
                v_abar[h] = r.op("scalar", "activation", s_ab[:, hs], s_dt[:, hs], AF.Exp,
                                 scale=A_t[:, j:j + 1], inc="ACT")
                bank = pb[p * 4 + 2 + h]
                r.wait("vector", "PE", v_bp_g[h])
                v_t = r.op("vector", "tensor_mul", s_a[:, hs], s_dt[:, hs], bank[:, :], inc="DVE")
                r.consume(f"pb{p * 4 + 2 + h}", "DVE", v_t)
                r.waitv("vector", v_xs)
                v_u[h] = r.op("vector", "tensor_mul", s_e[:, hs], s_a[:, hs],
                              xs_t[:, p * T + h * HF: p * T + (h + 1) * HF], inc="DVE")
            r.consume(f"xs{p}", "DVE", v_u[1])

            r.acquire("vector", f"sp_hp{p}")
            r.wait("vector", "ACT", v_abar[0])
            r.wait("vector", "ACT", v_abar[1])
            r.wait("vector", "DVE", v_zero)
            r.op("vector", "tensor_tensor_scan", s_hl[:, :], s_ab[:, :], s_e[:, :], 0.0, ALU.mult, ALU.add)
            v_scan_p = r.op("vector", "tensor_tensor_scan", s_pq[:, :], s_ab[:, :], zeros, 1.0,
                            ALU.mult, ALU.add, inc="DVE")
            v_scan_last = v_scan_p
            r.consume(f"sp_lr{p}_0", "DVE", v_dt[0])
            r.consume(f"sp_lr{p}_1", "DVE", v_dt[1])
            r.consume(f"sp_ae{p}_0", "DVE", v_scan_p)
            r.consume(f"sp_ae{p}_1", "DVE", v_scan_p)

            r.wait("gpsimd", "DVE", v_scan_p)
            r.op("gpsimd", "tensor_copy", hb[:, j * T:(j + 1) * T], s_hl[:, :])
            r.op("gpsimd", "tensor_copy", hb[:, (DC + j) * T:(DC + j + 1) * T], s_pq[:, :])
            r.op("gpsimd", "tensor_copy", hP_t[:, j:j + 1], s_hl[:, T - 1:T])
            v_pl = r.op("gpsimd", "tensor_copy", hP_t[:, 16 + j:17 + j], s_pq[:, T - 1:T], inc="PL")
            r.consume(f"sp_hp{p}", "PL", v_pl)

        r.consume("sil0", "DVE", v_scan_last)
        r.consume("sil1", "DVE", v_scan_last)
        TAP("hP", hP_t[:, :])
        TAP("slots", U2[:, :])
        TAP("hbA", hb[:, :])
        TAP("xbA", xb)
        TAP("Acol", A_t[:, :])

        if upto == "A1":
            return _finish(r, sems, tap_specs, nc, block), tap_specs

        r.wait("gpsimd", "PL", r.cnt["PL"])
        v_cci = r.dma("gpsimd", d_ccin, hP_t[:, :], "CCS")
        r.waitv("gpsimd", v_cci)
        r.op("gpsimd", "collective_compute", "AllGather", ALU.bypass,
             replica_groups=[list(range(NCORE))], ins=[d_ccin], outs=[d_ccout], inc="CC")
        r.wait("gpsimd", "CC", 1)
        v_carr = r.dma("gpsimd", carr_t[:, :].rearrange("p (k c) -> p k c", k=NCORE),
                       d_ccout.rearrange("(k p) c -> p k c", p=128), "CCS")

        r.waitv("vector", v_carr)
        r.waitv("vector", v_smalls)

        def dve_step(attr, *a, **k):
            v = r.op("vector", attr, *a, inc="DVE", **k)
            r.wait("vector", "DVE", v)
            return v

        dve_step("memset", cvec_t[:, :], 0.0)
        for k in range(NCORE - 1):
            hsl = carr_t[:, k * 32:k * 32 + 16]
            psl = carr_t[:, k * 32 + 16:k * 32 + 32]
            dve_step("tensor_mul", ct1_t[:, :], psl, cvec_t[:, :])
            dve_step("tensor_add", ct1_t[:, :], ct1_t[:, :], hsl)
            dve_step("tensor_sub", ct1_t[:, :], ct1_t[:, :], cvec_t[:, :])
            dve_step("tensor_single_scalar", ct1_t[:, :], ct1_t[:, :], sel_t[:, k:k + 1], ALU.mult)
            dve_step("tensor_add", cvec_t[:, :], cvec_t[:, :], ct1_t[:, :])
        v_cvec = r.cnt["DVE"]
        TAP("cvec", cvec_t[:, :])
        TAP("carr", carr_t[:, :])

        corrb = U2[:, 8 * T:(8 + 1) * T].bitcast(BF16)
        r.wait("scalar", "DVE", v_cvec)
        r.wait("scalar", "PL", r.cnt["PL"])
        r.wait("vector", "DVE", v_cvec)
        r.wait("gpsimd", "DVE", v_cvec)
        v_fix_c = {}
        v_corr_prev = {}
        for j in range(DC):
            pq_bf = hb[:, (DC + j) * T:(DC + j + 1) * T]
            cslot = corrb[:, (j % 2) * T:(j % 2 + 1) * T]
            if j - 2 in v_fix_c:
                r.waitv("scalar", v_fix_c[j - 2])
            v_corr = r.op("scalar", "activation", cslot, pq_bf, AF.Copy,
                          scale=cvec_t[:, j:j + 1], inc="ACT")
            eng, sem = ("vector", "DVE") if j % 2 == 0 else ("gpsimd", "PL")
            r.wait(eng, "ACT", v_corr)
            v = r.op(eng, "tensor_add", hb[:, j * T:(j + 1) * T],
                     hb[:, j * T:(j + 1) * T], cslot, inc=sem)
            v_fix_c[j] = (sem, v)
        v_fix = v_fix_c[DC - 2]
        v_fix_pl = v_fix_c[DC - 1]
        TAP("hb", hb[:, 0:DC * T])

        if upto == "fix":
            return _finish(r, sems, tap_specs, nc, block), tap_specs

        for m0 in range(2):
            r.acquire("gpsimd", f"wdtb{m0 % 2}")
            wload[("wc", m0)] = r.dma("gpsimd", wdtb_t[:, (m0 % 2) * 2048:(m0 % 2 + 1) * 2048], d_wc[m0], f"WDT{m0 % 2}")

        v_r1 = {}
        last_sq_pe1 = {0: 0, 1: 0}
        silb = sil_t.bitcast(BF16)[:, 0:T]
        for m in range(DC):
            p = m % 2
            wc_tile = lambda c, p=p: wdtb_t[:, p * 2048 + c * 128: p * 2048 + (c + 1) * 128]
            r.acquire("sync", f"xs{p}")
            v_xs = r.dma("sync", xs_t[:, p * T:(p + 1) * T], d_xfm[m * 128:(m + 1) * 128, :], f"XS{p}")
            r.waitv("tensor", wload[("wc", m)])
            v_mix = {}
            for h in range(HFN):
                bk = p * 2 + h
                r.acquire("tensor", f"pb{bk}")
                for c in range(DC):
                    r.waitv("tensor", v_fix_c[c])
                    r.op("tensor", "matmul", pb[bk][:, :], wc_tile(c),
                         hb[:, c * T + h * HF: c * T + (h + 1) * HF],
                         start=(c == 0), stop=(c == DC - 1), inc="PE" if c == DC - 1 else None)
                v_mix[h] = r.cnt["PE"]
            r.consume(f"wdtb{p}", "PE", v_mix[1])
            if m + 2 < DC:
                r.acquire("gpsimd", f"wdtb{p}")
                wload[("wc", m + 2)] = r.dma("gpsimd", wdtb_t[:, p * 2048:(p + 1) * 2048], d_wc[m + 2], f"WDT{p}")
            v = None
            for h in range(HFN):
                bk = p * 2 + h
                r.wait("vector", "PE", v_mix[h])
                r.waitv("vector", v_xs)
                r.wait("vector", "DVE", v_ds1)
                v = r.op("vector", "scalar_tensor_tensor",
                         U2[:, m * T + h * HF: m * T + (h + 1) * HF],
                         xs_t[:, p * T + h * HF: p * T + (h + 1) * HF],
                         ds1_t[:, m:m + 1], pb[bk][:, :], ALU.mult, ALU.add, inc="DVE")
                r.consume(f"pb{bk}", "DVE", v)
            r.consume(f"xs{p}", "DVE", v)
            v_r1[m] = v
            for h in range(HFN):
                r.acquire("scalar", f"sil{h}")
                r.wait("scalar", "DVE", v_r1[m])
                if last_sq_pe1[h]:
                    r.wait("scalar", "PE", last_sq_pe1[h])
                v_sq = r.op("scalar", "activation", silb[:, h * HF:(h + 1) * HF],
                            U2[:, m * T + h * HF: m * T + (h + 1) * HF], AF.Square, inc="ACT")
                r.wait("tensor", "ACT", v_sq)
                r.waitv("tensor", v_seb)
                if m == 0:
                    r.acquire("tensor", f"pb{4 + h}")
                    r.acquire("tensor", f"pb{6 + h}")
                r.op("tensor", "matmul", pb[4 + h][0:1, :], onesc_t[:, :],
                     U2[:, m * T + h * HF: m * T + (h + 1) * HF],
                     start=(m == 0), stop=(m == DC - 1))
                last_sq_pe1[h] = r.op("tensor", "matmul", pb[6 + h][0:1, :], onescb_t[:, :],
                                      silb[:, h * HF:(h + 1) * HF],
                                      start=(m == 0), stop=(m == DC - 1), inc="PE")
        for h in range(HFN):
            r.consume(f"sil{h}", "PE", last_sq_pe1[h])
        TAP("r1", U2[:, :])

        if upto == "A2":
            return _finish(r, sems, tap_specs, nc, block), tap_specs

        def emit_ln_stats_chunk(src_fn, v_src, m, last_sq_pe):
            for h in range(HFN):
                r.acquire("scalar", f"sil{h}")
                r.wait("scalar", "DVE", v_src)
                if last_sq_pe[h]:
                    r.wait("scalar", "PE", last_sq_pe[h])
                v_sq = r.op("scalar", "activation", silb[:, h * HF:(h + 1) * HF],
                            src_fn(m, h), AF.Square, inc="ACT")
                r.wait("tensor", "ACT", v_sq)
                if m == 0:
                    r.acquire("tensor", f"pb{4 + h}")
                    r.acquire("tensor", f"pb{6 + h}")
                r.op("tensor", "matmul", pb[4 + h][0:1, :], onesc_t[:, :], src_fn(m, h),
                     start=(m == 0), stop=(m == DC - 1))
                last_sq_pe[h] = r.op("tensor", "matmul", pb[6 + h][0:1, :], onescb_t[:, :],
                                     silb[:, h * HF:(h + 1) * HF],
                                     start=(m == 0), stop=(m == DC - 1), inc="PE")

        def emit_ln(src_fn, v_src_fn):
            last_sq_pe = {0: 0, 1: 0}
            for m in range(DC):
                emit_ln_stats_chunk(src_fn, v_src_fn(m), m, last_sq_pe)
            v_stats = r.cnt["PE"]
            for h in range(HFN):
                r.consume(f"sil{h}", "PE", last_sq_pe[h])
            return emit_ln_rows(v_stats)
        def emit_ln_rows(v_stats=None):
            if v_stats is None:
                v_stats = r.cnt["PE"]
            mrow = rows8_t[0:1, 0:1024]
            rrow = rows8_t[0:1, 1024:2048]
            r.acquire("vector", "rows8_0")
            r.acquire("vector", "rows8_1")
            r.wait("vector", "PE", v_stats)

            def row_step(attr, *a, **k):
                v = r.op("vector", attr, *a, inc="DVE", **k)
                r.wait("vector", "DVE", v)
                return v

            for h in range(HFN):
                hs = slice(h * HF, (h + 1) * HF)
                row_step("tensor_scalar_mul", mrow[:, hs], pb[4 + h][0:1, :], 1.0 / 2048.0)
            row_step("tensor_scalar_mul", rrow[:, 0:HF], pb[6][0:1, :], 1.0 / 2048.0)
            row_step("tensor_scalar_mul", rrow[:, HF:2 * HF], pb[7][0:1, :], 1.0 / 2048.0)
            row_step("tensor_mul", srow_t[0:1, 0:1024], mrow[:, :], mrow[:, :])
            v_var = row_step("tensor_sub", rrow[:, :], rrow[:, :], srow_t[0:1, 0:1024])
            for i in range(4, 8):
                r.consume(f"pb{i}", "DVE", v_var)
            r.wait("scalar", "DVE", v_var)
            v_ln = r.op("scalar", "activation", srow_t[0:1, 0:1024], rrow[:, :], AF.Ln,
                        bias=cst_t[0:1, 0:1], inc="ACT")
            r.wait("scalar", "ACT", v_ln)
            v_rstd = r.op("scalar", "activation", rrow[:, :], srow_t[0:1, 0:1024], AF.Exp,
                          scale=cst_t[0:1, 1:2], inc="ACT")
            r.wait("tensor", "DVE", v_var)
            r.wait("tensor", "ACT", v_rstd)
            v_bc = None
            for h in range(HFN):
                hs = slice(h * HF, (h + 1) * HF)
                r.acquire("tensor", f"pb{4 + h}")
                r.acquire("tensor", f"pb{6 + h}")
                r.op("tensor", "matmul", pb[4 + h][:, :], ones1_t[:, :], mrow[:, hs],
                     start=True, stop=True)
                v_bc = r.op("tensor", "matmul", pb[6 + h][:, :], ones1_t[:, :], rrow[:, hs],
                            start=True, stop=True, inc="PE")
            r.consume("rows8_0", "PE", v_bc)
            r.consume("rows8_1", "PE", v_bc)
            return v_bc

        v_bc1 = emit_ln_rows()
        v_spill = {}
        v_cast_m = {}
        v_norm1 = None
        for m in range(DC):
            v = None
            for h in range(HFN):
                sl = U2[:, m * T + h * HF: m * T + (h + 1) * HF]
                r.wait("vector", "PE", v_bc1)
                r.acquire("vector", f"sil{h}")
                tb = sil_t[:, h * HF:(h + 1) * HF]
                r.op("vector", "tensor_sub", tb, sl, pb[4 + h][:, :])
                v = r.op("vector", "tensor_mul", sl, tb, pb[6 + h][:, :], inc="DVE")
                r.consume(f"sil{h}", "DVE", v)
            v_norm1 = v
            r.wait("gpsimd", "DVE", v)
            v_cast_m[m] = r.op("gpsimd", "tensor_copy", xb_t[:, m * T:(m + 1) * T],
                               U2[:, m * T:(m + 1) * T], inc="PL")
            r.wait("sync", "DVE", v)
            v_spill[m] = r.dma("sync", d_r1d[:, m * T:(m + 1) * T], U2[:, m * T:(m + 1) * T], "SP")
        for i in range(4, 8):
            r.consume(f"pb{i}", "DVE", v_norm1)
        v_x1_pl = r.cnt["PL"]
        v_spill_all = v_spill[DC - 1]
        TAP("x1", U2[:, :])

        if upto == "ln1":
            return _finish(r, sems, tap_specs, nc, block), tap_specs

        r.waitv("tensor", v_seb)
        for h in range(HFN):
            r.acquire("tensor", f"pb{h}")
            for c in range(DC):
                r.wait("tensor", "PL", v_cast_m[c])
                r.op("tensor", "matmul", pb[h][0:16, :], rwb_t[:, c * 16:(c + 1) * 16],
                     xb[:, c * T + h * HF: c * T + (h + 1) * HF],
                     start=(c == 0), stop=(c == DC - 1), inc="PE" if c == DC - 1 else None)
        v_logit = r.cnt["PE"]
        r.acquire("scalar", "xs0")
        r.acquire("scalar", "xs1")
        r.wait("scalar", "PE", v_logit)
        r.op("scalar", "copy", lg_s[:, 0:HF], pb[0][0:16, :])
        v_lg = r.op("scalar", "copy", lg_s[:, HF:2 * HF], pb[1][0:16, :], inc="ACT")
        r.consume("pb0", "ACT", v_lg)
        r.consume("pb1", "ACT", v_lg)
        r.wait("tensor", "ACT", v_lg)
        r.acquire("tensor", "pb2")
        for b in range(8):
            r.op("tensor", "transpose", pb[2][:, b * 16:(b + 1) * 16],
                 lg_s[:, b * 128:(b + 1) * 128], idn_t[0:16, 0:16],
                 inc="PE" if b == 7 else None)
        v_tp = r.cnt["PE"]
        r.wait("scalar", "PE", v_tp)
        v_probs = r.op("scalar", "copy", probs_t[:, :], pb[2][:, 0:128], inc="ACT")
        r.consume("pb2", "ACT", v_probs)
        r.wait("scalar", "ACT", v_probs)
        v_e = r.op("scalar", "activation", pr2_t[:, :], probs_t[:, :], AF.Exp, inc="ACT")
        r.wait("vector", "ACT", v_e)
        for b in range(8):
            v_s = r.op("vector", "reduce_sum", ssum8_t[:, b:b + 1], pr2_t[:, b * 16:(b + 1) * 16],
                       AX.X, inc="DVE")
        r.wait("vector", "DVE", v_s)
        v_rc = r.op("vector", "reciprocal", rcp8_t[:, :], ssum8_t[:, :], inc="DVE")
        r.wait("vector", "DVE", v_rc)
        for b in range(8):
            v_wt = r.op("vector", "tensor_single_scalar", wtm_t[:, b * 16:(b + 1) * 16],
                        pr2_t[:, b * 16:(b + 1) * 16], rcp8_t[:, b:b + 1], ALU.mult, inc="DVE")
        r.wait("vector", "DVE", v_wt)
        for b in range(8):
            v_m8 = r.op("vector", "max", out=m8_8_t[:, b * 8:(b + 1) * 8],
                        in_=wtm_t[:, b * 16:(b + 1) * 16], inc="DVE")
        r.wait("vector", "DVE", v_m8)
        for b in range(8):
            v_mk = r.op("vector", "tensor_single_scalar", msk_t[:, b * 16:(b + 1) * 16],
                        wtm_t[:, b * 16:(b + 1) * 16], m8_8_t[:, b * 8 + 3:b * 8 + 4],
                        ALU.is_ge, inc="DVE")
        r.wait("vector", "DVE", v_mk)
        for b in range(8):
            v_w = r.op("vector", "tensor_mul", pr2_t[:, b * 16:(b + 1) * 16],
                       msk_t[:, b * 16:(b + 1) * 16], wtm_t[:, b * 16:(b + 1) * 16], inc="DVE")
        r.wait("vector", "DVE", v_w)
        TAP("wdense_tm", pr2_t[:, :])
        r.wait("tensor", "DVE", v_w)
        r.acquire("tensor", "pb3")
        r.acquire("tensor", "pb4")
        for b in range(8):
            bk = 3 if b < 4 else 4
            r.op("tensor", "transpose", pb[bk][0:16, (b % 4) * 128:(b % 4 + 1) * 128],
                 pr2_t[:, b * 16:(b + 1) * 16], idn_t[:, :], inc="PE" if b == 7 else None)
        v_wt = r.cnt["PE"]
        r.wait("scalar", "PE", v_wt)
        r.op("scalar", "copy", wfm[:, 0:HF], pb[3][0:16, 0:512])
        v_wfm = r.op("scalar", "copy", wfm[:, HF:2 * HF], pb[4][0:16, 0:512], inc="ACT")
        r.consume("pb3", "ACT", v_wfm)
        r.consume("pb4", "ACT", v_wfm)

        r.acquire("tensor", "pb5")
        r.acquire("tensor", "pb2")
        for h in range(HFN):
            bank = pb[5] if h == 0 else pb[2]
            for c in range(DC):
                r.op("tensor", "matmul", bank[0:1, :], seb_t[:, c:c + 1],
                     xb[:, c * T + h * HF: c * T + (h + 1) * HF],
                     start=(c == 0), stop=(c == DC - 1), inc="PE" if c == DC - 1 else None)
        v_se = r.cnt["PE"]
        r.wait("scalar", "PE", v_se)
        r.op("scalar", "activation", srow_t[0:1, 0:HF], pb[5][0:1, :], AF.Exp, scale=-1.0)
        v_sex = r.op("scalar", "activation", srow_t[0:1, HF:2 * HF], pb[2][0:1, :], AF.Exp,
                     scale=-1.0, inc="ACT")
        r.consume("pb5", "ACT", v_sex)
        r.consume("pb2", "ACT", v_sex)
        r.wait("vector", "ACT", v_sex)
        v_sa = r.op("vector", "tensor_scalar_add", srow_t[0:1, :], srow_t[0:1, :], 1.0, inc="DVE")
        r.wait("vector", "DVE", v_sa)
        v_sig = r.op("vector", "reciprocal", srow_t[0:1, :], srow_t[0:1, :], inc="DVE")
        r.wait("vector", "DVE", v_sig)
        r.wait("tensor", "DVE", v_sig)
        r.acquire("tensor", "pb6")
        r.acquire("tensor", "pb7")
        r.op("tensor", "matmul", pb[6][:, :], ones1_t[:, :], srow_t[0:1, 0:HF], start=True, stop=True)
        v_sbc_pe = r.op("tensor", "matmul", pb[7][:, :], ones1_t[:, :], srow_t[0:1, HF:2 * HF],
                        start=True, stop=True, inc="PE")
        r.wait("scalar", "PE", v_sbc_pe)
        r.waitv("scalar", v_spill_all)
        r.op("scalar", "copy", s_bc[:, 0:HF], pb[6][:, :])
        v_sbc = r.op("scalar", "copy", s_bc[:, HF:2 * HF], pb[7][:, :], inc="ACT")
        r.consume("pb6", "ACT", v_sbc)
        r.consume("pb7", "ACT", v_sbc)
        TAP("s_bc", s_bc)

        if upto == "router":
            return _finish(r, sems, tap_specs, nc, block), tap_specs

        he = lambda c, h: heb[:, c * T + h * HF: c * T + (h + 1) * HF]
        dwb = lambda q: heb[:, 11264 + q * 1408: 11264 + (q + 1) * 1408]
        gwb = lambda q: wdtb_t[:, q * 2048:(q + 1) * 2048]
        uwb = lambda q: wbb_t[:, q * 2048:(q + 1) * 2048]

        jobs = [("sg", g) for g in range(SGN)] + [("ex", e) for e in range(NE)]
        gu_load, dw_load = {}, {}

        r.waitv("gpsimd", v_spill_all)
        r.waitv("vector", v_spill_all)

        def gu_dma(jkind, jid, m):
            q = m % 2
            src_g = d_sgw[jid, m] if jkind == "sg" else d_gw[jid, m]
            src_u = d_suw[jid, m] if jkind == "sg" else d_uw[jid, m]
            r.acquire("gpsimd", f"wdtb{q}")
            gu_load[("g", jkind, jid, m)] = r.dma("gpsimd", gwb(q), src_g, f"GW{q}")
            r.acquire("gpsimd", f"wbb{q}")
            gu_load[("u", jkind, jid, m)] = r.dma("gpsimd", uwb(q), src_u, f"UW{q}")

        def dw_dma(jkind, jid, m2):
            q = m2 % 2
            src = d_sdw[m2][:, jid * 1408:(jid + 1) * 1408] if jkind == "sg" else d_dw[jid, m2]
            r.acquire("gpsimd", f"dwb{q}")
            dw_load[(jkind, jid, m2)] = r.dma("gpsimd", dwb(q), src, f"DW{q}")

        gu_dma(*jobs[0], 0)
        gu_dma(*jobs[0], 1)

        v_acc = {}
        for ji, (jkind, jid) in enumerate(jobs):
            if jkind == "ex":
                e = jid
                sslot = e % 2
                r.acquire("sync", f"rows8_{sslot}")
                r.wait("sync", "ACT", v_wfm)
                v_wrow = r.dma("sync", rows8_t[0:1, sslot * 1024:sslot * 1024 + T],
                               wfm[e:e + 1, 0:T], f"WR{sslot}")
                r.waitv("tensor", v_wrow)
                r.acquire("tensor", "pb6")
                r.acquire("tensor", "pb7")
                r.op("tensor", "matmul", pb[6][:, :], ones1_t[:, :],
                     rows8_t[0:1, sslot * 1024:sslot * 1024 + HF], start=True, stop=True)
                v_pe_bc = r.op("tensor", "matmul", pb[7][:, :], ones1_t[:, :],
                               rows8_t[0:1, sslot * 1024 + HF:sslot * 1024 + T],
                               start=True, stop=True, inc="PE")
                r.consume(f"rows8_{sslot}", "PE", v_pe_bc)
                r.wait("scalar", "PE", v_pe_bc)
                r.acquire("scalar", "w_bc")
                r.op("scalar", "copy", w_bc[:, 0:HF], pb[6][:, :])
                v_wbc = r.op("scalar", "copy", w_bc[:, HF:2 * HF], pb[7][:, :], inc="ACT")
                r.consume("pb6", "ACT", v_wbc)
                r.consume("pb7", "ACT", v_wbc)
                scale_tile, v_scale = w_bc, ("ACT", v_wbc)
            else:
                scale_tile, v_scale = s_bc, ("ACT", v_sbc)

            v_gu_last = None
            v_gu = {}
            for m in range(IC):
                q = m % 2
                r.waitv("tensor", gu_load[("g", jkind, jid, m)])
                r.waitv("tensor", gu_load[("u", jkind, jid, m)])
                for h in range(HFN):
                    r.acquire("tensor", f"pb{h}")
                    for c in range(DC):
                        r.op("tensor", "matmul", pb[h][:, :], gwb(q)[:, c * 128:(c + 1) * 128],
                             xb[:, c * T + h * HF: c * T + (h + 1) * HF],
                             start=(c == 0), stop=(c == DC - 1), inc="PE" if c == DC - 1 else None)
                    v_g = r.cnt["PE"]
                    r.acquire("tensor", f"pb{2 + h}")
                    for c in range(DC):
                        r.op("tensor", "matmul", pb[2 + h][:, :], uwb(q)[:, c * 128:(c + 1) * 128],
                             xb[:, c * T + h * HF: c * T + (h + 1) * HF],
                             start=(c == 0), stop=(c == DC - 1), inc="PE" if c == DC - 1 else None)
                    v_upe = r.cnt["PE"]
                    if h == 1:
                        r.consume(f"wdtb{q}", "PE", v_upe)
                        r.consume(f"wbb{q}", "PE", v_upe)
                    r.acquire("scalar", f"sil{h}")
                    r.wait("scalar", "PE", v_g)
                    v_sil = r.op("scalar", "activation", sil_t[:, h * HF:(h + 1) * HF],
                                 pb[h][:, :], AF.Silu, inc="ACT")
                    r.consume(f"pb{h}", "ACT", v_sil)
                    r.wait("vector", "ACT", v_sil)
                    r.wait("vector", "PE", v_upe)
                    v_h = r.op("vector", "tensor_mul", he(m, h), sil_t[:, h * HF:(h + 1) * HF],
                               pb[2 + h][:, :], inc="DVE")
                    r.consume(f"pb{2 + h}", "DVE", v_h)
                    r.consume(f"sil{h}", "DVE", v_h)
                    v_gu[(m, h)] = v_h
                    v_gu_last = v_h
                if m + 2 < IC:
                    gu_dma(jkind, jid, m + 2)
            dw_dma(jkind, jid, 0)
            dw_dma(jkind, jid, 1)
            if ji + 1 < len(jobs):
                njk, njid = jobs[ji + 1]
                gu_dma(njk, njid, 0)
                gu_dma(njk, njid, 1)
            for m2 in range(DC):
                q = m2 % 2
                r.waitv("tensor", dw_load[(jkind, jid, m2)])
                for h in range(HFN):
                    bk = 4 + h
                    r.acquire("tensor", f"pb{bk}")
                    for c in range(IC):
                        r.wait("tensor", "DVE", v_gu[(c, h)])
                        r.op("tensor", "matmul", pb[bk][:, :], dwb(q)[:, c * 128:(c + 1) * 128],
                             he(c, h), start=(c == 0), stop=(c == IC - 1),
                             inc="PE" if c == IC - 1 else None)
                    v_d = r.cnt["PE"]
                    if h == 1:
                        r.consume(f"dwb{q}", "PE", v_d)
                    r.wait("vector", "PE", v_d)
                    r.wait("vector", v_scale[0], v_scale[1])
                    acc_sl = accv[:, m2 * T + h * HF: m2 * T + (h + 1) * HF]
                    ssl = scale_tile[:, h * HF:(h + 1) * HF]
                    if ji == 0:
                        v_a = r.op("vector", "tensor_mul", acc_sl, ssl, pb[bk][:, :], inc="DVE")
                    else:
                        r.acquire("vector", f"tscr{h}")
                        tb = tscr[:, h * HF:(h + 1) * HF]
                        r.op("vector", "tensor_mul", tb, ssl, pb[bk][:, :])
                        v_a = r.op("vector", "tensor_add", acc_sl, acc_sl, tb, inc="DVE")
                        r.consume(f"tscr{h}", "DVE", v_a)
                    r.consume(f"pb{bk}", "DVE", v_a)
                    v_acc[(m2, h)] = v_a
                if m2 + 2 < DC:
                    dw_dma(jkind, jid, m2 + 2)
            if jkind == "ex":
                r.consume("w_bc", "DVE", v_acc[(DC - 1, 1)])

        TAP("acc", accv[:, :])

        if upto == "moe":
            return _finish(r, sems, tap_specs, nc, block), tap_specs

        v_r2 = {}
        for m in range(DC):
            q = m % 2
            rbs = xs_t[:, q * T:(q + 1) * T]
            r.acquire("sync", f"rb{q}")
            v_back = r.dma("sync", rbs, d_r1d[:, m * T:(m + 1) * T], f"XS{q}")
            r.waitv("vector", v_back)
            r.wait("vector", "DVE", v_acc[(m, 1)])
            v = r.op("vector", "tensor_add", accv[:, m * T:(m + 1) * T],
                     accv[:, m * T:(m + 1) * T], rbs, inc="DVE")
            r.consume(f"rb{q}", "DVE", v)
            v_r2[m] = v

        v_bc2 = emit_ln(lambda m, h: accv[:, m * T + h * HF: m * T + (h + 1) * HF],
                        lambda m: v_r2[m])
        TAP("r2", accv[:, :])
        TAP("rows2", rows8_t[:, :])
        if upto == "ln2a":
            return _finish(r, sems, tap_specs, nc, block), tap_specs
        for m in range(DC):
            v = None
            for h in range(HFN):
                sl = accv[:, m * T + h * HF: m * T + (h + 1) * HF]
                r.wait("vector", "PE", v_bc2)
                r.acquire("vector", f"tscr{h}")
                tb = tscr[:, h * HF:(h + 1) * HF]
                r.op("vector", "tensor_sub", tb, sl, pb[4 + h][:, :])
                v = r.op("vector", "tensor_mul", sl, tb, pb[6 + h][:, :], inc="DVE")
                r.consume(f"tscr{h}", "DVE", v)
            r.wait("sync", "DVE", v)
            r.dma("sync", d_y[m * 128:(m + 1) * 128, :], accv[:, m * T:(m + 1) * T], "Y")

        return _finish(r, sems, tap_specs, nc, block), tap_specs


def _finish(r, sems, tap_specs, nc, block):
    if tap_specs:
        for name in r.cnt:
            if name != "Y":
                r.wait("sync", name, r.cnt[name])
        for name in list(tap_specs):
            ap = tap_specs[name]
            d = nc.dram_tensor(f"tap_{name}", list(ap.shape), ap.dtype, kind="ExternalOutput").ap()
            r.dma("sync", d, ap, "Y")
    r.wait("sync", "Y", r.cnt["Y"])
    r.wait("gpsimd", "Y", r.cnt["Y"])

    @block.sync
    def _(eng):
        r.replay(eng, "sync", sems)

    @block.tensor
    def _(eng):
        r.replay(eng, "tensor", sems)

    @block.scalar
    def _(eng):
        r.replay(eng, "scalar", sems)

    @block.vector
    def _(eng):
        r.replay(eng, "vector", sems)

    @block.gpsimd
    def _(eng):
        r.replay(eng, "gpsimd", sems)

    return nc


def _pack_weights(i):
    f32 = lambda a: np.ascontiguousarray(np.asarray(a, dtype=np.float32))
    C = np.ascontiguousarray
    W_dt = f32(i["L0_attn_W_dt"]); W_B = f32(i["L0_attn_W_B"]); W_C = f32(i["L0_attn_W_C"])
    gw = f32(i["L0_ffn_gate_w"]); uw = f32(i["L0_ffn_up_w"]); dwt = f32(i["L0_ffn_down_w"])
    sgw = f32(i["L0_ffn_shared_gate_w"]); suw = f32(i["L0_ffn_shared_up_w"])
    sdw = f32(i["L0_ffn_shared_down_w"]); rw = f32(i["L0_ffn_router_w"]); se = f32(i["L0_ffn_se_gate_w"])

    d = {}
    d["wdt"] = C(W_dt.reshape(16, 128, 16, 128).transpose(0, 3, 2, 1).reshape(16, 128, 2048))
    d["wb"] = C(W_B.reshape(16, 128, 16, 128).transpose(0, 3, 2, 1).reshape(16, 128, 2048))
    d["wc"] = C(W_C.reshape(16, 128, 16, 128).transpose(0, 3, 2, 1).reshape(16, 128, 2048))
    d["gw"] = C(gw.reshape(16, 11, 128, 16, 128).transpose(0, 1, 4, 3, 2).reshape(16, 11, 128, 2048))
    d["uw"] = C(uw.reshape(16, 11, 128, 16, 128).transpose(0, 1, 4, 3, 2).reshape(16, 11, 128, 2048))
    d["dw"] = C(dwt.reshape(16, 16, 128, 11, 128).transpose(0, 1, 4, 3, 2).reshape(16, 16, 128, 1408))
    d["sgw"] = C(sgw.reshape(44, 128, 16, 128).transpose(0, 3, 2, 1).reshape(4, 11, 128, 2048))
    d["suw"] = C(suw.reshape(44, 128, 16, 128).transpose(0, 3, 2, 1).reshape(4, 11, 128, 2048))
    d["sdw"] = C(sdw.reshape(16, 128, 44, 128).transpose(0, 3, 2, 1).reshape(16, 128, 5632))
    d["rw"] = C(rw.reshape(16, 16, 128).transpose(2, 1, 0).reshape(128, 256))
    d["sew"] = C(se.reshape(16, 128).T)
    v16 = lambda v: C(np.asarray(v, np.float32).reshape(16, 128).T)
    d["bdt"] = v16(i["L0_attn_b_dt"])
    d["alog"] = v16(i["L0_attn_A_log"])
    d["dskip"] = v16(i["L0_attn_D_skip"])
    d["idn"] = np.eye(128, dtype=np.float32)
    d["ones1"] = np.ones((1, 128), np.float32)
    d["onesc"] = np.ones((128, 1), np.float32)
    cst = np.zeros((128, 4), np.float32); cst[:, 0] = EPS; cst[:, 1] = -0.5
    d["cst"] = cst
    return d


def make_in_maps(inputs, shared=None):
    x = np.asarray(inputs["L0_attn_x"], dtype=np.float32)
    if shared is None:
        shared = _pack_weights(inputs)
    in_maps = []
    for k in range(NCORE):
        m = dict(shared)
        m["xfm"] = np.ascontiguousarray(x[0, k * T:(k + 1) * T, :].T)
        m["sel"] = np.ascontiguousarray(
            np.tile((np.arange(8) < k).astype(np.float32), (128, 1)))
        in_maps.append(m)
    return in_maps


def kernel(**inputs):
    nc, _ = build_program(upto="full", taps=())
    in_maps = make_in_maps(inputs)
    res = run_bass_kernel_spmd(nc, in_maps, core_ids=list(range(NCORE)))
    out = np.empty((1, 8192, 2048), dtype=np.float32)
    for k in range(NCORE):
        out[0, k * T:(k + 1) * T, :] = res.results[k]["y"].T
    return out


# version 08
# speedup vs baseline: 1.0020x; 1.0020x over previous
import numpy as np
from collections import defaultdict
from contextlib import ExitStack

import concourse.bass as bass
import concourse.mybir as mybir
from concourse.bass_utils import run_bass_kernel_spmd


F32 = mybir.dt.float32
BF16 = mybir.dt.bfloat16
AF = mybir.ActivationFunctionType
ALU = mybir.AluOpType
AX = mybir.AxisListType

NCORE = 8
T = 1024
HFN = 2
HF = 512
DC = 16
IC = 11
NE = 16
SGN = 4
EPS = 1e-5
ENGINES = ("sync", "tensor", "scalar", "vector", "gpsimd")


class Rec:

    def __init__(self):
        self.q = {e: [] for e in ENGINES}
        self.cnt = defaultdict(int)
        self.waited = defaultdict(int)
        self.released = {}
        self.n = defaultdict(int)

    def op(self, eng, attr, *a, inc=None, incby=1, **k):
        rec = [attr, a, k, None]
        self.q[eng].append(rec)
        self.n[eng] += 1
        if inc is not None:
            self.cnt[inc] += incby
            rec[3] = (inc, incby)
            return self.cnt[inc]
        return None

    def wait(self, eng, sem, val):
        if val is None or val <= 0:
            return
        if self.waited[(eng, sem)] >= val:
            return
        self.waited[(eng, sem)] = val
        self.q[eng].append(["wait_ge", (("SEM", sem), val), {}, None])
        self.n[eng] += 1

    def dma(self, eng, out, in_, sem):
        v = self.op(eng, "dma_start", inc=sem, incby=16, out=out, in_=in_)
        return (sem, v)

    def waitv(self, eng, sv):
        self.wait(eng, sv[0], sv[1])

    def consume(self, key, sem, val):
        self.released.setdefault(key, []).append((sem, val))

    def acquire(self, eng, key):
        for sem, val in self.released.pop(key, []):
            self.wait(eng, sem, val)

    def replay(self, eng_obj, eng_name, sems):
        for attr, a, k, then in self.q[eng_name]:
            a = [sems[x[1]] if isinstance(x, tuple) and len(x) == 2 and x[0] == "SEM" else x
                 for x in a]
            r = getattr(eng_obj, attr)(*a, **k)
            if then is not None:
                r.then_inc(sems[then[0]], then[1])


def build_program(upto="full", taps=()):
    nc = bass.Bass("TRN2", target_bir_lowering=False, debug=False)

    d_xfm = nc.dram_tensor("xfm", [2048, T], F32, kind="ExternalInput").ap()
    d_wdt = nc.dram_tensor("wdt", [DC, 128, 2048], F32, kind="ExternalInput").ap()
    d_wb = nc.dram_tensor("wb", [DC, 128, 2048], F32, kind="ExternalInput").ap()
    d_wc = nc.dram_tensor("wc", [DC, 128, 2048], F32, kind="ExternalInput").ap()
    d_gw = nc.dram_tensor("gw", [NE, IC, 128, 2048], F32, kind="ExternalInput").ap()
    d_uw = nc.dram_tensor("uw", [NE, IC, 128, 2048], F32, kind="ExternalInput").ap()
    d_dw = nc.dram_tensor("dw", [NE, DC, 128, 1408], F32, kind="ExternalInput").ap()
    d_sgw = nc.dram_tensor("sgw", [SGN, IC, 128, 2048], F32, kind="ExternalInput").ap()
    d_suw = nc.dram_tensor("suw", [SGN, IC, 128, 2048], F32, kind="ExternalInput").ap()
    d_sdw = nc.dram_tensor("sdw", [DC, 128, 5632], F32, kind="ExternalInput").ap()
    d_rw = nc.dram_tensor("rw", [128, 256], F32, kind="ExternalInput").ap()
    d_sew = nc.dram_tensor("sew", [128, 16], F32, kind="ExternalInput").ap()
    d_bdt = nc.dram_tensor("bdt", [128, 16], F32, kind="ExternalInput").ap()
    d_alog = nc.dram_tensor("alog", [128, 16], F32, kind="ExternalInput").ap()
    d_dskip = nc.dram_tensor("dskip", [128, 16], F32, kind="ExternalInput").ap()
    d_sel = nc.dram_tensor("sel", [128, 8], F32, kind="ExternalInput").ap()
    d_idn = nc.dram_tensor("idn", [128, 128], F32, kind="ExternalInput").ap()
    d_ones1 = nc.dram_tensor("ones1", [1, 128], F32, kind="ExternalInput").ap()
    d_onesc = nc.dram_tensor("onesc", [128, 1], F32, kind="ExternalInput").ap()
    d_cst = nc.dram_tensor("cst", [128, 4], F32, kind="ExternalInput").ap()

    d_ccin = nc.dram_tensor("cc_in", [128, 32], F32, kind="Internal").ap()
    d_ccout = nc.dram_tensor("cc_out", [128 * NCORE, 32], F32, kind="Internal").ap()
    d_r1d = nc.dram_tensor("r1d", [128, DC * T], F32, kind="Internal").ap()
    d_y = nc.dram_tensor("y", [2048, T], F32, kind="ExternalOutput").ap()

    tap_specs = {}

    with ExitStack() as ctx:
        sb = lambda name, shape, dt: ctx.enter_context(nc.sbuf_tensor(name, shape, dt))
        xb_t = sb("xb", [128, DC * T], BF16)
        U1_t = sb("U1", [128, DC * T], F32)
        U2_t = sb("U2", [128, DC * T], F32)
        wdtb_t = sb("wdtb", [128, 2 * 2048], BF16)
        wbb_t = sb("wbb", [128, 2 * 2048], BF16)
        xs_t = sb("xs", [128, 2 * T], F32)
        sil_t = sb("sil", [128, T], F32)
        rows8_t = sb("rows8", [1, 2048], F32)
        srow_t = sb("srow", [1, T], F32)
        probs_t = sb("probs", [128, 128], F32)
        pr2_t = sb("pr2", [128, 128], F32)
        wtm_t = sb("wtm", [128, 128], F32)
        m8_t = sb("m8", [128, 8], F32)
        msk_t = sb("msk", [128, 128], F32)
        ssum8_t = sb("ssum8", [128, 8], F32)
        rcp8_t = sb("rcp8", [128, 8], F32)
        m8_8_t = sb("m8_8", [128, 64], F32)
        nmx_t = sb("nmx", [128, 1], F32)
        ssum_t = sb("ssum", [128, 1], F32)
        rcp_t = sb("rcp", [128, 1], F32)
        hP_t = sb("hP", [128, 32], F32)
        carr_t = sb("carr", [128, 256], F32)
        cvec_t = sb("cvec", [128, 16], F32)
        ct1_t = sb("ct1", [128, 16], F32)
        bdt_t = sb("bdt_s", [128, 16], F32)
        alog_t = sb("alog_s", [128, 16], F32)
        dskip_t = sb("dskip_s", [128, 16], F32)
        A_t = sb("A_s", [128, 16], F32)
        ds1_t = sb("ds1_s", [128, 16], F32)
        sel_t = sb("sel_s", [128, 8], F32)
        idn_t = sb("idn_s", [128, 128], F32)
        ones1_t = sb("ones1_s", [1, 128], F32)
        onesc_t = sb("onesc_s", [128, 1], F32)
        onescb_t = sb("onescb_s", [128, 1], BF16)
        cst_t = sb("cst_s", [128, 4], F32)
        rwb_t = sb("rwb", [128, 256], BF16)
        seb_t = sb("seb", [128, 16], BF16)

        pb = [ctx.enter_context(nc.psum_tensor(f"pb{i}", [128, 512], F32)) for i in range(8)]

        semnames = ("SM", "XB", "RW", "WDT0", "WDT1", "WBB0", "WBB1", "XS0", "XS1",
                    "GW0", "GW1", "UW0", "UW1", "DW0", "DW1", "SP", "WR0", "WR1",
                    "CCS", "Y", "CC", "PE", "ACT", "DVE", "PL")
        sems = {n: ctx.enter_context(nc.semaphore(n)) for n in semnames}
        block = ctx.enter_context(nc.Block())

        xb = xb_t[:, :]
        hb = U1_t.bitcast(BF16)
        accv = U1_t
        U2 = U2_t
        heb = U2_t.bitcast(BF16)
        slot = lambda s: U2[:, s * T:(s + 1) * T]
        w_bc = U2[:, 7040:8064]
        s_bc = U2[:, 8064:9088]
        tscr = U2[:, 9088:11136]
        zeros = sil_t[:, 0:T]
        lg_s = xs_t[0:16, 0:T]
        wfm = xs_t[0:16, T:2 * T]

        r = Rec()

        def TAP(name, ap):
            if name in taps:
                tap_specs[name] = ap

        r.dma("sync", bdt_t[:, :], d_bdt, "SM")
        r.dma("sync", alog_t[:, :], d_alog, "SM")
        r.dma("sync", dskip_t[:, :], d_dskip, "SM")
        r.dma("sync", sel_t[:, :], d_sel, "SM")
        r.dma("sync", idn_t[:, :], d_idn, "SM")
        r.dma("sync", ones1_t[:, :], d_ones1, "SM")
        r.dma("sync", onesc_t[:, :], d_onesc, "SM")
        v_smalls = r.dma("sync", cst_t[:, :], d_cst, "SM")

        for j in range(DC):
            v_xb = r.dma("gpsimd", xb_t[:, j * T:(j + 1) * T], d_xfm[j * 128:(j + 1) * 128, :], "XB")
        r.dma("gpsimd", onescb_t[:, :], d_onesc, "RW")
        r.dma("gpsimd", rwb_t[:, :], d_rw, "RW")
        v_seb = r.dma("gpsimd", seb_t[:, :], d_sew, "RW")

        r.waitv("scalar", v_smalls)
        v_atmp = r.op("scalar", "activation", A_t[:, :], alog_t[:, :], AF.Exp, inc="ACT")
        r.wait("vector", "ACT", v_atmp)
        v_A = r.op("vector", "tensor_scalar_mul", A_t[:, :], A_t[:, :], -1.0, inc="DVE")
        r.waitv("vector", v_smalls)
        v_ds1 = r.op("vector", "tensor_scalar_add", ds1_t[:, :], dskip_t[:, :], 1.0, inc="DVE")
        v_zero = r.op("vector", "memset", zeros, 0.0, inc="DVE")

        wload = {}
        for j0 in range(2):
            wload[("wdt", j0)] = r.dma("gpsimd", wdtb_t[:, (j0 % 2) * 2048:(j0 % 2 + 1) * 2048], d_wdt[j0], f"WDT{j0 % 2}")
            wload[("wb", j0)] = r.dma("gpsimd", wbb_t[:, (j0 % 2) * 2048:(j0 % 2 + 1) * 2048], d_wb[j0], f"WBB{j0 % 2}")

        v_scan_last = None
        for j in range(DC):
            p = j % 2
            wdt_tile = lambda c, p=p: wdtb_t[:, p * 2048 + c * 128: p * 2048 + (c + 1) * 128]
            wb_tile = lambda c, p=p: wbb_t[:, p * 2048 + c * 128: p * 2048 + (c + 1) * 128]
            s_dt, s_ab, s_l, s_r, s_a, s_e, s_hl, s_pq = (slot(0 + p), slot(2 + p), slot(4 + p),
                                                          slot(6 + p), slot(8 + p), slot(10 + p),
                                                          slot(12 + p), slot(14 + p))
            r.acquire("sync", f"xs{p}")
            v_xs = r.dma("sync", xs_t[:, p * T:(p + 1) * T], d_xfm[j * 128:(j + 1) * 128, :], f"XS{p}")

            r.waitv("tensor", v_xb)
            r.waitv("tensor", wload[("wdt", j)])
            r.waitv("tensor", wload[("wb", j)])
            v_dt_g, v_bp_g = {}, {}
            for h in range(HFN):
                r.acquire("tensor", f"pb{p * 4 + h}")
                for c in range(DC):
                    r.op("tensor", "matmul", pb[p * 4 + h][:, :], wdt_tile(c),
                         xb[:, c * T + h * HF: c * T + (h + 1) * HF],
                         start=(c == 0), stop=(c == DC - 1), inc="PE" if c == DC - 1 else None)
                v_dt_g[h] = r.cnt["PE"]
            for h in range(HFN):
                r.acquire("tensor", f"pb{p * 4 + 2 + h}")
                for c in range(DC):
                    r.op("tensor", "matmul", pb[p * 4 + 2 + h][:, :], wb_tile(c),
                         xb[:, c * T + h * HF: c * T + (h + 1) * HF],
                         start=(c == 0), stop=(c == DC - 1), inc="PE" if c == DC - 1 else None)
                v_bp_g[h] = r.cnt["PE"]
            r.consume(f"wdtb{p}", "PE", v_dt_g[1])
            r.consume(f"wbb{p}", "PE", v_bp_g[1])
            if j + 2 < DC:
                r.acquire("gpsimd", f"wdtb{p}")
                wload[("wdt", j + 2)] = r.dma("gpsimd", wdtb_t[:, p * 2048:(p + 1) * 2048], d_wdt[j + 2], f"WDT{p}")
                r.acquire("gpsimd", f"wbb{p}")
                wload[("wb", j + 2)] = r.dma("gpsimd", wbb_t[:, p * 2048:(p + 1) * 2048], d_wb[j + 2], f"WBB{p}")

            v_sp = {}
            for h in range(HFN):
                bank = pb[p * 4 + h]
                hs = slice(h * HF, (h + 1) * HF)
                r.acquire("scalar", f"sp_lr{p}_{h}")
                r.acquire("scalar", f"sp_ae{p}_{h}")
                r.wait("scalar", "PE", v_dt_g[h])
                r.op("scalar", "activation", s_r[:, hs], bank[:, :], AF.Relu, bias=bdt_t[:, j:j + 1])
                v_abs = r.op("scalar", "activation", s_a[:, hs], bank[:, :], AF.Abs,
                             bias=bdt_t[:, j:j + 1], inc="ACT")
                r.consume(f"pb{p * 4 + h}", "ACT", v_abs)
                r.op("scalar", "activation", s_e[:, hs], s_a[:, hs], AF.Exp, scale=-1.0)
                v_sp[h] = r.op("scalar", "activation", s_l[:, hs], s_e[:, hs], AF.Ln, bias=1.0, inc="ACT")

            v_dt, v_u, v_abar = {}, {}, {}
            for h in range(HFN):
                hs = slice(h * HF, (h + 1) * HF)
                r.wait("vector", "ACT", v_sp[h])
                v_dt[h] = r.op("vector", "tensor_add", s_dt[:, hs], s_l[:, hs], s_r[:, hs], inc="DVE")
                r.wait("scalar", "DVE", v_dt[h])
                r.wait("scalar", "DVE", v_A)
                v_abar[h] = r.op("scalar", "activation", s_ab[:, hs], s_dt[:, hs], AF.Exp,
                                 scale=A_t[:, j:j + 1], inc="ACT")
                bank = pb[p * 4 + 2 + h]
                r.wait("vector", "PE", v_bp_g[h])
                v_t = r.op("vector", "tensor_mul", s_a[:, hs], s_dt[:, hs], bank[:, :], inc="DVE")
                r.consume(f"pb{p * 4 + 2 + h}", "DVE", v_t)
                r.waitv("vector", v_xs)
                v_u[h] = r.op("vector", "tensor_mul", s_e[:, hs], s_a[:, hs],
                              xs_t[:, p * T + h * HF: p * T + (h + 1) * HF], inc="DVE")
            r.consume(f"xs{p}", "DVE", v_u[1])

            r.acquire("vector", f"sp_hp{p}")
            r.wait("vector", "ACT", v_abar[0])
            r.wait("vector", "ACT", v_abar[1])
            r.wait("vector", "DVE", v_zero)
            r.op("vector", "tensor_tensor_scan", s_hl[:, :], s_ab[:, :], s_e[:, :], 0.0, ALU.mult, ALU.add)
            v_scan_p = r.op("vector", "tensor_tensor_scan", s_pq[:, :], s_ab[:, :], zeros, 1.0,
                            ALU.mult, ALU.add, inc="DVE")
            v_scan_last = v_scan_p
            r.consume(f"sp_lr{p}_0", "DVE", v_dt[0])
            r.consume(f"sp_lr{p}_1", "DVE", v_dt[1])
            r.consume(f"sp_ae{p}_0", "DVE", v_scan_p)
            r.consume(f"sp_ae{p}_1", "DVE", v_scan_p)

            r.wait("gpsimd", "DVE", v_scan_p)
            r.op("gpsimd", "tensor_copy", hb[:, j * T:(j + 1) * T], s_hl[:, :])
            r.op("gpsimd", "tensor_copy", hb[:, (DC + j) * T:(DC + j + 1) * T], s_pq[:, :])
            r.op("gpsimd", "tensor_copy", hP_t[:, j:j + 1], s_hl[:, T - 1:T])
            v_pl = r.op("gpsimd", "tensor_copy", hP_t[:, 16 + j:17 + j], s_pq[:, T - 1:T], inc="PL")
            r.consume(f"sp_hp{p}", "PL", v_pl)

        r.consume("sil0", "DVE", v_scan_last)
        r.consume("sil1", "DVE", v_scan_last)
        TAP("hP", hP_t[:, :])
        TAP("slots", U2[:, :])
        TAP("hbA", hb[:, :])
        TAP("xbA", xb)
        TAP("Acol", A_t[:, :])

        if upto == "A1":
            return _finish(r, sems, tap_specs, nc, block), tap_specs

        r.wait("gpsimd", "PL", r.cnt["PL"])
        v_cci = r.dma("gpsimd", d_ccin, hP_t[:, :], "CCS")
        r.waitv("gpsimd", v_cci)
        r.op("gpsimd", "collective_compute", "AllGather", ALU.bypass,
             replica_groups=[list(range(NCORE))], ins=[d_ccin], outs=[d_ccout], inc="CC")
        r.wait("gpsimd", "CC", 1)
        v_carr = r.dma("gpsimd", carr_t[:, :].rearrange("p (k c) -> p k c", k=NCORE),
                       d_ccout.rearrange("(k p) c -> p k c", p=128), "CCS")

        r.waitv("vector", v_carr)
        r.waitv("vector", v_smalls)

        def dve_step(attr, *a, **k):
            v = r.op("vector", attr, *a, inc="DVE", **k)
            r.wait("vector", "DVE", v)
            return v

        dve_step("memset", cvec_t[:, :], 0.0)
        for k in range(NCORE - 1):
            hsl = carr_t[:, k * 32:k * 32 + 16]
            psl = carr_t[:, k * 32 + 16:k * 32 + 32]
            dve_step("tensor_mul", ct1_t[:, :], psl, cvec_t[:, :])
            dve_step("tensor_add", ct1_t[:, :], ct1_t[:, :], hsl)
            dve_step("tensor_sub", ct1_t[:, :], ct1_t[:, :], cvec_t[:, :])
            dve_step("tensor_single_scalar", ct1_t[:, :], ct1_t[:, :], sel_t[:, k:k + 1], ALU.mult)
            dve_step("tensor_add", cvec_t[:, :], cvec_t[:, :], ct1_t[:, :])
        v_cvec = r.cnt["DVE"]
        TAP("cvec", cvec_t[:, :])
        TAP("carr", carr_t[:, :])

        for m0 in range(2):
            r.acquire("gpsimd", f"wdtb{m0 % 2}")
            wload[("wc", m0)] = r.dma("gpsimd", wdtb_t[:, (m0 % 2) * 2048:(m0 % 2 + 1) * 2048],
                                      d_wc[m0], f"WDT{m0 % 2}")

        corrb = U2[:, 8 * T:(8 + 1) * T].bitcast(BF16)
        r.wait("scalar", "DVE", v_cvec)
        r.wait("scalar", "PL", r.cnt["PL"])
        r.wait("vector", "DVE", v_cvec)
        r.wait("gpsimd", "DVE", v_cvec)
        v_fix_c = {}
        v_corr_prev = {}
        for j in range(DC):
            pq_bf = hb[:, (DC + j) * T:(DC + j + 1) * T]
            cslot = corrb[:, (j % 2) * T:(j % 2 + 1) * T]
            if j - 2 in v_fix_c:
                r.waitv("scalar", v_fix_c[j - 2])
            v_corr = r.op("scalar", "activation", cslot, pq_bf, AF.Copy,
                          scale=cvec_t[:, j:j + 1], inc="ACT")
            eng, sem = ("vector", "DVE") if j % 2 == 0 else ("gpsimd", "PL")
            r.wait(eng, "ACT", v_corr)
            v = r.op(eng, "tensor_add", hb[:, j * T:(j + 1) * T],
                     hb[:, j * T:(j + 1) * T], cslot, inc=sem)
            v_fix_c[j] = (sem, v)
        v_fix = v_fix_c[DC - 2]
        v_fix_pl = v_fix_c[DC - 1]
        TAP("hb", hb[:, 0:DC * T])

        if upto == "fix":
            return _finish(r, sems, tap_specs, nc, block), tap_specs

        v_r1 = {}
        last_sq_pe1 = {0: 0, 1: 0}
        silb = sil_t.bitcast(BF16)[:, 0:T]
        for m in range(DC):
            p = m % 2
            wc_tile = lambda c, p=p: wdtb_t[:, p * 2048 + c * 128: p * 2048 + (c + 1) * 128]
            r.acquire("sync", f"xs{p}")
            v_xs = r.dma("sync", xs_t[:, p * T:(p + 1) * T], d_xfm[m * 128:(m + 1) * 128, :], f"XS{p}")
            r.waitv("tensor", wload[("wc", m)])
            v_mix = {}
            for h in range(HFN):
                bk = p * 2 + h
                r.acquire("tensor", f"pb{bk}")
                for c in range(DC):
                    r.waitv("tensor", v_fix_c[c])
                    r.op("tensor", "matmul", pb[bk][:, :], wc_tile(c),
                         hb[:, c * T + h * HF: c * T + (h + 1) * HF],
                         start=(c == 0), stop=(c == DC - 1), inc="PE" if c == DC - 1 else None)
                v_mix[h] = r.cnt["PE"]
            r.consume(f"wdtb{p}", "PE", v_mix[1])
            if m + 2 < DC:
                r.acquire("gpsimd", f"wdtb{p}")
                wload[("wc", m + 2)] = r.dma("gpsimd", wdtb_t[:, p * 2048:(p + 1) * 2048], d_wc[m + 2], f"WDT{p}")
            v = None
            for h in range(HFN):
                bk = p * 2 + h
                r.wait("vector", "PE", v_mix[h])
                r.waitv("vector", v_xs)
                r.wait("vector", "DVE", v_ds1)
                v = r.op("vector", "scalar_tensor_tensor",
                         U2[:, m * T + h * HF: m * T + (h + 1) * HF],
                         xs_t[:, p * T + h * HF: p * T + (h + 1) * HF],
                         ds1_t[:, m:m + 1], pb[bk][:, :], ALU.mult, ALU.add, inc="DVE")
                r.consume(f"pb{bk}", "DVE", v)
            r.consume(f"xs{p}", "DVE", v)
            v_r1[m] = v
            for h in range(HFN):
                r.acquire("scalar", f"sil{h}")
                r.wait("scalar", "DVE", v_r1[m])
                if last_sq_pe1[h]:
                    r.wait("scalar", "PE", last_sq_pe1[h])
                v_sq = r.op("scalar", "activation", silb[:, h * HF:(h + 1) * HF],
                            U2[:, m * T + h * HF: m * T + (h + 1) * HF], AF.Square, inc="ACT")
                r.wait("tensor", "ACT", v_sq)
                r.waitv("tensor", v_seb)
                if m == 0:
                    r.acquire("tensor", f"pb{4 + h}")
                    r.acquire("tensor", f"pb{6 + h}")
                r.op("tensor", "matmul", pb[4 + h][0:1, :], onesc_t[:, :],
                     U2[:, m * T + h * HF: m * T + (h + 1) * HF],
                     start=(m == 0), stop=(m == DC - 1))
                last_sq_pe1[h] = r.op("tensor", "matmul", pb[6 + h][0:1, :], onescb_t[:, :],
                                      silb[:, h * HF:(h + 1) * HF],
                                      start=(m == 0), stop=(m == DC - 1), inc="PE")
        for h in range(HFN):
            r.consume(f"sil{h}", "PE", last_sq_pe1[h])
        TAP("r1", U2[:, :])

        if upto == "A2":
            return _finish(r, sems, tap_specs, nc, block), tap_specs

        def emit_ln_stats_chunk(src_fn, v_src, m, last_sq_pe):
            for h in range(HFN):
                r.acquire("scalar", f"sil{h}")
                r.wait("scalar", "DVE", v_src)
                if last_sq_pe[h]:
                    r.wait("scalar", "PE", last_sq_pe[h])
                v_sq = r.op("scalar", "activation", silb[:, h * HF:(h + 1) * HF],
                            src_fn(m, h), AF.Square, inc="ACT")
                r.wait("tensor", "ACT", v_sq)
                if m == 0:
                    r.acquire("tensor", f"pb{4 + h}")
                    r.acquire("tensor", f"pb{6 + h}")
                r.op("tensor", "matmul", pb[4 + h][0:1, :], onesc_t[:, :], src_fn(m, h),
                     start=(m == 0), stop=(m == DC - 1))
                last_sq_pe[h] = r.op("tensor", "matmul", pb[6 + h][0:1, :], onescb_t[:, :],
                                     silb[:, h * HF:(h + 1) * HF],
                                     start=(m == 0), stop=(m == DC - 1), inc="PE")

        def emit_ln(src_fn, v_src_fn):
            last_sq_pe = {0: 0, 1: 0}
            for m in range(DC):
                emit_ln_stats_chunk(src_fn, v_src_fn(m), m, last_sq_pe)
            v_stats = r.cnt["PE"]
            for h in range(HFN):
                r.consume(f"sil{h}", "PE", last_sq_pe[h])
            return emit_ln_rows(v_stats)
        def emit_ln_rows(v_stats=None):
            if v_stats is None:
                v_stats = r.cnt["PE"]
            mrow = rows8_t[0:1, 0:1024]
            rrow = rows8_t[0:1, 1024:2048]
            r.acquire("vector", "rows8_0")
            r.acquire("vector", "rows8_1")
            r.wait("vector", "PE", v_stats)

            def row_step(attr, *a, **k):
                v = r.op("vector", attr, *a, inc="DVE", **k)
                r.wait("vector", "DVE", v)
                return v

            for h in range(HFN):
                hs = slice(h * HF, (h + 1) * HF)
                row_step("tensor_scalar_mul", mrow[:, hs], pb[4 + h][0:1, :], 1.0 / 2048.0)
            row_step("tensor_scalar_mul", rrow[:, 0:HF], pb[6][0:1, :], 1.0 / 2048.0)
            row_step("tensor_scalar_mul", rrow[:, HF:2 * HF], pb[7][0:1, :], 1.0 / 2048.0)
            row_step("tensor_mul", srow_t[0:1, 0:1024], mrow[:, :], mrow[:, :])
            v_var = row_step("tensor_sub", rrow[:, :], rrow[:, :], srow_t[0:1, 0:1024])
            for i in range(4, 8):
                r.consume(f"pb{i}", "DVE", v_var)
            r.wait("scalar", "DVE", v_var)
            v_ln = r.op("scalar", "activation", srow_t[0:1, 0:1024], rrow[:, :], AF.Ln,
                        bias=cst_t[0:1, 0:1], inc="ACT")
            r.wait("scalar", "ACT", v_ln)
            v_rstd = r.op("scalar", "activation", rrow[:, :], srow_t[0:1, 0:1024], AF.Exp,
                          scale=cst_t[0:1, 1:2], inc="ACT")
            r.wait("tensor", "DVE", v_var)
            r.wait("tensor", "ACT", v_rstd)
            v_bc = None
            for h in range(HFN):
                hs = slice(h * HF, (h + 1) * HF)
                r.acquire("tensor", f"pb{4 + h}")
                r.acquire("tensor", f"pb{6 + h}")
                r.op("tensor", "matmul", pb[4 + h][:, :], ones1_t[:, :], mrow[:, hs],
                     start=True, stop=True)
                v_bc = r.op("tensor", "matmul", pb[6 + h][:, :], ones1_t[:, :], rrow[:, hs],
                            start=True, stop=True, inc="PE")
            r.consume("rows8_0", "PE", v_bc)
            r.consume("rows8_1", "PE", v_bc)
            return v_bc

        v_bc1 = emit_ln_rows()
        v_spill = {}
        v_cast_m = {}
        v_norm1 = None
        for m in range(DC):
            v = None
            for h in range(HFN):
                sl = U2[:, m * T + h * HF: m * T + (h + 1) * HF]
                r.wait("vector", "PE", v_bc1)
                r.acquire("vector", f"sil{h}")
                tb = sil_t[:, h * HF:(h + 1) * HF]
                r.op("vector", "tensor_sub", tb, sl, pb[4 + h][:, :])
                v = r.op("vector", "tensor_mul", sl, tb, pb[6 + h][:, :], inc="DVE")
                r.consume(f"sil{h}", "DVE", v)
            v_norm1 = v
            r.wait("gpsimd", "DVE", v)
            v_cast_m[m] = r.op("gpsimd", "tensor_copy", xb_t[:, m * T:(m + 1) * T],
                               U2[:, m * T:(m + 1) * T], inc="PL")
            r.wait("sync", "DVE", v)
            v_spill[m] = r.dma("sync", d_r1d[:, m * T:(m + 1) * T], U2[:, m * T:(m + 1) * T], "SP")
        for i in range(4, 8):
            r.consume(f"pb{i}", "DVE", v_norm1)
        v_x1_pl = r.cnt["PL"]
        v_spill_all = v_spill[DC - 1]
        TAP("x1", U2[:, :])

        if upto == "ln1":
            return _finish(r, sems, tap_specs, nc, block), tap_specs

        r.waitv("tensor", v_seb)
        for h in range(HFN):
            r.acquire("tensor", f"pb{h}")
            for c in range(DC):
                r.wait("tensor", "PL", v_cast_m[c])
                r.op("tensor", "matmul", pb[h][0:16, :], rwb_t[:, c * 16:(c + 1) * 16],
                     xb[:, c * T + h * HF: c * T + (h + 1) * HF],
                     start=(c == 0), stop=(c == DC - 1), inc="PE" if c == DC - 1 else None)
        v_logit = r.cnt["PE"]
        r.acquire("scalar", "xs0")
        r.acquire("scalar", "xs1")
        r.wait("scalar", "PE", v_logit)
        r.op("scalar", "copy", lg_s[:, 0:HF], pb[0][0:16, :])
        v_lg = r.op("scalar", "copy", lg_s[:, HF:2 * HF], pb[1][0:16, :], inc="ACT")
        r.consume("pb0", "ACT", v_lg)
        r.consume("pb1", "ACT", v_lg)
        r.wait("tensor", "ACT", v_lg)
        r.acquire("tensor", "pb2")
        for b in range(8):
            r.op("tensor", "transpose", pb[2][:, b * 16:(b + 1) * 16],
                 lg_s[:, b * 128:(b + 1) * 128], idn_t[0:16, 0:16],
                 inc="PE" if b == 7 else None)
        v_tp = r.cnt["PE"]
        r.wait("scalar", "PE", v_tp)
        v_probs = r.op("scalar", "copy", probs_t[:, :], pb[2][:, 0:128], inc="ACT")
        r.consume("pb2", "ACT", v_probs)
        r.wait("scalar", "ACT", v_probs)
        v_e = r.op("scalar", "activation", pr2_t[:, :], probs_t[:, :], AF.Exp, inc="ACT")
        r.wait("vector", "ACT", v_e)
        for b in range(8):
            v_s = r.op("vector", "reduce_sum", ssum8_t[:, b:b + 1], pr2_t[:, b * 16:(b + 1) * 16],
                       AX.X, inc="DVE")
        r.wait("vector", "DVE", v_s)
        v_rc = r.op("vector", "reciprocal", rcp8_t[:, :], ssum8_t[:, :], inc="DVE")
        r.wait("vector", "DVE", v_rc)
        for b in range(8):
            v_wt = r.op("vector", "tensor_single_scalar", wtm_t[:, b * 16:(b + 1) * 16],
                        pr2_t[:, b * 16:(b + 1) * 16], rcp8_t[:, b:b + 1], ALU.mult, inc="DVE")
        r.wait("vector", "DVE", v_wt)
        for b in range(8):
            v_m8 = r.op("vector", "max", out=m8_8_t[:, b * 8:(b + 1) * 8],
                        in_=wtm_t[:, b * 16:(b + 1) * 16], inc="DVE")
        r.wait("vector", "DVE", v_m8)
        for b in range(8):
            v_mk = r.op("vector", "tensor_single_scalar", msk_t[:, b * 16:(b + 1) * 16],
                        wtm_t[:, b * 16:(b + 1) * 16], m8_8_t[:, b * 8 + 3:b * 8 + 4],
                        ALU.is_ge, inc="DVE")
        r.wait("vector", "DVE", v_mk)
        for b in range(8):
            v_w = r.op("vector", "tensor_mul", pr2_t[:, b * 16:(b + 1) * 16],
                       msk_t[:, b * 16:(b + 1) * 16], wtm_t[:, b * 16:(b + 1) * 16], inc="DVE")
        r.wait("vector", "DVE", v_w)
        TAP("wdense_tm", pr2_t[:, :])
        r.wait("tensor", "DVE", v_w)
        r.acquire("tensor", "pb3")
        r.acquire("tensor", "pb4")
        for b in range(8):
            bk = 3 if b < 4 else 4
            r.op("tensor", "transpose", pb[bk][0:16, (b % 4) * 128:(b % 4 + 1) * 128],
                 pr2_t[:, b * 16:(b + 1) * 16], idn_t[:, :], inc="PE" if b == 7 else None)
        v_wt = r.cnt["PE"]
        r.wait("scalar", "PE", v_wt)
        r.op("scalar", "copy", wfm[:, 0:HF], pb[3][0:16, 0:512])
        v_wfm = r.op("scalar", "copy", wfm[:, HF:2 * HF], pb[4][0:16, 0:512], inc="ACT")
        r.consume("pb3", "ACT", v_wfm)
        r.consume("pb4", "ACT", v_wfm)

        r.acquire("tensor", "pb5")
        r.acquire("tensor", "pb2")
        for h in range(HFN):
            bank = pb[5] if h == 0 else pb[2]
            for c in range(DC):
                r.op("tensor", "matmul", bank[0:1, :], seb_t[:, c:c + 1],
                     xb[:, c * T + h * HF: c * T + (h + 1) * HF],
                     start=(c == 0), stop=(c == DC - 1), inc="PE" if c == DC - 1 else None)
        v_se = r.cnt["PE"]
        r.wait("scalar", "PE", v_se)
        r.op("scalar", "activation", srow_t[0:1, 0:HF], pb[5][0:1, :], AF.Exp, scale=-1.0)
        v_sex = r.op("scalar", "activation", srow_t[0:1, HF:2 * HF], pb[2][0:1, :], AF.Exp,
                     scale=-1.0, inc="ACT")
        r.consume("pb5", "ACT", v_sex)
        r.consume("pb2", "ACT", v_sex)
        r.wait("vector", "ACT", v_sex)
        v_sa = r.op("vector", "tensor_scalar_add", srow_t[0:1, :], srow_t[0:1, :], 1.0, inc="DVE")
        r.wait("vector", "DVE", v_sa)
        v_sig = r.op("vector", "reciprocal", srow_t[0:1, :], srow_t[0:1, :], inc="DVE")
        r.wait("vector", "DVE", v_sig)
        r.wait("tensor", "DVE", v_sig)
        r.acquire("tensor", "pb6")
        r.acquire("tensor", "pb7")
        r.op("tensor", "matmul", pb[6][:, :], ones1_t[:, :], srow_t[0:1, 0:HF], start=True, stop=True)
        v_sbc_pe = r.op("tensor", "matmul", pb[7][:, :], ones1_t[:, :], srow_t[0:1, HF:2 * HF],
                        start=True, stop=True, inc="PE")
        r.wait("scalar", "PE", v_sbc_pe)
        r.waitv("scalar", v_spill_all)
        r.op("scalar", "copy", s_bc[:, 0:HF], pb[6][:, :])
        v_sbc = r.op("scalar", "copy", s_bc[:, HF:2 * HF], pb[7][:, :], inc="ACT")
        r.consume("pb6", "ACT", v_sbc)
        r.consume("pb7", "ACT", v_sbc)
        TAP("s_bc", s_bc)

        if upto == "router":
            return _finish(r, sems, tap_specs, nc, block), tap_specs

        he = lambda c, h: heb[:, c * T + h * HF: c * T + (h + 1) * HF]
        dwb = lambda q: heb[:, 11264 + q * 1408: 11264 + (q + 1) * 1408]
        gwb = lambda q: wdtb_t[:, q * 2048:(q + 1) * 2048]
        uwb = lambda q: wbb_t[:, q * 2048:(q + 1) * 2048]

        jobs = [("sg", g) for g in range(SGN)] + [("ex", e) for e in range(NE)]
        gu_load, dw_load = {}, {}

        r.waitv("gpsimd", v_spill_all)
        r.waitv("vector", v_spill_all)

        def gu_dma(jkind, jid, m):
            q = m % 2
            src_g = d_sgw[jid, m] if jkind == "sg" else d_gw[jid, m]
            src_u = d_suw[jid, m] if jkind == "sg" else d_uw[jid, m]
            r.acquire("gpsimd", f"wdtb{q}")
            gu_load[("g", jkind, jid, m)] = r.dma("gpsimd", gwb(q), src_g, f"GW{q}")
            r.acquire("gpsimd", f"wbb{q}")
            gu_load[("u", jkind, jid, m)] = r.dma("gpsimd", uwb(q), src_u, f"UW{q}")

        def dw_dma(jkind, jid, m2):
            q = m2 % 2
            src = d_sdw[m2][:, jid * 1408:(jid + 1) * 1408] if jkind == "sg" else d_dw[jid, m2]
            r.acquire("gpsimd", f"dwb{q}")
            dw_load[(jkind, jid, m2)] = r.dma("gpsimd", dwb(q), src, f"DW{q}")

        gu_dma(*jobs[0], 0)
        gu_dma(*jobs[0], 1)

        v_acc = {}
        for ji, (jkind, jid) in enumerate(jobs):
            if jkind == "ex":
                e = jid
                sslot = e % 2
                r.acquire("sync", f"rows8_{sslot}")
                r.wait("sync", "ACT", v_wfm)
                v_wrow = r.dma("sync", rows8_t[0:1, sslot * 1024:sslot * 1024 + T],
                               wfm[e:e + 1, 0:T], f"WR{sslot}")
                r.waitv("tensor", v_wrow)
                r.acquire("tensor", "pb6")
                r.acquire("tensor", "pb7")
                r.op("tensor", "matmul", pb[6][:, :], ones1_t[:, :],
                     rows8_t[0:1, sslot * 1024:sslot * 1024 + HF], start=True, stop=True)
                v_pe_bc = r.op("tensor", "matmul", pb[7][:, :], ones1_t[:, :],
                               rows8_t[0:1, sslot * 1024 + HF:sslot * 1024 + T],
                               start=True, stop=True, inc="PE")
                r.consume(f"rows8_{sslot}", "PE", v_pe_bc)
                r.wait("scalar", "PE", v_pe_bc)
                r.acquire("scalar", "w_bc")
                r.op("scalar", "copy", w_bc[:, 0:HF], pb[6][:, :])
                v_wbc = r.op("scalar", "copy", w_bc[:, HF:2 * HF], pb[7][:, :], inc="ACT")
                r.consume("pb6", "ACT", v_wbc)
                r.consume("pb7", "ACT", v_wbc)
                scale_tile, v_scale = w_bc, ("ACT", v_wbc)
            else:
                scale_tile, v_scale = s_bc, ("ACT", v_sbc)

            dw_dma(jkind, jid, 0)
            dw_dma(jkind, jid, 1)
            v_gu_last = None
            v_gu = {}
            for m in range(IC):
                q = m % 2
                r.waitv("tensor", gu_load[("g", jkind, jid, m)])
                r.waitv("tensor", gu_load[("u", jkind, jid, m)])
                for h in range(HFN):
                    r.acquire("tensor", f"pb{h}")
                    for c in range(DC):
                        r.op("tensor", "matmul", pb[h][:, :], gwb(q)[:, c * 128:(c + 1) * 128],
                             xb[:, c * T + h * HF: c * T + (h + 1) * HF],
                             start=(c == 0), stop=(c == DC - 1), inc="PE" if c == DC - 1 else None)
                    v_g = r.cnt["PE"]
                    r.acquire("tensor", f"pb{2 + h}")
                    for c in range(DC):
                        r.op("tensor", "matmul", pb[2 + h][:, :], uwb(q)[:, c * 128:(c + 1) * 128],
                             xb[:, c * T + h * HF: c * T + (h + 1) * HF],
                             start=(c == 0), stop=(c == DC - 1), inc="PE" if c == DC - 1 else None)
                    v_upe = r.cnt["PE"]
                    if h == 1:
                        r.consume(f"wdtb{q}", "PE", v_upe)
                        r.consume(f"wbb{q}", "PE", v_upe)
                    r.acquire("scalar", f"sil{h}")
                    r.wait("scalar", "PE", v_g)
                    v_sil = r.op("scalar", "activation", sil_t[:, h * HF:(h + 1) * HF],
                                 pb[h][:, :], AF.Silu, inc="ACT")
                    r.consume(f"pb{h}", "ACT", v_sil)
                    r.wait("vector", "ACT", v_sil)
                    r.wait("vector", "PE", v_upe)
                    v_h = r.op("vector", "tensor_mul", he(m, h), sil_t[:, h * HF:(h + 1) * HF],
                               pb[2 + h][:, :], inc="DVE")
                    r.consume(f"pb{2 + h}", "DVE", v_h)
                    r.consume(f"sil{h}", "DVE", v_h)
                    v_gu[(m, h)] = v_h
                    v_gu_last = v_h
                if m + 2 < IC:
                    gu_dma(jkind, jid, m + 2)
            if ji + 1 < len(jobs):
                njk, njid = jobs[ji + 1]
                gu_dma(njk, njid, 0)
                gu_dma(njk, njid, 1)
            for m2 in range(DC):
                q = m2 % 2
                r.waitv("tensor", dw_load[(jkind, jid, m2)])
                for h in range(HFN):
                    bk = 4 + h
                    r.acquire("tensor", f"pb{bk}")
                    for c in range(IC):
                        r.wait("tensor", "DVE", v_gu[(c, h)])
                        r.op("tensor", "matmul", pb[bk][:, :], dwb(q)[:, c * 128:(c + 1) * 128],
                             he(c, h), start=(c == 0), stop=(c == IC - 1),
                             inc="PE" if c == IC - 1 else None)
                    v_d = r.cnt["PE"]
                    if h == 1:
                        r.consume(f"dwb{q}", "PE", v_d)
                    r.wait("vector", "PE", v_d)
                    r.wait("vector", v_scale[0], v_scale[1])
                    acc_sl = accv[:, m2 * T + h * HF: m2 * T + (h + 1) * HF]
                    ssl = scale_tile[:, h * HF:(h + 1) * HF]
                    if ji == 0:
                        v_a = r.op("vector", "tensor_mul", acc_sl, ssl, pb[bk][:, :], inc="DVE")
                    else:
                        r.acquire("vector", f"tscr{h}")
                        tb = tscr[:, h * HF:(h + 1) * HF]
                        r.op("vector", "tensor_mul", tb, ssl, pb[bk][:, :])
                        v_a = r.op("vector", "tensor_add", acc_sl, acc_sl, tb, inc="DVE")
                        r.consume(f"tscr{h}", "DVE", v_a)
                    r.consume(f"pb{bk}", "DVE", v_a)
                    v_acc[(m2, h)] = v_a
                if m2 + 2 < DC:
                    dw_dma(jkind, jid, m2 + 2)
            if jkind == "ex":
                r.consume("w_bc", "DVE", v_acc[(DC - 1, 1)])

        TAP("acc", accv[:, :])

        if upto == "moe":
            return _finish(r, sems, tap_specs, nc, block), tap_specs

        v_r2 = {}
        for m in range(DC):
            q = m % 2
            rbs = xs_t[:, q * T:(q + 1) * T]
            r.acquire("sync", f"rb{q}")
            v_back = r.dma("sync", rbs, d_r1d[:, m * T:(m + 1) * T], f"XS{q}")
            r.waitv("vector", v_back)
            r.wait("vector", "DVE", v_acc[(m, 1)])
            v = r.op("vector", "tensor_add", accv[:, m * T:(m + 1) * T],
                     accv[:, m * T:(m + 1) * T], rbs, inc="DVE")
            r.consume(f"rb{q}", "DVE", v)
            v_r2[m] = v

        v_bc2 = emit_ln(lambda m, h: accv[:, m * T + h * HF: m * T + (h + 1) * HF],
                        lambda m: v_r2[m])
        TAP("r2", accv[:, :])
        TAP("rows2", rows8_t[:, :])
        if upto == "ln2a":
            return _finish(r, sems, tap_specs, nc, block), tap_specs
        for m in range(DC):
            v = None
            for h in range(HFN):
                sl = accv[:, m * T + h * HF: m * T + (h + 1) * HF]
                r.wait("vector", "PE", v_bc2)
                r.acquire("vector", f"tscr{h}")
                tb = tscr[:, h * HF:(h + 1) * HF]
                r.op("vector", "tensor_sub", tb, sl, pb[4 + h][:, :])
                v = r.op("vector", "tensor_mul", sl, tb, pb[6 + h][:, :], inc="DVE")
                r.consume(f"tscr{h}", "DVE", v)
            r.wait("sync", "DVE", v)
            r.dma("sync", d_y[m * 128:(m + 1) * 128, :], accv[:, m * T:(m + 1) * T], "Y")

        return _finish(r, sems, tap_specs, nc, block), tap_specs


def _finish(r, sems, tap_specs, nc, block):
    if tap_specs:
        for name in r.cnt:
            if name != "Y":
                r.wait("sync", name, r.cnt[name])
        for name in list(tap_specs):
            ap = tap_specs[name]
            d = nc.dram_tensor(f"tap_{name}", list(ap.shape), ap.dtype, kind="ExternalOutput").ap()
            r.dma("sync", d, ap, "Y")
    r.wait("sync", "Y", r.cnt["Y"])
    r.wait("gpsimd", "Y", r.cnt["Y"])

    @block.sync
    def _(eng):
        r.replay(eng, "sync", sems)

    @block.tensor
    def _(eng):
        r.replay(eng, "tensor", sems)

    @block.scalar
    def _(eng):
        r.replay(eng, "scalar", sems)

    @block.vector
    def _(eng):
        r.replay(eng, "vector", sems)

    @block.gpsimd
    def _(eng):
        r.replay(eng, "gpsimd", sems)

    return nc


def _pack_weights(i):
    f32 = lambda a: np.ascontiguousarray(np.asarray(a, dtype=np.float32))
    C = np.ascontiguousarray
    W_dt = f32(i["L0_attn_W_dt"]); W_B = f32(i["L0_attn_W_B"]); W_C = f32(i["L0_attn_W_C"])
    gw = f32(i["L0_ffn_gate_w"]); uw = f32(i["L0_ffn_up_w"]); dwt = f32(i["L0_ffn_down_w"])
    sgw = f32(i["L0_ffn_shared_gate_w"]); suw = f32(i["L0_ffn_shared_up_w"])
    sdw = f32(i["L0_ffn_shared_down_w"]); rw = f32(i["L0_ffn_router_w"]); se = f32(i["L0_ffn_se_gate_w"])

    d = {}
    d["wdt"] = C(W_dt.reshape(16, 128, 16, 128).transpose(0, 3, 2, 1).reshape(16, 128, 2048))
    d["wb"] = C(W_B.reshape(16, 128, 16, 128).transpose(0, 3, 2, 1).reshape(16, 128, 2048))
    d["wc"] = C(W_C.reshape(16, 128, 16, 128).transpose(0, 3, 2, 1).reshape(16, 128, 2048))
    d["gw"] = C(gw.reshape(16, 11, 128, 16, 128).transpose(0, 1, 4, 3, 2).reshape(16, 11, 128, 2048))
    d["uw"] = C(uw.reshape(16, 11, 128, 16, 128).transpose(0, 1, 4, 3, 2).reshape(16, 11, 128, 2048))
    d["dw"] = C(dwt.reshape(16, 16, 128, 11, 128).transpose(0, 1, 4, 3, 2).reshape(16, 16, 128, 1408))
    d["sgw"] = C(sgw.reshape(44, 128, 16, 128).transpose(0, 3, 2, 1).reshape(4, 11, 128, 2048))
    d["suw"] = C(suw.reshape(44, 128, 16, 128).transpose(0, 3, 2, 1).reshape(4, 11, 128, 2048))
    d["sdw"] = C(sdw.reshape(16, 128, 44, 128).transpose(0, 3, 2, 1).reshape(16, 128, 5632))
    d["rw"] = C(rw.reshape(16, 16, 128).transpose(2, 1, 0).reshape(128, 256))
    d["sew"] = C(se.reshape(16, 128).T)
    v16 = lambda v: C(np.asarray(v, np.float32).reshape(16, 128).T)
    d["bdt"] = v16(i["L0_attn_b_dt"])
    d["alog"] = v16(i["L0_attn_A_log"])
    d["dskip"] = v16(i["L0_attn_D_skip"])
    d["idn"] = np.eye(128, dtype=np.float32)
    d["ones1"] = np.ones((1, 128), np.float32)
    d["onesc"] = np.ones((128, 1), np.float32)
    cst = np.zeros((128, 4), np.float32); cst[:, 0] = EPS; cst[:, 1] = -0.5
    d["cst"] = cst
    return d


def make_in_maps(inputs, shared=None):
    x = np.asarray(inputs["L0_attn_x"], dtype=np.float32)
    if shared is None:
        shared = _pack_weights(inputs)
    in_maps = []
    for k in range(NCORE):
        m = dict(shared)
        m["xfm"] = np.ascontiguousarray(x[0, k * T:(k + 1) * T, :].T)
        m["sel"] = np.ascontiguousarray(
            np.tile((np.arange(8) < k).astype(np.float32), (128, 1)))
        in_maps.append(m)
    return in_maps


def kernel(**inputs):
    nc, _ = build_program(upto="full", taps=())
    in_maps = make_in_maps(inputs)
    res = run_bass_kernel_spmd(nc, in_maps, core_ids=list(range(NCORE)))
    out = np.empty((1, 8192, 2048), dtype=np.float32)
    for k in range(NCORE):
        out[0, k * T:(k + 1) * T, :] = res.results[k]["y"].T
    return out


# version 09
# speedup vs baseline: 1.0050x; 1.0029x over previous
import numpy as np
from collections import defaultdict
from contextlib import ExitStack

import concourse.bass as bass
import concourse.mybir as mybir
from concourse.bass_utils import run_bass_kernel_spmd


F32 = mybir.dt.float32
BF16 = mybir.dt.bfloat16
AF = mybir.ActivationFunctionType
ALU = mybir.AluOpType
AX = mybir.AxisListType

NCORE = 8
T = 1024
HFN = 2
HF = 512
DC = 16
IC = 11
NE = 16
SGN = 4
EPS = 1e-5
ENGINES = ("sync", "tensor", "scalar", "vector", "gpsimd")


class Rec:

    def __init__(self):
        self.q = {e: [] for e in ENGINES}
        self.cnt = defaultdict(int)
        self.waited = defaultdict(int)
        self.released = {}
        self.n = defaultdict(int)

    def op(self, eng, attr, *a, inc=None, incby=1, **k):
        rec = [attr, a, k, None]
        self.q[eng].append(rec)
        self.n[eng] += 1
        if inc is not None:
            self.cnt[inc] += incby
            rec[3] = (inc, incby)
            return self.cnt[inc]
        return None

    def wait(self, eng, sem, val):
        if val is None or val <= 0:
            return
        if self.waited[(eng, sem)] >= val:
            return
        self.waited[(eng, sem)] = val
        self.q[eng].append(["wait_ge", (("SEM", sem), val), {}, None])
        self.n[eng] += 1

    def dma(self, eng, out, in_, sem):
        v = self.op(eng, "dma_start", inc=sem, incby=16, out=out, in_=in_)
        return (sem, v)

    def waitv(self, eng, sv):
        self.wait(eng, sv[0], sv[1])

    def consume(self, key, sem, val):
        self.released.setdefault(key, []).append((sem, val))

    def acquire(self, eng, key):
        for sem, val in self.released.pop(key, []):
            self.wait(eng, sem, val)

    def replay(self, eng_obj, eng_name, sems):
        for attr, a, k, then in self.q[eng_name]:
            a = [sems[x[1]] if isinstance(x, tuple) and len(x) == 2 and x[0] == "SEM" else x
                 for x in a]
            r = getattr(eng_obj, attr)(*a, **k)
            if then is not None:
                r.then_inc(sems[then[0]], then[1])


def build_program(upto="full", taps=()):
    nc = bass.Bass("TRN2", target_bir_lowering=False, debug=False)

    d_xfm = nc.dram_tensor("xfm", [2048, T], F32, kind="ExternalInput").ap()
    d_wdt = nc.dram_tensor("wdt", [DC, 128, 2048], F32, kind="ExternalInput").ap()
    d_wb = nc.dram_tensor("wb", [DC, 128, 2048], F32, kind="ExternalInput").ap()
    d_wc = nc.dram_tensor("wc", [DC, 128, 2048], F32, kind="ExternalInput").ap()
    d_gw = nc.dram_tensor("gw", [NE, IC, 128, 2048], F32, kind="ExternalInput").ap()
    d_uw = nc.dram_tensor("uw", [NE, IC, 128, 2048], F32, kind="ExternalInput").ap()
    d_dw = nc.dram_tensor("dw", [NE, DC, 128, 1408], F32, kind="ExternalInput").ap()
    d_sgw = nc.dram_tensor("sgw", [SGN, IC, 128, 2048], F32, kind="ExternalInput").ap()
    d_suw = nc.dram_tensor("suw", [SGN, IC, 128, 2048], F32, kind="ExternalInput").ap()
    d_sdw = nc.dram_tensor("sdw", [DC, 128, 5632], F32, kind="ExternalInput").ap()
    d_rw = nc.dram_tensor("rw", [128, 256], F32, kind="ExternalInput").ap()
    d_sew = nc.dram_tensor("sew", [128, 16], F32, kind="ExternalInput").ap()
    d_bdt = nc.dram_tensor("bdt", [128, 16], F32, kind="ExternalInput").ap()
    d_alog = nc.dram_tensor("alog", [128, 16], F32, kind="ExternalInput").ap()
    d_dskip = nc.dram_tensor("dskip", [128, 16], F32, kind="ExternalInput").ap()
    d_sel = nc.dram_tensor("sel", [128, 8], F32, kind="ExternalInput").ap()
    d_idn = nc.dram_tensor("idn", [128, 128], F32, kind="ExternalInput").ap()
    d_ones1 = nc.dram_tensor("ones1", [1, 128], F32, kind="ExternalInput").ap()
    d_onesc = nc.dram_tensor("onesc", [128, 1], F32, kind="ExternalInput").ap()
    d_cst = nc.dram_tensor("cst", [128, 4], F32, kind="ExternalInput").ap()

    d_ccin = nc.dram_tensor("cc_in", [128, 32], F32, kind="Internal").ap()
    d_ccout = nc.dram_tensor("cc_out", [128 * NCORE, 32], F32, kind="Internal").ap()
    d_r1d = nc.dram_tensor("r1d", [128, DC * T], F32, kind="Internal").ap()
    d_y = nc.dram_tensor("y", [2048, T], F32, kind="ExternalOutput").ap()

    tap_specs = {}

    with ExitStack() as ctx:
        sb = lambda name, shape, dt: ctx.enter_context(nc.sbuf_tensor(name, shape, dt))
        xb_t = sb("xb", [128, DC * T], BF16)
        U1_t = sb("U1", [128, DC * T], F32)
        U2_t = sb("U2", [128, DC * T], F32)
        wdtb_t = sb("wdtb", [128, 2 * 2048], BF16)
        wbb_t = sb("wbb", [128, 2 * 2048], BF16)
        xs_t = sb("xs", [128, 2 * T], F32)
        sil_t = sb("sil", [128, T], F32)
        rows8_t = sb("rows8", [1, 2048], F32)
        srow_t = sb("srow", [1, T], F32)
        probs_t = sb("probs", [128, 128], F32)
        pr2_t = sb("pr2", [128, 128], F32)
        wtm_t = sb("wtm", [128, 128], F32)
        m8_t = sb("m8", [128, 8], F32)
        msk_t = sb("msk", [128, 128], F32)
        ssum8_t = sb("ssum8", [128, 8], F32)
        rcp8_t = sb("rcp8", [128, 8], F32)
        m8_8_t = sb("m8_8", [128, 64], F32)
        nmx_t = sb("nmx", [128, 1], F32)
        ssum_t = sb("ssum", [128, 1], F32)
        rcp_t = sb("rcp", [128, 1], F32)
        hP_t = sb("hP", [128, 32], F32)
        carr_t = sb("carr", [128, 256], F32)
        cvec_t = sb("cvec", [128, 16], F32)
        ct1_t = sb("ct1", [128, 16], F32)
        bdt_t = sb("bdt_s", [128, 16], F32)
        alog_t = sb("alog_s", [128, 16], F32)
        dskip_t = sb("dskip_s", [128, 16], F32)
        A_t = sb("A_s", [128, 16], F32)
        ds1_t = sb("ds1_s", [128, 16], F32)
        sel_t = sb("sel_s", [128, 8], F32)
        idn_t = sb("idn_s", [128, 128], F32)
        ones1_t = sb("ones1_s", [1, 128], F32)
        onesc_t = sb("onesc_s", [128, 1], F32)
        onescb_t = sb("onescb_s", [128, 1], BF16)
        cst_t = sb("cst_s", [128, 4], F32)
        rwb_t = sb("rwb", [128, 256], BF16)
        seb_t = sb("seb", [128, 16], BF16)

        pb = [ctx.enter_context(nc.psum_tensor(f"pb{i}", [128, 512], F32)) for i in range(8)]

        semnames = ("SM", "XB", "RW", "WDT0", "WDT1", "WBB0", "WBB1", "XS0", "XS1",
                    "GW0", "GW1", "UW0", "UW1", "DW0", "DW1", "SP", "WR0", "WR1",
                    "CCS", "Y", "CC", "PE", "ACT", "DVE", "PL")
        sems = {n: ctx.enter_context(nc.semaphore(n)) for n in semnames}
        block = ctx.enter_context(nc.Block())

        xb = xb_t[:, :]
        hb = U1_t.bitcast(BF16)
        accv = U1_t
        U2 = U2_t
        heb = U2_t.bitcast(BF16)
        slot = lambda s: U2[:, s * T:(s + 1) * T]
        w_bc = U2[:, 7040:8064]
        s_bc = U2[:, 8064:9088]
        tscr = U2[:, 9088:11136]
        zeros = sil_t[:, 0:T]
        lg_s = xs_t[0:16, 0:T]
        wfm = xs_t[0:16, T:2 * T]

        r = Rec()

        def TAP(name, ap):
            if name in taps:
                tap_specs[name] = ap

        r.dma("sync", bdt_t[:, :], d_bdt, "SM")
        r.dma("sync", alog_t[:, :], d_alog, "SM")
        r.dma("sync", dskip_t[:, :], d_dskip, "SM")
        r.dma("sync", sel_t[:, :], d_sel, "SM")
        r.dma("sync", idn_t[:, :], d_idn, "SM")
        r.dma("sync", ones1_t[:, :], d_ones1, "SM")
        r.dma("sync", onesc_t[:, :], d_onesc, "SM")
        v_smalls = r.dma("sync", cst_t[:, :], d_cst, "SM")

        for j in range(DC):
            v_xb = r.dma("gpsimd", xb_t[:, j * T:(j + 1) * T], d_xfm[j * 128:(j + 1) * 128, :], "XB")
        r.dma("gpsimd", onescb_t[:, :], d_onesc, "RW")
        r.dma("gpsimd", rwb_t[:, :], d_rw, "RW")
        v_seb = r.dma("gpsimd", seb_t[:, :], d_sew, "RW")

        r.waitv("scalar", v_smalls)
        v_atmp = r.op("scalar", "activation", A_t[:, :], alog_t[:, :], AF.Exp, inc="ACT")
        r.wait("vector", "ACT", v_atmp)
        v_A = r.op("vector", "tensor_scalar_mul", A_t[:, :], A_t[:, :], -1.0, inc="DVE")
        r.waitv("vector", v_smalls)
        v_ds1 = r.op("vector", "tensor_scalar_add", ds1_t[:, :], dskip_t[:, :], 1.0, inc="DVE")
        v_zero = r.op("vector", "memset", zeros, 0.0, inc="DVE")

        wload = {}
        for j0 in range(2):
            wload[("wdt", j0)] = r.dma("gpsimd", wdtb_t[:, (j0 % 2) * 2048:(j0 % 2 + 1) * 2048], d_wdt[j0], f"WDT{j0 % 2}")
            wload[("wb", j0)] = r.dma("gpsimd", wbb_t[:, (j0 % 2) * 2048:(j0 % 2 + 1) * 2048], d_wb[j0], f"WBB{j0 % 2}")

        v_scan_last = None
        pending_cast = []
        for j in range(DC):
            p = j % 2
            wdt_tile = lambda c, p=p: wdtb_t[:, p * 2048 + c * 128: p * 2048 + (c + 1) * 128]
            wb_tile = lambda c, p=p: wbb_t[:, p * 2048 + c * 128: p * 2048 + (c + 1) * 128]
            s_dt, s_ab, s_l, s_r, s_a, s_e, s_hl, s_pq = (slot(0 + p), slot(2 + p), slot(4 + p),
                                                          slot(6 + p), slot(8 + p), slot(10 + p),
                                                          slot(12 + p), slot(14 + p))
            r.acquire("sync", f"xs{p}")
            v_xs = r.dma("sync", xs_t[:, p * T:(p + 1) * T], d_xfm[j * 128:(j + 1) * 128, :], f"XS{p}")

            r.waitv("tensor", v_xb)
            r.waitv("tensor", wload[("wdt", j)])
            r.waitv("tensor", wload[("wb", j)])
            v_dt_g, v_bp_g = {}, {}
            for h in range(HFN):
                r.acquire("tensor", f"pb{p * 4 + h}")
                for c in range(DC):
                    r.op("tensor", "matmul", pb[p * 4 + h][:, :], wdt_tile(c),
                         xb[:, c * T + h * HF: c * T + (h + 1) * HF],
                         start=(c == 0), stop=(c == DC - 1), inc="PE" if c == DC - 1 else None)
                v_dt_g[h] = r.cnt["PE"]
            for h in range(HFN):
                r.acquire("tensor", f"pb{p * 4 + 2 + h}")
                for c in range(DC):
                    r.op("tensor", "matmul", pb[p * 4 + 2 + h][:, :], wb_tile(c),
                         xb[:, c * T + h * HF: c * T + (h + 1) * HF],
                         start=(c == 0), stop=(c == DC - 1), inc="PE" if c == DC - 1 else None)
                v_bp_g[h] = r.cnt["PE"]
            r.consume(f"wdtb{p}", "PE", v_dt_g[1])
            r.consume(f"wbb{p}", "PE", v_bp_g[1])
            if j + 2 < DC:
                r.acquire("gpsimd", f"wdtb{p}")
                wload[("wdt", j + 2)] = r.dma("gpsimd", wdtb_t[:, p * 2048:(p + 1) * 2048], d_wdt[j + 2], f"WDT{p}")
                r.acquire("gpsimd", f"wbb{p}")
                wload[("wb", j + 2)] = r.dma("gpsimd", wbb_t[:, p * 2048:(p + 1) * 2048], d_wb[j + 2], f"WBB{p}")

            v_sp = {}
            for h in range(HFN):
                bank = pb[p * 4 + h]
                hs = slice(h * HF, (h + 1) * HF)
                r.acquire("scalar", f"sp_lr{p}_{h}")
                r.acquire("scalar", f"sp_ae{p}_{h}")
                r.wait("scalar", "PE", v_dt_g[h])
                r.op("scalar", "activation", s_r[:, hs], bank[:, :], AF.Relu, bias=bdt_t[:, j:j + 1])
                v_abs = r.op("scalar", "activation", s_a[:, hs], bank[:, :], AF.Abs,
                             bias=bdt_t[:, j:j + 1], inc="ACT")
                r.consume(f"pb{p * 4 + h}", "ACT", v_abs)
                r.op("scalar", "activation", s_e[:, hs], s_a[:, hs], AF.Exp, scale=-1.0)
                v_sp[h] = r.op("scalar", "activation", s_l[:, hs], s_e[:, hs], AF.Ln, bias=1.0, inc="ACT")

            v_dt, v_u, v_abar = {}, {}, {}
            for h in range(HFN):
                hs = slice(h * HF, (h + 1) * HF)
                r.wait("gpsimd", "ACT", v_sp[h])
                v_dt[h] = r.op("gpsimd", "tensor_add", s_dt[:, hs], s_l[:, hs], s_r[:, hs], inc="PL")
                r.wait("scalar", "PL", v_dt[h])
                r.wait("scalar", "DVE", v_A)
                v_abar[h] = r.op("scalar", "activation", s_ab[:, hs], s_dt[:, hs], AF.Exp,
                                 scale=A_t[:, j:j + 1], inc="ACT")
                bank = pb[p * 4 + 2 + h]
                r.wait("vector", "PE", v_bp_g[h])
                r.wait("vector", "PL", v_dt[h])
                v_t = r.op("vector", "tensor_mul", s_a[:, hs], s_dt[:, hs], bank[:, :], inc="DVE")
                r.consume(f"pb{p * 4 + 2 + h}", "DVE", v_t)
                r.waitv("vector", v_xs)
                v_u[h] = r.op("vector", "tensor_mul", s_e[:, hs], s_a[:, hs],
                              xs_t[:, p * T + h * HF: p * T + (h + 1) * HF], inc="DVE")
            r.consume(f"xs{p}", "DVE", v_u[1])

            r.acquire("vector", f"sp_hp{p}")
            r.wait("vector", "ACT", v_abar[0])
            r.wait("vector", "ACT", v_abar[1])
            r.wait("vector", "DVE", v_zero)
            r.op("vector", "tensor_tensor_scan", s_hl[:, :], s_ab[:, :], s_e[:, :], 0.0, ALU.mult, ALU.add)
            v_scan_p = r.op("vector", "tensor_tensor_scan", s_pq[:, :], s_ab[:, :], zeros, 1.0,
                            ALU.mult, ALU.add, inc="DVE")
            v_scan_last = v_scan_p
            r.consume(f"sp_lr{p}_0", "DVE", v_dt[0])
            r.consume(f"sp_lr{p}_1", "DVE", v_dt[1])
            r.consume(f"sp_ae{p}_0", "DVE", v_scan_p)
            r.consume(f"sp_ae{p}_1", "DVE", v_scan_p)

            r.wait("gpsimd", "DVE", v_scan_p)
            r.op("gpsimd", "tensor_copy", hP_t[:, j:j + 1], s_hl[:, T - 1:T])
            v_pl = r.op("gpsimd", "tensor_copy", hP_t[:, 16 + j:17 + j], s_pq[:, T - 1:T], inc="PL")
            r.consume(f"sp_hp{p}", "PL", v_pl)

            def emit_cast(jc, s_hl=s_hl, s_pq=s_pq, v_scan_p=v_scan_p, pc=p):
                r.wait("scalar", "DVE", v_scan_p)
                r.op("scalar", "activation", hb[:, jc * T:(jc + 1) * T], s_hl[:, :], AF.Copy)
                v_cast = r.op("scalar", "activation", hb[:, (DC + jc) * T:(DC + jc + 1) * T],
                              s_pq[:, :], AF.Copy, inc="ACT")
                r.consume(f"sp_hp{pc}", "ACT", v_cast)

            pending_cast.append((j, emit_cast))
            if len(pending_cast) > 1:
                jc, fn = pending_cast.pop(0)
                fn(jc)

        for jc, fn in pending_cast:
            fn(jc)
        pending_cast.clear()
        r.consume("sil0", "DVE", v_scan_last)
        r.consume("sil1", "DVE", v_scan_last)
        TAP("hP", hP_t[:, :])
        TAP("slots", U2[:, :])
        TAP("hbA", hb[:, :])
        TAP("xbA", xb)
        TAP("Acol", A_t[:, :])

        if upto == "A1":
            return _finish(r, sems, tap_specs, nc, block), tap_specs

        r.wait("gpsimd", "PL", r.cnt["PL"])
        v_cci = r.dma("gpsimd", d_ccin, hP_t[:, :], "CCS")
        r.waitv("gpsimd", v_cci)
        r.op("gpsimd", "collective_compute", "AllGather", ALU.bypass,
             replica_groups=[list(range(NCORE))], ins=[d_ccin], outs=[d_ccout], inc="CC")
        r.wait("gpsimd", "CC", 1)
        v_carr = r.dma("gpsimd", carr_t[:, :].rearrange("p (k c) -> p k c", k=NCORE),
                       d_ccout.rearrange("(k p) c -> p k c", p=128), "CCS")

        r.waitv("vector", v_carr)
        r.waitv("vector", v_smalls)

        def dve_step(attr, *a, **k):
            v = r.op("vector", attr, *a, inc="DVE", **k)
            r.wait("vector", "DVE", v)
            return v

        dve_step("memset", cvec_t[:, :], 0.0)
        for k in range(NCORE - 1):
            hsl = carr_t[:, k * 32:k * 32 + 16]
            psl = carr_t[:, k * 32 + 16:k * 32 + 32]
            dve_step("tensor_mul", ct1_t[:, :], psl, cvec_t[:, :])
            dve_step("tensor_add", ct1_t[:, :], ct1_t[:, :], hsl)
            dve_step("tensor_sub", ct1_t[:, :], ct1_t[:, :], cvec_t[:, :])
            dve_step("tensor_single_scalar", ct1_t[:, :], ct1_t[:, :], sel_t[:, k:k + 1], ALU.mult)
            dve_step("tensor_add", cvec_t[:, :], cvec_t[:, :], ct1_t[:, :])
        v_cvec = r.cnt["DVE"]
        TAP("cvec", cvec_t[:, :])
        TAP("carr", carr_t[:, :])

        for m0 in range(2):
            r.acquire("gpsimd", f"wdtb{m0 % 2}")
            wload[("wc", m0)] = r.dma("gpsimd", wdtb_t[:, (m0 % 2) * 2048:(m0 % 2 + 1) * 2048],
                                      d_wc[m0], f"WDT{m0 % 2}")

        corrb = U2[:, 8 * T:(8 + 1) * T].bitcast(BF16)
        r.wait("scalar", "DVE", v_cvec)
        r.wait("scalar", "PL", r.cnt["PL"])
        r.wait("scalar", "ACT", r.cnt["ACT"])
        r.wait("vector", "DVE", v_cvec)
        r.wait("vector", "ACT", r.cnt["ACT"])
        r.wait("gpsimd", "DVE", v_cvec)
        r.wait("gpsimd", "ACT", r.cnt["ACT"])
        v_fix_c = {}
        v_corr_prev = {}
        for j in range(DC):
            pq_bf = hb[:, (DC + j) * T:(DC + j + 1) * T]
            cslot = corrb[:, (j % 2) * T:(j % 2 + 1) * T]
            if j - 2 in v_fix_c:
                r.waitv("scalar", v_fix_c[j - 2])
            v_corr = r.op("scalar", "activation", cslot, pq_bf, AF.Copy,
                          scale=cvec_t[:, j:j + 1], inc="ACT")
            eng, sem = ("vector", "DVE") if j % 2 == 0 else ("gpsimd", "PL")
            r.wait(eng, "ACT", v_corr)
            v = r.op(eng, "tensor_add", hb[:, j * T:(j + 1) * T],
                     hb[:, j * T:(j + 1) * T], cslot, inc=sem)
            v_fix_c[j] = (sem, v)
        v_fix = v_fix_c[DC - 2]
        v_fix_pl = v_fix_c[DC - 1]
        TAP("hb", hb[:, 0:DC * T])

        if upto == "fix":
            return _finish(r, sems, tap_specs, nc, block), tap_specs

        v_r1 = {}
        last_sq_pe1 = {0: 0, 1: 0}
        silb = sil_t.bitcast(BF16)[:, 0:T]
        for m in range(DC):
            p = m % 2
            wc_tile = lambda c, p=p: wdtb_t[:, p * 2048 + c * 128: p * 2048 + (c + 1) * 128]
            r.acquire("sync", f"xs{p}")
            v_xs = r.dma("sync", xs_t[:, p * T:(p + 1) * T], d_xfm[m * 128:(m + 1) * 128, :], f"XS{p}")
            r.waitv("tensor", wload[("wc", m)])
            v_mix = {}
            for h in range(HFN):
                bk = p * 2 + h
                r.acquire("tensor", f"pb{bk}")
                for c in range(DC):
                    r.waitv("tensor", v_fix_c[c])
                    r.op("tensor", "matmul", pb[bk][:, :], wc_tile(c),
                         hb[:, c * T + h * HF: c * T + (h + 1) * HF],
                         start=(c == 0), stop=(c == DC - 1), inc="PE" if c == DC - 1 else None)
                v_mix[h] = r.cnt["PE"]
            r.consume(f"wdtb{p}", "PE", v_mix[1])
            if m + 2 < DC:
                r.acquire("gpsimd", f"wdtb{p}")
                wload[("wc", m + 2)] = r.dma("gpsimd", wdtb_t[:, p * 2048:(p + 1) * 2048], d_wc[m + 2], f"WDT{p}")
            v = None
            for h in range(HFN):
                bk = p * 2 + h
                r.wait("vector", "PE", v_mix[h])
                r.waitv("vector", v_xs)
                r.wait("vector", "DVE", v_ds1)
                v = r.op("vector", "scalar_tensor_tensor",
                         U2[:, m * T + h * HF: m * T + (h + 1) * HF],
                         xs_t[:, p * T + h * HF: p * T + (h + 1) * HF],
                         ds1_t[:, m:m + 1], pb[bk][:, :], ALU.mult, ALU.add, inc="DVE")
                r.consume(f"pb{bk}", "DVE", v)
            r.consume(f"xs{p}", "DVE", v)
            v_r1[m] = v
            for h in range(HFN):
                r.acquire("scalar", f"sil{h}")
                r.wait("scalar", "DVE", v_r1[m])
                if last_sq_pe1[h]:
                    r.wait("scalar", "PE", last_sq_pe1[h])
                v_sq = r.op("scalar", "activation", silb[:, h * HF:(h + 1) * HF],
                            U2[:, m * T + h * HF: m * T + (h + 1) * HF], AF.Square, inc="ACT")
                r.wait("tensor", "ACT", v_sq)
                r.waitv("tensor", v_seb)
                if m == 0:
                    r.acquire("tensor", f"pb{4 + h}")
                    r.acquire("tensor", f"pb{6 + h}")
                r.op("tensor", "matmul", pb[4 + h][0:1, :], onesc_t[:, :],
                     U2[:, m * T + h * HF: m * T + (h + 1) * HF],
                     start=(m == 0), stop=(m == DC - 1))
                last_sq_pe1[h] = r.op("tensor", "matmul", pb[6 + h][0:1, :], onescb_t[:, :],
                                      silb[:, h * HF:(h + 1) * HF],
                                      start=(m == 0), stop=(m == DC - 1), inc="PE")
        for h in range(HFN):
            r.consume(f"sil{h}", "PE", last_sq_pe1[h])
        TAP("r1", U2[:, :])

        if upto == "A2":
            return _finish(r, sems, tap_specs, nc, block), tap_specs

        def emit_ln_stats_chunk(src_fn, v_src, m, last_sq_pe):
            for h in range(HFN):
                r.acquire("scalar", f"sil{h}")
                r.wait("scalar", "DVE", v_src)
                if last_sq_pe[h]:
                    r.wait("scalar", "PE", last_sq_pe[h])
                v_sq = r.op("scalar", "activation", silb[:, h * HF:(h + 1) * HF],
                            src_fn(m, h), AF.Square, inc="ACT")
                r.wait("tensor", "ACT", v_sq)
                if m == 0:
                    r.acquire("tensor", f"pb{4 + h}")
                    r.acquire("tensor", f"pb{6 + h}")
                r.op("tensor", "matmul", pb[4 + h][0:1, :], onesc_t[:, :], src_fn(m, h),
                     start=(m == 0), stop=(m == DC - 1))
                last_sq_pe[h] = r.op("tensor", "matmul", pb[6 + h][0:1, :], onescb_t[:, :],
                                     silb[:, h * HF:(h + 1) * HF],
                                     start=(m == 0), stop=(m == DC - 1), inc="PE")

        def emit_ln(src_fn, v_src_fn):
            last_sq_pe = {0: 0, 1: 0}
            for m in range(DC):
                emit_ln_stats_chunk(src_fn, v_src_fn(m), m, last_sq_pe)
            v_stats = r.cnt["PE"]
            for h in range(HFN):
                r.consume(f"sil{h}", "PE", last_sq_pe[h])
            return emit_ln_rows(v_stats)
        def emit_ln_rows(v_stats=None):
            if v_stats is None:
                v_stats = r.cnt["PE"]
            mrow = rows8_t[0:1, 0:1024]
            rrow = rows8_t[0:1, 1024:2048]
            r.acquire("vector", "rows8_0")
            r.acquire("vector", "rows8_1")
            r.wait("vector", "PE", v_stats)

            def row_step(attr, *a, **k):
                v = r.op("vector", attr, *a, inc="DVE", **k)
                r.wait("vector", "DVE", v)
                return v

            for h in range(HFN):
                hs = slice(h * HF, (h + 1) * HF)
                row_step("tensor_scalar_mul", mrow[:, hs], pb[4 + h][0:1, :], 1.0 / 2048.0)
            row_step("tensor_scalar_mul", rrow[:, 0:HF], pb[6][0:1, :], 1.0 / 2048.0)
            row_step("tensor_scalar_mul", rrow[:, HF:2 * HF], pb[7][0:1, :], 1.0 / 2048.0)
            row_step("tensor_mul", srow_t[0:1, 0:1024], mrow[:, :], mrow[:, :])
            v_var = row_step("tensor_sub", rrow[:, :], rrow[:, :], srow_t[0:1, 0:1024])
            for i in range(4, 8):
                r.consume(f"pb{i}", "DVE", v_var)
            r.wait("scalar", "DVE", v_var)
            v_ln = r.op("scalar", "activation", srow_t[0:1, 0:1024], rrow[:, :], AF.Ln,
                        bias=cst_t[0:1, 0:1], inc="ACT")
            r.wait("scalar", "ACT", v_ln)
            v_rstd = r.op("scalar", "activation", rrow[:, :], srow_t[0:1, 0:1024], AF.Exp,
                          scale=cst_t[0:1, 1:2], inc="ACT")
            r.wait("tensor", "DVE", v_var)
            r.wait("tensor", "ACT", v_rstd)
            v_bc = None
            for h in range(HFN):
                hs = slice(h * HF, (h + 1) * HF)
                r.acquire("tensor", f"pb{4 + h}")
                r.acquire("tensor", f"pb{6 + h}")
                r.op("tensor", "matmul", pb[4 + h][:, :], ones1_t[:, :], mrow[:, hs],
                     start=True, stop=True)
                v_bc = r.op("tensor", "matmul", pb[6 + h][:, :], ones1_t[:, :], rrow[:, hs],
                            start=True, stop=True, inc="PE")
            r.consume("rows8_0", "PE", v_bc)
            r.consume("rows8_1", "PE", v_bc)
            return v_bc

        v_bc1 = emit_ln_rows()
        v_spill = {}
        v_cast_m = {}
        v_norm1 = None
        for m in range(DC):
            v = None
            for h in range(HFN):
                sl = U2[:, m * T + h * HF: m * T + (h + 1) * HF]
                r.wait("vector", "PE", v_bc1)
                r.acquire("vector", f"sil{h}")
                tb = sil_t[:, h * HF:(h + 1) * HF]
                r.op("vector", "tensor_sub", tb, sl, pb[4 + h][:, :])
                v = r.op("vector", "tensor_mul", sl, tb, pb[6 + h][:, :], inc="DVE")
                r.consume(f"sil{h}", "DVE", v)
            v_norm1 = v
            ceng = ("scalar", "gpsimd", "vector")[m % 3]
            r.wait(ceng, "DVE", v)
            if ceng == "scalar":
                v_cast_m[m] = ("ACT", r.op("scalar", "activation", xb_t[:, m * T:(m + 1) * T],
                                           U2[:, m * T:(m + 1) * T], AF.Copy, inc="ACT"))
            elif ceng == "gpsimd":
                v_cast_m[m] = ("PL", r.op("gpsimd", "tensor_copy", xb_t[:, m * T:(m + 1) * T],
                                          U2[:, m * T:(m + 1) * T], inc="PL"))
            else:
                v_cast_m[m] = ("DVE", r.op("vector", "tensor_copy", xb_t[:, m * T:(m + 1) * T],
                                           U2[:, m * T:(m + 1) * T], inc="DVE"))
            r.wait("sync", "DVE", v)
            v_spill[m] = r.dma("sync", d_r1d[:, m * T:(m + 1) * T], U2[:, m * T:(m + 1) * T], "SP")
        for i in range(4, 8):
            r.consume(f"pb{i}", "DVE", v_norm1)
        v_x1_pl = r.cnt["PL"]
        v_spill_all = v_spill[DC - 1]
        TAP("x1", U2[:, :])

        if upto == "ln1":
            return _finish(r, sems, tap_specs, nc, block), tap_specs

        r.waitv("tensor", v_seb)
        for h in range(HFN):
            r.acquire("tensor", f"pb{h}")
            for c in range(DC):
                r.waitv("tensor", v_cast_m[c])
                r.op("tensor", "matmul", pb[h][0:16, :], rwb_t[:, c * 16:(c + 1) * 16],
                     xb[:, c * T + h * HF: c * T + (h + 1) * HF],
                     start=(c == 0), stop=(c == DC - 1), inc="PE" if c == DC - 1 else None)
        v_logit = r.cnt["PE"]
        r.acquire("scalar", "xs0")
        r.acquire("scalar", "xs1")
        r.wait("scalar", "PE", v_logit)
        r.op("scalar", "copy", lg_s[:, 0:HF], pb[0][0:16, :])
        v_lg = r.op("scalar", "copy", lg_s[:, HF:2 * HF], pb[1][0:16, :], inc="ACT")
        r.consume("pb0", "ACT", v_lg)
        r.consume("pb1", "ACT", v_lg)
        r.wait("tensor", "ACT", v_lg)
        r.acquire("tensor", "pb2")
        for b in range(8):
            r.op("tensor", "transpose", pb[2][:, b * 16:(b + 1) * 16],
                 lg_s[:, b * 128:(b + 1) * 128], idn_t[0:16, 0:16],
                 inc="PE" if b == 7 else None)
        v_tp = r.cnt["PE"]
        r.wait("scalar", "PE", v_tp)
        v_probs = r.op("scalar", "copy", probs_t[:, :], pb[2][:, 0:128], inc="ACT")
        r.consume("pb2", "ACT", v_probs)
        r.wait("scalar", "ACT", v_probs)
        v_e = r.op("scalar", "activation", pr2_t[:, :], probs_t[:, :], AF.Exp, inc="ACT")
        r.wait("vector", "ACT", v_e)
        for b in range(8):
            v_s = r.op("vector", "reduce_sum", ssum8_t[:, b:b + 1], pr2_t[:, b * 16:(b + 1) * 16],
                       AX.X, inc="DVE")
        r.wait("vector", "DVE", v_s)
        v_rc = r.op("vector", "reciprocal", rcp8_t[:, :], ssum8_t[:, :], inc="DVE")
        r.wait("vector", "DVE", v_rc)
        for b in range(8):
            v_wt = r.op("vector", "tensor_single_scalar", wtm_t[:, b * 16:(b + 1) * 16],
                        pr2_t[:, b * 16:(b + 1) * 16], rcp8_t[:, b:b + 1], ALU.mult, inc="DVE")
        r.wait("vector", "DVE", v_wt)
        for b in range(8):
            v_m8 = r.op("vector", "max", out=m8_8_t[:, b * 8:(b + 1) * 8],
                        in_=wtm_t[:, b * 16:(b + 1) * 16], inc="DVE")
        r.wait("vector", "DVE", v_m8)
        for b in range(8):
            v_mk = r.op("vector", "tensor_single_scalar", msk_t[:, b * 16:(b + 1) * 16],
                        wtm_t[:, b * 16:(b + 1) * 16], m8_8_t[:, b * 8 + 3:b * 8 + 4],
                        ALU.is_ge, inc="DVE")
        r.wait("vector", "DVE", v_mk)
        for b in range(8):
            v_w = r.op("vector", "tensor_mul", pr2_t[:, b * 16:(b + 1) * 16],
                       msk_t[:, b * 16:(b + 1) * 16], wtm_t[:, b * 16:(b + 1) * 16], inc="DVE")
        r.wait("vector", "DVE", v_w)
        TAP("wdense_tm", pr2_t[:, :])
        r.wait("tensor", "DVE", v_w)
        r.acquire("tensor", "pb3")
        r.acquire("tensor", "pb4")
        for b in range(8):
            bk = 3 if b < 4 else 4
            r.op("tensor", "transpose", pb[bk][0:16, (b % 4) * 128:(b % 4 + 1) * 128],
                 pr2_t[:, b * 16:(b + 1) * 16], idn_t[:, :], inc="PE" if b == 7 else None)
        v_wt = r.cnt["PE"]
        r.wait("scalar", "PE", v_wt)
        r.op("scalar", "copy", wfm[:, 0:HF], pb[3][0:16, 0:512])
        v_wfm = r.op("scalar", "copy", wfm[:, HF:2 * HF], pb[4][0:16, 0:512], inc="ACT")
        r.consume("pb3", "ACT", v_wfm)
        r.consume("pb4", "ACT", v_wfm)

        r.acquire("tensor", "pb5")
        r.acquire("tensor", "pb2")
        for h in range(HFN):
            bank = pb[5] if h == 0 else pb[2]
            for c in range(DC):
                r.op("tensor", "matmul", bank[0:1, :], seb_t[:, c:c + 1],
                     xb[:, c * T + h * HF: c * T + (h + 1) * HF],
                     start=(c == 0), stop=(c == DC - 1), inc="PE" if c == DC - 1 else None)
        v_se = r.cnt["PE"]
        r.wait("scalar", "PE", v_se)
        r.op("scalar", "activation", srow_t[0:1, 0:HF], pb[5][0:1, :], AF.Exp, scale=-1.0)
        v_sex = r.op("scalar", "activation", srow_t[0:1, HF:2 * HF], pb[2][0:1, :], AF.Exp,
                     scale=-1.0, inc="ACT")
        r.consume("pb5", "ACT", v_sex)
        r.consume("pb2", "ACT", v_sex)
        r.wait("vector", "ACT", v_sex)
        v_sa = r.op("vector", "tensor_scalar_add", srow_t[0:1, :], srow_t[0:1, :], 1.0, inc="DVE")
        r.wait("vector", "DVE", v_sa)
        v_sig = r.op("vector", "reciprocal", srow_t[0:1, :], srow_t[0:1, :], inc="DVE")
        r.wait("vector", "DVE", v_sig)
        r.wait("tensor", "DVE", v_sig)
        r.acquire("tensor", "pb6")
        r.acquire("tensor", "pb7")
        r.op("tensor", "matmul", pb[6][:, :], ones1_t[:, :], srow_t[0:1, 0:HF], start=True, stop=True)
        v_sbc_pe = r.op("tensor", "matmul", pb[7][:, :], ones1_t[:, :], srow_t[0:1, HF:2 * HF],
                        start=True, stop=True, inc="PE")
        r.wait("scalar", "PE", v_sbc_pe)
        r.waitv("scalar", v_spill_all)
        r.op("scalar", "copy", s_bc[:, 0:HF], pb[6][:, :])
        v_sbc = r.op("scalar", "copy", s_bc[:, HF:2 * HF], pb[7][:, :], inc="ACT")
        r.consume("pb6", "ACT", v_sbc)
        r.consume("pb7", "ACT", v_sbc)
        TAP("s_bc", s_bc)

        if upto == "router":
            return _finish(r, sems, tap_specs, nc, block), tap_specs

        he = lambda c, h: heb[:, c * T + h * HF: c * T + (h + 1) * HF]
        dwb = lambda q: heb[:, 11264 + q * 1408: 11264 + (q + 1) * 1408]
        gwb = lambda q: wdtb_t[:, q * 2048:(q + 1) * 2048]
        uwb = lambda q: wbb_t[:, q * 2048:(q + 1) * 2048]

        jobs = [("sg", g) for g in range(SGN)] + [("ex", e) for e in range(NE)]
        gu_load, dw_load = {}, {}

        r.waitv("gpsimd", v_spill_all)
        r.waitv("vector", v_spill_all)

        def gu_dma(jkind, jid, m):
            q = m % 2
            src_g = d_sgw[jid, m] if jkind == "sg" else d_gw[jid, m]
            src_u = d_suw[jid, m] if jkind == "sg" else d_uw[jid, m]
            r.acquire("gpsimd", f"wdtb{q}")
            gu_load[("g", jkind, jid, m)] = r.dma("gpsimd", gwb(q), src_g, f"GW{q}")
            r.acquire("gpsimd", f"wbb{q}")
            gu_load[("u", jkind, jid, m)] = r.dma("gpsimd", uwb(q), src_u, f"UW{q}")

        def dw_dma(jkind, jid, m2):
            q = m2 % 2
            src = d_sdw[m2][:, jid * 1408:(jid + 1) * 1408] if jkind == "sg" else d_dw[jid, m2]
            r.acquire("gpsimd", f"dwb{q}")
            dw_load[(jkind, jid, m2)] = r.dma("gpsimd", dwb(q), src, f"DW{q}")

        gu_dma(*jobs[0], 0)
        gu_dma(*jobs[0], 1)

        v_acc = {}
        for ji, (jkind, jid) in enumerate(jobs):
            if jkind == "ex":
                e = jid
                sslot = e % 2
                r.acquire("sync", f"rows8_{sslot}")
                r.wait("sync", "ACT", v_wfm)
                v_wrow = r.dma("sync", rows8_t[0:1, sslot * 1024:sslot * 1024 + T],
                               wfm[e:e + 1, 0:T], f"WR{sslot}")
                r.waitv("tensor", v_wrow)
                r.acquire("tensor", "pb6")
                r.acquire("tensor", "pb7")
                r.op("tensor", "matmul", pb[6][:, :], ones1_t[:, :],
                     rows8_t[0:1, sslot * 1024:sslot * 1024 + HF], start=True, stop=True)
                v_pe_bc = r.op("tensor", "matmul", pb[7][:, :], ones1_t[:, :],
                               rows8_t[0:1, sslot * 1024 + HF:sslot * 1024 + T],
                               start=True, stop=True, inc="PE")
                r.consume(f"rows8_{sslot}", "PE", v_pe_bc)
                r.wait("scalar", "PE", v_pe_bc)
                r.acquire("scalar", "w_bc")
                r.op("scalar", "copy", w_bc[:, 0:HF], pb[6][:, :])
                v_wbc = r.op("scalar", "copy", w_bc[:, HF:2 * HF], pb[7][:, :], inc="ACT")
                r.consume("pb6", "ACT", v_wbc)
                r.consume("pb7", "ACT", v_wbc)
                scale_tile, v_scale = w_bc, ("ACT", v_wbc)
            else:
                scale_tile, v_scale = s_bc, ("ACT", v_sbc)

            dw_dma(jkind, jid, 0)
            dw_dma(jkind, jid, 1)
            v_gu_last = None
            v_gu = {}
            for m in range(IC):
                q = m % 2
                r.waitv("tensor", gu_load[("g", jkind, jid, m)])
                r.waitv("tensor", gu_load[("u", jkind, jid, m)])
                for h in range(HFN):
                    r.acquire("tensor", f"pb{h}")
                    for c in range(DC):
                        r.op("tensor", "matmul", pb[h][:, :], gwb(q)[:, c * 128:(c + 1) * 128],
                             xb[:, c * T + h * HF: c * T + (h + 1) * HF],
                             start=(c == 0), stop=(c == DC - 1), inc="PE" if c == DC - 1 else None)
                    v_g = r.cnt["PE"]
                    r.acquire("tensor", f"pb{2 + h}")
                    for c in range(DC):
                        r.op("tensor", "matmul", pb[2 + h][:, :], uwb(q)[:, c * 128:(c + 1) * 128],
                             xb[:, c * T + h * HF: c * T + (h + 1) * HF],
                             start=(c == 0), stop=(c == DC - 1), inc="PE" if c == DC - 1 else None)
                    v_upe = r.cnt["PE"]
                    if h == 1:
                        r.consume(f"wdtb{q}", "PE", v_upe)
                        r.consume(f"wbb{q}", "PE", v_upe)
                    r.acquire("scalar", f"sil{h}")
                    r.wait("scalar", "PE", v_g)
                    v_sil = r.op("scalar", "activation", sil_t[:, h * HF:(h + 1) * HF],
                                 pb[h][:, :], AF.Silu, inc="ACT")
                    r.consume(f"pb{h}", "ACT", v_sil)
                    r.wait("vector", "ACT", v_sil)
                    r.wait("vector", "PE", v_upe)
                    v_h = r.op("vector", "tensor_mul", he(m, h), sil_t[:, h * HF:(h + 1) * HF],
                               pb[2 + h][:, :], inc="DVE")
                    r.consume(f"pb{2 + h}", "DVE", v_h)
                    r.consume(f"sil{h}", "DVE", v_h)
                    v_gu[(m, h)] = v_h
                    v_gu_last = v_h
                if m + 2 < IC:
                    gu_dma(jkind, jid, m + 2)
            if ji + 1 < len(jobs):
                njk, njid = jobs[ji + 1]
                gu_dma(njk, njid, 0)
                gu_dma(njk, njid, 1)
            for m2 in range(DC):
                q = m2 % 2
                r.waitv("tensor", dw_load[(jkind, jid, m2)])
                for h in range(HFN):
                    bk = 4 + h
                    r.acquire("tensor", f"pb{bk}")
                    for c in range(IC):
                        r.wait("tensor", "DVE", v_gu[(c, h)])
                        r.op("tensor", "matmul", pb[bk][:, :], dwb(q)[:, c * 128:(c + 1) * 128],
                             he(c, h), start=(c == 0), stop=(c == IC - 1),
                             inc="PE" if c == IC - 1 else None)
                    v_d = r.cnt["PE"]
                    if h == 1:
                        r.consume(f"dwb{q}", "PE", v_d)
                    r.wait("vector", "PE", v_d)
                    r.wait("vector", v_scale[0], v_scale[1])
                    acc_sl = accv[:, m2 * T + h * HF: m2 * T + (h + 1) * HF]
                    ssl = scale_tile[:, h * HF:(h + 1) * HF]
                    if ji == 0:
                        v_a = r.op("vector", "tensor_mul", acc_sl, ssl, pb[bk][:, :], inc="DVE")
                    else:
                        r.acquire("vector", f"tscr{h}")
                        tb = tscr[:, h * HF:(h + 1) * HF]
                        r.op("vector", "tensor_mul", tb, ssl, pb[bk][:, :])
                        v_a = r.op("vector", "tensor_add", acc_sl, acc_sl, tb, inc="DVE")
                        r.consume(f"tscr{h}", "DVE", v_a)
                    r.consume(f"pb{bk}", "DVE", v_a)
                    v_acc[(m2, h)] = v_a
                if m2 + 2 < DC:
                    dw_dma(jkind, jid, m2 + 2)
            if jkind == "ex":
                r.consume("w_bc", "DVE", v_acc[(DC - 1, 1)])

        TAP("acc", accv[:, :])

        if upto == "moe":
            return _finish(r, sems, tap_specs, nc, block), tap_specs

        v_r2 = {}
        for m in range(DC):
            q = m % 2
            rbs = xs_t[:, q * T:(q + 1) * T]
            r.acquire("sync", f"rb{q}")
            v_back = r.dma("sync", rbs, d_r1d[:, m * T:(m + 1) * T], f"XS{q}")
            r.waitv("vector", v_back)
            r.wait("vector", "DVE", v_acc[(m, 1)])
            v = r.op("vector", "tensor_add", accv[:, m * T:(m + 1) * T],
                     accv[:, m * T:(m + 1) * T], rbs, inc="DVE")
            r.consume(f"rb{q}", "DVE", v)
            v_r2[m] = v

        v_bc2 = emit_ln(lambda m, h: accv[:, m * T + h * HF: m * T + (h + 1) * HF],
                        lambda m: v_r2[m])
        TAP("r2", accv[:, :])
        TAP("rows2", rows8_t[:, :])
        if upto == "ln2a":
            return _finish(r, sems, tap_specs, nc, block), tap_specs
        for m in range(DC):
            v = None
            for h in range(HFN):
                sl = accv[:, m * T + h * HF: m * T + (h + 1) * HF]
                r.wait("vector", "PE", v_bc2)
                r.acquire("vector", f"tscr{h}")
                tb = tscr[:, h * HF:(h + 1) * HF]
                r.op("vector", "tensor_sub", tb, sl, pb[4 + h][:, :])
                v = r.op("vector", "tensor_mul", sl, tb, pb[6 + h][:, :], inc="DVE")
                r.consume(f"tscr{h}", "DVE", v)
            r.wait("sync", "DVE", v)
            r.dma("sync", d_y[m * 128:(m + 1) * 128, :], accv[:, m * T:(m + 1) * T], "Y")

        return _finish(r, sems, tap_specs, nc, block), tap_specs


def _finish(r, sems, tap_specs, nc, block):
    if tap_specs:
        for name in r.cnt:
            if name != "Y":
                r.wait("sync", name, r.cnt[name])
        for name in list(tap_specs):
            ap = tap_specs[name]
            d = nc.dram_tensor(f"tap_{name}", list(ap.shape), ap.dtype, kind="ExternalOutput").ap()
            r.dma("sync", d, ap, "Y")
    r.wait("sync", "Y", r.cnt["Y"])
    r.wait("gpsimd", "Y", r.cnt["Y"])

    @block.sync
    def _(eng):
        r.replay(eng, "sync", sems)

    @block.tensor
    def _(eng):
        r.replay(eng, "tensor", sems)

    @block.scalar
    def _(eng):
        r.replay(eng, "scalar", sems)

    @block.vector
    def _(eng):
        r.replay(eng, "vector", sems)

    @block.gpsimd
    def _(eng):
        r.replay(eng, "gpsimd", sems)

    return nc


def _pack_weights(i):
    f32 = lambda a: np.ascontiguousarray(np.asarray(a, dtype=np.float32))
    C = np.ascontiguousarray
    W_dt = f32(i["L0_attn_W_dt"]); W_B = f32(i["L0_attn_W_B"]); W_C = f32(i["L0_attn_W_C"])
    gw = f32(i["L0_ffn_gate_w"]); uw = f32(i["L0_ffn_up_w"]); dwt = f32(i["L0_ffn_down_w"])
    sgw = f32(i["L0_ffn_shared_gate_w"]); suw = f32(i["L0_ffn_shared_up_w"])
    sdw = f32(i["L0_ffn_shared_down_w"]); rw = f32(i["L0_ffn_router_w"]); se = f32(i["L0_ffn_se_gate_w"])

    d = {}
    d["wdt"] = C(W_dt.reshape(16, 128, 16, 128).transpose(0, 3, 2, 1).reshape(16, 128, 2048))
    d["wb"] = C(W_B.reshape(16, 128, 16, 128).transpose(0, 3, 2, 1).reshape(16, 128, 2048))
    d["wc"] = C(W_C.reshape(16, 128, 16, 128).transpose(0, 3, 2, 1).reshape(16, 128, 2048))
    d["gw"] = C(gw.reshape(16, 11, 128, 16, 128).transpose(0, 1, 4, 3, 2).reshape(16, 11, 128, 2048))
    d["uw"] = C(uw.reshape(16, 11, 128, 16, 128).transpose(0, 1, 4, 3, 2).reshape(16, 11, 128, 2048))
    d["dw"] = C(dwt.reshape(16, 16, 128, 11, 128).transpose(0, 1, 4, 3, 2).reshape(16, 16, 128, 1408))
    d["sgw"] = C(sgw.reshape(44, 128, 16, 128).transpose(0, 3, 2, 1).reshape(4, 11, 128, 2048))
    d["suw"] = C(suw.reshape(44, 128, 16, 128).transpose(0, 3, 2, 1).reshape(4, 11, 128, 2048))
    d["sdw"] = C(sdw.reshape(16, 128, 44, 128).transpose(0, 3, 2, 1).reshape(16, 128, 5632))
    d["rw"] = C(rw.reshape(16, 16, 128).transpose(2, 1, 0).reshape(128, 256))
    d["sew"] = C(se.reshape(16, 128).T)
    v16 = lambda v: C(np.asarray(v, np.float32).reshape(16, 128).T)
    d["bdt"] = v16(i["L0_attn_b_dt"])
    d["alog"] = v16(i["L0_attn_A_log"])
    d["dskip"] = v16(i["L0_attn_D_skip"])
    d["idn"] = np.eye(128, dtype=np.float32)
    d["ones1"] = np.ones((1, 128), np.float32)
    d["onesc"] = np.ones((128, 1), np.float32)
    cst = np.zeros((128, 4), np.float32); cst[:, 0] = EPS; cst[:, 1] = -0.5
    d["cst"] = cst
    return d


def make_in_maps(inputs, shared=None):
    x = np.asarray(inputs["L0_attn_x"], dtype=np.float32)
    if shared is None:
        shared = _pack_weights(inputs)
    in_maps = []
    for k in range(NCORE):
        m = dict(shared)
        m["xfm"] = np.ascontiguousarray(x[0, k * T:(k + 1) * T, :].T)
        m["sel"] = np.ascontiguousarray(
            np.tile((np.arange(8) < k).astype(np.float32), (128, 1)))
        in_maps.append(m)
    return in_maps


def kernel(**inputs):
    nc, _ = build_program(upto="full", taps=())
    in_maps = make_in_maps(inputs)
    res = run_bass_kernel_spmd(nc, in_maps, core_ids=list(range(NCORE)))
    out = np.empty((1, 8192, 2048), dtype=np.float32)
    for k in range(NCORE):
        out[0, k * T:(k + 1) * T, :] = res.results[k]["y"].T
    return out


# version 10
# speedup vs baseline: 1.0051x; 1.0001x over previous
import numpy as np
from collections import defaultdict
from contextlib import ExitStack

import concourse.bass as bass
import concourse.mybir as mybir
from concourse.bass_utils import run_bass_kernel_spmd

F32 = mybir.dt.float32
BF16 = mybir.dt.bfloat16
AF = mybir.ActivationFunctionType
ALU = mybir.AluOpType
AX = mybir.AxisListType

NCORE = 8
T = 1024
HFN = 2
HF = 512
DC = 16
IC = 11
NE = 16
SGN = 4
EPS = 1e-5
ENGINES = ("sync", "tensor", "scalar", "vector", "gpsimd")


class Rec:

    def __init__(self):
        self.q = {e: [] for e in ENGINES}
        self.cnt = defaultdict(int)
        self.waited = defaultdict(int)
        self.released = {}
        self.n = defaultdict(int)

    def op(self, eng, attr, *a, inc=None, incby=1, **k):
        rec = [attr, a, k, None]
        self.q[eng].append(rec)
        self.n[eng] += 1
        if inc is not None:
            self.cnt[inc] += incby
            rec[3] = (inc, incby)
            return self.cnt[inc]
        return None

    def wait(self, eng, sem, val):
        if val is None or val <= 0:
            return
        if self.waited[(eng, sem)] >= val:
            return
        self.waited[(eng, sem)] = val
        self.q[eng].append(["wait_ge", (("SEM", sem), val), {}, None])
        self.n[eng] += 1

    def dma(self, eng, out, in_, sem):
        v = self.op(eng, "dma_start", inc=sem, incby=16, out=out, in_=in_)
        return (sem, v)

    def waitv(self, eng, sv):
        self.wait(eng, sv[0], sv[1])

    def consume(self, key, sem, val):
        self.released.setdefault(key, []).append((sem, val))

    def acquire(self, eng, key):
        for sem, val in self.released.pop(key, []):
            self.wait(eng, sem, val)

    def replay(self, eng_obj, eng_name, sems):
        for attr, a, k, then in self.q[eng_name]:
            a = [sems[x[1]] if isinstance(x, tuple) and len(x) == 2 and x[0] == "SEM" else x
                 for x in a]
            r = getattr(eng_obj, attr)(*a, **k)
            if then is not None:
                r.then_inc(sems[then[0]], then[1])


def build_program(upto="full", taps=()):
    nc = bass.Bass("TRN2", target_bir_lowering=False, debug=False)

    d_xfm = nc.dram_tensor("xfm", [2048, T], F32, kind="ExternalInput").ap()
    d_wdt = nc.dram_tensor("wdt", [DC, 128, 2048], F32, kind="ExternalInput").ap()
    d_wb = nc.dram_tensor("wb", [DC, 128, 2048], F32, kind="ExternalInput").ap()
    d_wc = nc.dram_tensor("wc", [DC, 128, 2048], F32, kind="ExternalInput").ap()
    d_gw = nc.dram_tensor("gw", [NE, IC, 128, 2048], F32, kind="ExternalInput").ap()
    d_uw = nc.dram_tensor("uw", [NE, IC, 128, 2048], F32, kind="ExternalInput").ap()
    d_dw = nc.dram_tensor("dw", [NE, DC, 128, 1408], F32, kind="ExternalInput").ap()
    d_sgw = nc.dram_tensor("sgw", [SGN, IC, 128, 2048], F32, kind="ExternalInput").ap()
    d_suw = nc.dram_tensor("suw", [SGN, IC, 128, 2048], F32, kind="ExternalInput").ap()
    d_sdw = nc.dram_tensor("sdw", [DC, 128, 5632], F32, kind="ExternalInput").ap()
    d_rw = nc.dram_tensor("rw", [128, 256], F32, kind="ExternalInput").ap()
    d_sew = nc.dram_tensor("sew", [128, 16], F32, kind="ExternalInput").ap()
    d_bdt = nc.dram_tensor("bdt", [128, 16], F32, kind="ExternalInput").ap()
    d_alog = nc.dram_tensor("alog", [128, 16], F32, kind="ExternalInput").ap()
    d_dskip = nc.dram_tensor("dskip", [128, 16], F32, kind="ExternalInput").ap()
    d_sel = nc.dram_tensor("sel", [128, 8], F32, kind="ExternalInput").ap()
    d_idn = nc.dram_tensor("idn", [128, 128], F32, kind="ExternalInput").ap()
    d_ones1 = nc.dram_tensor("ones1", [1, 128], F32, kind="ExternalInput").ap()
    d_onesc = nc.dram_tensor("onesc", [128, 1], F32, kind="ExternalInput").ap()
    d_cst = nc.dram_tensor("cst", [128, 4], F32, kind="ExternalInput").ap()

    d_ccin = nc.dram_tensor("cc_in", [128, 32], F32, kind="Internal").ap()
    d_ccout = nc.dram_tensor("cc_out", [128 * NCORE, 32], F32, kind="Internal").ap()
    d_r1d = nc.dram_tensor("r1d", [128, DC * T], F32, kind="Internal").ap()
    d_y = nc.dram_tensor("y", [2048, T], F32, kind="ExternalOutput").ap()

    tap_specs = {}

    with ExitStack() as ctx:
        sb = lambda name, shape, dt: ctx.enter_context(nc.sbuf_tensor(name, shape, dt))
        xb_t = sb("xb", [128, DC * T], BF16)
        U1_t = sb("U1", [128, DC * T], F32)
        U2_t = sb("U2", [128, DC * T], F32)
        wdtb_t = sb("wdtb", [128, 2 * 2048], BF16)
        wbb_t = sb("wbb", [128, 2 * 2048], BF16)
        xs_t = sb("xs", [128, 2 * T], F32)
        sil_t = sb("sil", [128, T], F32)
        rows8_t = sb("rows8", [1, 2048], F32)
        srow_t = sb("srow", [1, T], F32)
        probs_t = sb("probs", [128, 128], F32)
        pr2_t = sb("pr2", [128, 128], F32)
        wtm_t = sb("wtm", [128, 128], F32)
        m8_t = sb("m8", [128, 8], F32)
        msk_t = sb("msk", [128, 128], F32)
        ssum8_t = sb("ssum8", [128, 8], F32)
        rcp8_t = sb("rcp8", [128, 8], F32)
        m8_8_t = sb("m8_8", [128, 64], F32)
        nmx_t = sb("nmx", [128, 1], F32)
        ssum_t = sb("ssum", [128, 1], F32)
        rcp_t = sb("rcp", [128, 1], F32)
        hP_t = sb("hP", [128, 32], F32)
        carr_t = sb("carr", [128, 256], F32)
        cvec_t = sb("cvec", [128, 16], F32)
        ct1_t = sb("ct1", [128, 16], F32)
        bdt_t = sb("bdt_s", [128, 16], F32)
        alog_t = sb("alog_s", [128, 16], F32)
        dskip_t = sb("dskip_s", [128, 16], F32)
        A_t = sb("A_s", [128, 16], F32)
        ds1_t = sb("ds1_s", [128, 16], F32)
        sel_t = sb("sel_s", [128, 8], F32)
        idn_t = sb("idn_s", [128, 128], F32)
        ones1_t = sb("ones1_s", [1, 128], F32)
        onesc_t = sb("onesc_s", [128, 1], F32)
        onescb_t = sb("onescb_s", [128, 1], BF16)
        cst_t = sb("cst_s", [128, 4], F32)
        rwb_t = sb("rwb", [128, 256], BF16)
        seb_t = sb("seb", [128, 16], BF16)

        pb = [ctx.enter_context(nc.psum_tensor(f"pb{i}", [128, 512], F32)) for i in range(8)]

        semnames = ("SM", "XB", "RW", "WDT0", "WDT1", "WBB0", "WBB1", "XS0", "XS1",
                    "GW0", "GW1", "UW0", "UW1", "DW0", "DW1", "SP", "WR0", "WR1",
                    "CCS", "Y", "CC", "PE", "ACT", "DVE", "PL")
        sems = {n: ctx.enter_context(nc.semaphore(n)) for n in semnames}
        block = ctx.enter_context(nc.Block())

        xb = xb_t[:, :]
        hb = U1_t.bitcast(BF16)
        accv = U1_t
        U2 = U2_t
        heb = U2_t.bitcast(BF16)
        slot = lambda s: U2[:, s * T:(s + 1) * T]
        w_bc = U2[:, 7040:8064]
        s_bc = U2[:, 8064:9088]
        tscr = U2[:, 9088:11136]
        zeros = sil_t[:, 0:T]
        lg_s = xs_t[0:16, 0:T]
        wfm = xs_t[0:16, T:2 * T]

        r = Rec()

        def TAP(name, ap):
            if name in taps:
                tap_specs[name] = ap

        r.dma("sync", bdt_t[:, :], d_bdt, "SM")
        r.dma("sync", alog_t[:, :], d_alog, "SM")
        r.dma("sync", dskip_t[:, :], d_dskip, "SM")
        r.dma("sync", sel_t[:, :], d_sel, "SM")
        r.dma("sync", idn_t[:, :], d_idn, "SM")
        r.dma("sync", ones1_t[:, :], d_ones1, "SM")
        r.dma("sync", onesc_t[:, :], d_onesc, "SM")
        v_smalls = r.dma("sync", cst_t[:, :], d_cst, "SM")

        for j in range(DC):
            v_xb = r.dma("gpsimd", xb_t[:, j * T:(j + 1) * T], d_xfm[j * 128:(j + 1) * 128, :], "XB")
        r.dma("gpsimd", onescb_t[:, :], d_onesc, "RW")
        r.dma("gpsimd", rwb_t[:, :], d_rw, "RW")
        v_seb = r.dma("gpsimd", seb_t[:, :], d_sew, "RW")

        r.waitv("scalar", v_smalls)
        v_atmp = r.op("scalar", "activation", A_t[:, :], alog_t[:, :], AF.Exp, inc="ACT")
        r.wait("vector", "ACT", v_atmp)
        v_A = r.op("vector", "tensor_scalar_mul", A_t[:, :], A_t[:, :], -1.0, inc="DVE")
        r.waitv("vector", v_smalls)
        v_ds1 = r.op("vector", "tensor_scalar_add", ds1_t[:, :], dskip_t[:, :], 1.0, inc="DVE")
        v_zero = r.op("vector", "memset", zeros, 0.0, inc="DVE")

        wload = {}
        for j0 in range(2):
            wload[("wdt", j0)] = r.dma("gpsimd", wdtb_t[:, (j0 % 2) * 2048:(j0 % 2 + 1) * 2048], d_wdt[j0], f"WDT{j0 % 2}")
            wload[("wb", j0)] = r.dma("gpsimd", wbb_t[:, (j0 % 2) * 2048:(j0 % 2 + 1) * 2048], d_wb[j0], f"WBB{j0 % 2}")

        v_scan_last = None
        for j in range(DC):
            p = j % 2
            wdt_tile = lambda c, p=p: wdtb_t[:, p * 2048 + c * 128: p * 2048 + (c + 1) * 128]
            wb_tile = lambda c, p=p: wbb_t[:, p * 2048 + c * 128: p * 2048 + (c + 1) * 128]
            s_dt, s_ab, s_l, s_r, s_a, s_e, s_hl, s_pq = (slot(0 + p), slot(2 + p), slot(4 + p),
                                                          slot(6 + p), slot(8 + p), slot(10 + p),
                                                          slot(12 + p), slot(14 + p))
            r.acquire("sync", f"xs{p}")
            v_xs = r.dma("sync", xs_t[:, p * T:(p + 1) * T], d_xfm[j * 128:(j + 1) * 128, :], f"XS{p}")

            r.waitv("tensor", v_xb)
            r.waitv("tensor", wload[("wdt", j)])
            r.waitv("tensor", wload[("wb", j)])
            v_dt_g, v_bp_g = {}, {}
            for h in range(HFN):
                r.acquire("tensor", f"pb{p * 4 + h}")
                for c in range(DC):
                    r.op("tensor", "matmul", pb[p * 4 + h][:, :], wdt_tile(c),
                         xb[:, c * T + h * HF: c * T + (h + 1) * HF],
                         start=(c == 0), stop=(c == DC - 1), inc="PE" if c == DC - 1 else None)
                v_dt_g[h] = r.cnt["PE"]
            for h in range(HFN):
                r.acquire("tensor", f"pb{p * 4 + 2 + h}")
                for c in range(DC):
                    r.op("tensor", "matmul", pb[p * 4 + 2 + h][:, :], wb_tile(c),
                         xb[:, c * T + h * HF: c * T + (h + 1) * HF],
                         start=(c == 0), stop=(c == DC - 1), inc="PE" if c == DC - 1 else None)
                v_bp_g[h] = r.cnt["PE"]
            r.consume(f"wdtb{p}", "PE", v_dt_g[1])
            r.consume(f"wbb{p}", "PE", v_bp_g[1])
            if j + 2 < DC:
                r.acquire("gpsimd", f"wdtb{p}")
                wload[("wdt", j + 2)] = r.dma("gpsimd", wdtb_t[:, p * 2048:(p + 1) * 2048], d_wdt[j + 2], f"WDT{p}")
                r.acquire("gpsimd", f"wbb{p}")
                wload[("wb", j + 2)] = r.dma("gpsimd", wbb_t[:, p * 2048:(p + 1) * 2048], d_wb[j + 2], f"WBB{p}")

            v_sp = {}
            for h in range(HFN):
                bank = pb[p * 4 + h]
                hs = slice(h * HF, (h + 1) * HF)
                r.acquire("scalar", f"sp_lr{p}_{h}")
                r.acquire("scalar", f"sp_ae{p}_{h}")
                r.wait("scalar", "PE", v_dt_g[h])
                r.op("scalar", "activation", s_r[:, hs], bank[:, :], AF.Relu, bias=bdt_t[:, j:j + 1])
                v_abs = r.op("scalar", "activation", s_a[:, hs], bank[:, :], AF.Abs,
                             bias=bdt_t[:, j:j + 1], inc="ACT")
                r.consume(f"pb{p * 4 + h}", "ACT", v_abs)
                r.op("scalar", "activation", s_e[:, hs], s_a[:, hs], AF.Exp, scale=-1.0)
                v_sp[h] = r.op("scalar", "activation", s_l[:, hs], s_e[:, hs], AF.Ln, bias=1.0, inc="ACT")

            v_dt, v_u, v_abar = {}, {}, {}
            for h in range(HFN):
                hs = slice(h * HF, (h + 1) * HF)
                r.wait("vector", "ACT", v_sp[h])
                v_dt[h] = r.op("vector", "tensor_add", s_dt[:, hs], s_l[:, hs], s_r[:, hs], inc="DVE")
                r.wait("scalar", "DVE", v_dt[h])
                r.wait("scalar", "DVE", v_A)
                v_abar[h] = r.op("scalar", "activation", s_ab[:, hs], s_dt[:, hs], AF.Exp,
                                 scale=A_t[:, j:j + 1], inc="ACT")
                bank = pb[p * 4 + 2 + h]
                r.wait("vector", "PE", v_bp_g[h])
                v_t = r.op("vector", "tensor_mul", s_a[:, hs], s_dt[:, hs], bank[:, :], inc="DVE")
                r.consume(f"pb{p * 4 + 2 + h}", "DVE", v_t)
                r.waitv("vector", v_xs)
                v_u[h] = r.op("vector", "tensor_mul", s_e[:, hs], s_a[:, hs],
                              xs_t[:, p * T + h * HF: p * T + (h + 1) * HF], inc="DVE")
            r.consume(f"xs{p}", "DVE", v_u[1])

            r.acquire("vector", f"sp_hp{p}")
            r.wait("vector", "ACT", v_abar[0])
            r.wait("vector", "ACT", v_abar[1])
            r.wait("vector", "DVE", v_zero)
            r.op("vector", "tensor_tensor_scan", s_hl[:, :], s_ab[:, :], s_e[:, :], 0.0, ALU.mult, ALU.add)
            v_scan_p = r.op("vector", "tensor_tensor_scan", s_pq[:, :], s_ab[:, :], zeros, 1.0,
                            ALU.mult, ALU.add, inc="DVE")
            v_scan_last = v_scan_p
            r.consume(f"sp_lr{p}_0", "DVE", v_dt[0])
            r.consume(f"sp_lr{p}_1", "DVE", v_dt[1])
            r.consume(f"sp_ae{p}_0", "DVE", v_scan_p)
            r.consume(f"sp_ae{p}_1", "DVE", v_scan_p)

            r.wait("gpsimd", "DVE", v_scan_p)
            r.op("gpsimd", "tensor_copy", hb[:, j * T:(j + 1) * T], s_hl[:, :])
            r.op("gpsimd", "tensor_copy", hb[:, (DC + j) * T:(DC + j + 1) * T], s_pq[:, :])
            r.op("gpsimd", "tensor_copy", hP_t[:, j:j + 1], s_hl[:, T - 1:T])
            v_pl = r.op("gpsimd", "tensor_copy", hP_t[:, 16 + j:17 + j], s_pq[:, T - 1:T], inc="PL")
            r.consume(f"sp_hp{p}", "PL", v_pl)

        r.consume("sil0", "DVE", v_scan_last)
        r.consume("sil1", "DVE", v_scan_last)
        TAP("hP", hP_t[:, :])
        TAP("slots", U2[:, :])
        TAP("hbA", hb[:, :])
        TAP("xbA", xb)
        TAP("Acol", A_t[:, :])

        if upto == "A1":
            return _finish(r, sems, tap_specs, nc, block), tap_specs

        r.wait("gpsimd", "PL", r.cnt["PL"])
        v_cci = r.dma("gpsimd", d_ccin, hP_t[:, :], "CCS")
        r.waitv("gpsimd", v_cci)
        r.op("gpsimd", "collective_compute", "AllGather", ALU.bypass,
             replica_groups=[list(range(NCORE))], ins=[d_ccin], outs=[d_ccout], inc="CC")
        r.wait("gpsimd", "CC", 1)
        v_carr = r.dma("gpsimd", carr_t[:, :].rearrange("p (k c) -> p k c", k=NCORE),
                       d_ccout.rearrange("(k p) c -> p k c", p=128), "CCS")

        r.waitv("vector", v_carr)
        r.waitv("vector", v_smalls)

        def dve_step(attr, *a, **k):
            v = r.op("vector", attr, *a, inc="DVE", **k)
            r.wait("vector", "DVE", v)
            return v

        dve_step("memset", cvec_t[:, :], 0.0)
        for k in range(NCORE - 1):
            hsl = carr_t[:, k * 32:k * 32 + 16]
            psl = carr_t[:, k * 32 + 16:k * 32 + 32]
            dve_step("tensor_mul", ct1_t[:, :], psl, cvec_t[:, :])
            dve_step("tensor_add", ct1_t[:, :], ct1_t[:, :], hsl)
            dve_step("tensor_sub", ct1_t[:, :], ct1_t[:, :], cvec_t[:, :])
            dve_step("tensor_single_scalar", ct1_t[:, :], ct1_t[:, :], sel_t[:, k:k + 1], ALU.mult)
            dve_step("tensor_add", cvec_t[:, :], cvec_t[:, :], ct1_t[:, :])
        v_cvec = r.cnt["DVE"]
        TAP("cvec", cvec_t[:, :])
        TAP("carr", carr_t[:, :])

        for m0 in range(2):
            r.acquire("gpsimd", f"wdtb{m0 % 2}")
            wload[("wc", m0)] = r.dma("gpsimd", wdtb_t[:, (m0 % 2) * 2048:(m0 % 2 + 1) * 2048],
                                      d_wc[m0], f"WDT{m0 % 2}")

        corrb = U2[:, 8 * T:(8 + 1) * T].bitcast(BF16)
        r.wait("scalar", "DVE", v_cvec)
        r.wait("scalar", "PL", r.cnt["PL"])
        r.wait("vector", "DVE", v_cvec)
        r.wait("gpsimd", "DVE", v_cvec)
        v_fix_c = {}
        v_corr_prev = {}
        for j in range(DC):
            pq_bf = hb[:, (DC + j) * T:(DC + j + 1) * T]
            cslot = corrb[:, (j % 2) * T:(j % 2 + 1) * T]
            if j - 2 in v_fix_c:
                r.waitv("scalar", v_fix_c[j - 2])
            v_corr = r.op("scalar", "activation", cslot, pq_bf, AF.Copy,
                          scale=cvec_t[:, j:j + 1], inc="ACT")
            eng, sem = ("vector", "DVE") if j % 2 == 0 else ("gpsimd", "PL")
            r.wait(eng, "ACT", v_corr)
            v = r.op(eng, "tensor_add", hb[:, j * T:(j + 1) * T],
                     hb[:, j * T:(j + 1) * T], cslot, inc=sem)
            v_fix_c[j] = (sem, v)
        v_fix = v_fix_c[DC - 2]
        v_fix_pl = v_fix_c[DC - 1]
        TAP("hb", hb[:, 0:DC * T])

        if upto == "fix":
            return _finish(r, sems, tap_specs, nc, block), tap_specs

        v_r1 = {}
        last_sq_pe1 = {0: 0, 1: 0}
        silb = sil_t.bitcast(BF16)[:, 0:T]
        for m in range(DC):
            p = m % 2
            wc_tile = lambda c, p=p: wdtb_t[:, p * 2048 + c * 128: p * 2048 + (c + 1) * 128]
            r.acquire("sync", f"xs{p}")
            v_xs = r.dma("sync", xs_t[:, p * T:(p + 1) * T], d_xfm[m * 128:(m + 1) * 128, :], f"XS{p}")
            r.waitv("tensor", wload[("wc", m)])
            v_mix = {}
            for h in range(HFN):
                bk = p * 2 + h
                r.acquire("tensor", f"pb{bk}")
                for c in range(DC):
                    r.waitv("tensor", v_fix_c[c])
                    r.op("tensor", "matmul", pb[bk][:, :], wc_tile(c),
                         hb[:, c * T + h * HF: c * T + (h + 1) * HF],
                         start=(c == 0), stop=(c == DC - 1), inc="PE" if c == DC - 1 else None)
                v_mix[h] = r.cnt["PE"]
            r.consume(f"wdtb{p}", "PE", v_mix[1])
            if m + 2 < DC:
                r.acquire("gpsimd", f"wdtb{p}")
                wload[("wc", m + 2)] = r.dma("gpsimd", wdtb_t[:, p * 2048:(p + 1) * 2048], d_wc[m + 2], f"WDT{p}")
            v = None
            for h in range(HFN):
                bk = p * 2 + h
                r.wait("vector", "PE", v_mix[h])
                r.waitv("vector", v_xs)
                r.wait("vector", "DVE", v_ds1)
                v = r.op("vector", "scalar_tensor_tensor",
                         U2[:, m * T + h * HF: m * T + (h + 1) * HF],
                         xs_t[:, p * T + h * HF: p * T + (h + 1) * HF],
                         ds1_t[:, m:m + 1], pb[bk][:, :], ALU.mult, ALU.add, inc="DVE")
                r.consume(f"pb{bk}", "DVE", v)
            r.consume(f"xs{p}", "DVE", v)
            v_r1[m] = v
            for h in range(HFN):
                r.acquire("scalar", f"sil{h}")
                r.wait("scalar", "DVE", v_r1[m])
                if last_sq_pe1[h]:
                    r.wait("scalar", "PE", last_sq_pe1[h])
                v_sq = r.op("scalar", "activation", silb[:, h * HF:(h + 1) * HF],
                            U2[:, m * T + h * HF: m * T + (h + 1) * HF], AF.Square, inc="ACT")
                r.wait("tensor", "ACT", v_sq)
                r.waitv("tensor", v_seb)
                if m == 0:
                    r.acquire("tensor", f"pb{4 + h}")
                    r.acquire("tensor", f"pb{6 + h}")
                r.op("tensor", "matmul", pb[4 + h][0:1, :], onesc_t[:, :],
                     U2[:, m * T + h * HF: m * T + (h + 1) * HF],
                     start=(m == 0), stop=(m == DC - 1))
                last_sq_pe1[h] = r.op("tensor", "matmul", pb[6 + h][0:1, :], onescb_t[:, :],
                                      silb[:, h * HF:(h + 1) * HF],
                                      start=(m == 0), stop=(m == DC - 1), inc="PE")
        for h in range(HFN):
            r.consume(f"sil{h}", "PE", last_sq_pe1[h])
        TAP("r1", U2[:, :])

        if upto == "A2":
            return _finish(r, sems, tap_specs, nc, block), tap_specs

        def emit_ln_stats_chunk(src_fn, v_src, m, last_sq_pe):
            for h in range(HFN):
                r.acquire("scalar", f"sil{h}")
                r.wait("scalar", "DVE", v_src)
                if last_sq_pe[h]:
                    r.wait("scalar", "PE", last_sq_pe[h])
                v_sq = r.op("scalar", "activation", silb[:, h * HF:(h + 1) * HF],
                            src_fn(m, h), AF.Square, inc="ACT")
                r.wait("tensor", "ACT", v_sq)
                if m == 0:
                    r.acquire("tensor", f"pb{4 + h}")
                    r.acquire("tensor", f"pb{6 + h}")
                r.op("tensor", "matmul", pb[4 + h][0:1, :], onesc_t[:, :], src_fn(m, h),
                     start=(m == 0), stop=(m == DC - 1))
                last_sq_pe[h] = r.op("tensor", "matmul", pb[6 + h][0:1, :], onescb_t[:, :],
                                     silb[:, h * HF:(h + 1) * HF],
                                     start=(m == 0), stop=(m == DC - 1), inc="PE")

        def emit_ln(src_fn, v_src_fn):
            last_sq_pe = {0: 0, 1: 0}
            for m in range(DC):
                emit_ln_stats_chunk(src_fn, v_src_fn(m), m, last_sq_pe)
            v_stats = r.cnt["PE"]
            for h in range(HFN):
                r.consume(f"sil{h}", "PE", last_sq_pe[h])
            return emit_ln_rows(v_stats)
        def emit_ln_rows(v_stats=None):
            if v_stats is None:
                v_stats = r.cnt["PE"]
            mrow = rows8_t[0:1, 0:1024]
            rrow = rows8_t[0:1, 1024:2048]
            r.acquire("vector", "rows8_0")
            r.acquire("vector", "rows8_1")
            r.wait("vector", "PE", v_stats)

            def row_step(attr, *a, **k):
                v = r.op("vector", attr, *a, inc="DVE", **k)
                r.wait("vector", "DVE", v)
                return v

            for h in range(HFN):
                hs = slice(h * HF, (h + 1) * HF)
                row_step("tensor_scalar_mul", mrow[:, hs], pb[4 + h][0:1, :], 1.0 / 2048.0)
            row_step("tensor_scalar_mul", rrow[:, 0:HF], pb[6][0:1, :], 1.0 / 2048.0)
            row_step("tensor_scalar_mul", rrow[:, HF:2 * HF], pb[7][0:1, :], 1.0 / 2048.0)
            row_step("tensor_mul", srow_t[0:1, 0:1024], mrow[:, :], mrow[:, :])
            v_var = row_step("tensor_sub", rrow[:, :], rrow[:, :], srow_t[0:1, 0:1024])
            for i in range(4, 8):
                r.consume(f"pb{i}", "DVE", v_var)
            r.wait("scalar", "DVE", v_var)
            v_ln = r.op("scalar", "activation", srow_t[0:1, 0:1024], rrow[:, :], AF.Ln,
                        bias=cst_t[0:1, 0:1], inc="ACT")
            r.wait("scalar", "ACT", v_ln)
            v_rstd = r.op("scalar", "activation", rrow[:, :], srow_t[0:1, 0:1024], AF.Exp,
                          scale=cst_t[0:1, 1:2], inc="ACT")
            r.wait("tensor", "DVE", v_var)
            r.wait("tensor", "ACT", v_rstd)
            v_bc = None
            for h in range(HFN):
                hs = slice(h * HF, (h + 1) * HF)
                r.acquire("tensor", f"pb{4 + h}")
                r.acquire("tensor", f"pb{6 + h}")
                r.op("tensor", "matmul", pb[4 + h][:, :], ones1_t[:, :], mrow[:, hs],
                     start=True, stop=True)
                v_bc = r.op("tensor", "matmul", pb[6 + h][:, :], ones1_t[:, :], rrow[:, hs],
                            start=True, stop=True, inc="PE")
            r.consume("rows8_0", "PE", v_bc)
            r.consume("rows8_1", "PE", v_bc)
            return v_bc

        v_bc1 = emit_ln_rows()
        v_spill = {}
        v_cast_m = {}
        v_norm1 = None
        for m in range(DC):
            v = None
            for h in range(HFN):
                sl = U2[:, m * T + h * HF: m * T + (h + 1) * HF]
                r.wait("vector", "PE", v_bc1)
                r.acquire("vector", f"sil{h}")
                tb = sil_t[:, h * HF:(h + 1) * HF]
                r.op("vector", "tensor_sub", tb, sl, pb[4 + h][:, :])
                v = r.op("vector", "tensor_mul", sl, tb, pb[6 + h][:, :], inc="DVE")
                r.consume(f"sil{h}", "DVE", v)
            v_norm1 = v
            ceng = ("scalar", "gpsimd", "vector")[m % 3]
            r.wait(ceng, "DVE", v)
            if ceng == "scalar":
                v_cast_m[m] = ("ACT", r.op("scalar", "activation", xb_t[:, m * T:(m + 1) * T],
                                           U2[:, m * T:(m + 1) * T], AF.Copy, inc="ACT"))
            elif ceng == "gpsimd":
                v_cast_m[m] = ("PL", r.op("gpsimd", "tensor_copy", xb_t[:, m * T:(m + 1) * T],
                                          U2[:, m * T:(m + 1) * T], inc="PL"))
            else:
                v_cast_m[m] = ("DVE", r.op("vector", "tensor_copy", xb_t[:, m * T:(m + 1) * T],
                                           U2[:, m * T:(m + 1) * T], inc="DVE"))
            r.wait("sync", "DVE", v)
            v_spill[m] = r.dma("sync", d_r1d[:, m * T:(m + 1) * T], U2[:, m * T:(m + 1) * T], "SP")
        for i in range(4, 8):
            r.consume(f"pb{i}", "DVE", v_norm1)
        v_x1_pl = r.cnt["PL"]
        v_spill_all = v_spill[DC - 1]
        TAP("x1", U2[:, :])

        if upto == "ln1":
            return _finish(r, sems, tap_specs, nc, block), tap_specs

        r.waitv("tensor", v_seb)
        for h in range(HFN):
            r.acquire("tensor", f"pb{h}")
            for c in range(DC):
                r.waitv("tensor", v_cast_m[c])
                r.op("tensor", "matmul", pb[h][0:16, :], rwb_t[:, c * 16:(c + 1) * 16],
                     xb[:, c * T + h * HF: c * T + (h + 1) * HF],
                     start=(c == 0), stop=(c == DC - 1), inc="PE" if c == DC - 1 else None)
        v_logit = r.cnt["PE"]
        r.acquire("scalar", "xs0")
        r.acquire("scalar", "xs1")
        r.wait("scalar", "PE", v_logit)
        r.op("scalar", "copy", lg_s[:, 0:HF], pb[0][0:16, :])
        v_lg = r.op("scalar", "copy", lg_s[:, HF:2 * HF], pb[1][0:16, :], inc="ACT")
        r.consume("pb0", "ACT", v_lg)
        r.consume("pb1", "ACT", v_lg)
        r.wait("tensor", "ACT", v_lg)
        r.acquire("tensor", "pb2")
        for b in range(8):
            r.op("tensor", "transpose", pb[2][:, b * 16:(b + 1) * 16],
                 lg_s[:, b * 128:(b + 1) * 128], idn_t[0:16, 0:16],
                 inc="PE" if b == 7 else None)
        v_tp = r.cnt["PE"]
        r.wait("scalar", "PE", v_tp)
        v_probs = r.op("scalar", "copy", probs_t[:, :], pb[2][:, 0:128], inc="ACT")
        r.consume("pb2", "ACT", v_probs)
        r.wait("scalar", "ACT", v_probs)
        v_e = r.op("scalar", "activation", pr2_t[:, :], probs_t[:, :], AF.Exp, inc="ACT")
        r.wait("vector", "ACT", v_e)
        for b in range(8):
            v_s = r.op("vector", "reduce_sum", ssum8_t[:, b:b + 1], pr2_t[:, b * 16:(b + 1) * 16],
                       AX.X, inc="DVE")
        r.wait("vector", "DVE", v_s)
        v_rc = r.op("vector", "reciprocal", rcp8_t[:, :], ssum8_t[:, :], inc="DVE")
        r.wait("vector", "DVE", v_rc)
        for b in range(8):
            v_wt = r.op("vector", "tensor_single_scalar", wtm_t[:, b * 16:(b + 1) * 16],
                        pr2_t[:, b * 16:(b + 1) * 16], rcp8_t[:, b:b + 1], ALU.mult, inc="DVE")
        r.wait("vector", "DVE", v_wt)
        for b in range(8):
            v_m8 = r.op("vector", "max", out=m8_8_t[:, b * 8:(b + 1) * 8],
                        in_=wtm_t[:, b * 16:(b + 1) * 16], inc="DVE")
        r.wait("vector", "DVE", v_m8)
        for b in range(8):
            v_mk = r.op("vector", "tensor_single_scalar", msk_t[:, b * 16:(b + 1) * 16],
                        wtm_t[:, b * 16:(b + 1) * 16], m8_8_t[:, b * 8 + 3:b * 8 + 4],
                        ALU.is_ge, inc="DVE")
        r.wait("vector", "DVE", v_mk)
        for b in range(8):
            v_w = r.op("vector", "tensor_mul", pr2_t[:, b * 16:(b + 1) * 16],
                       msk_t[:, b * 16:(b + 1) * 16], wtm_t[:, b * 16:(b + 1) * 16], inc="DVE")
        r.wait("vector", "DVE", v_w)
        TAP("wdense_tm", pr2_t[:, :])
        r.wait("tensor", "DVE", v_w)
        r.acquire("tensor", "pb3")
        r.acquire("tensor", "pb4")
        for b in range(8):
            bk = 3 if b < 4 else 4
            r.op("tensor", "transpose", pb[bk][0:16, (b % 4) * 128:(b % 4 + 1) * 128],
                 pr2_t[:, b * 16:(b + 1) * 16], idn_t[:, :], inc="PE" if b == 7 else None)
        v_wt = r.cnt["PE"]
        r.wait("scalar", "PE", v_wt)
        r.op("scalar", "copy", wfm[:, 0:HF], pb[3][0:16, 0:512])
        v_wfm = r.op("scalar", "copy", wfm[:, HF:2 * HF], pb[4][0:16, 0:512], inc="ACT")
        r.consume("pb3", "ACT", v_wfm)
        r.consume("pb4", "ACT", v_wfm)

        r.acquire("tensor", "pb5")
        r.acquire("tensor", "pb2")
        for h in range(HFN):
            bank = pb[5] if h == 0 else pb[2]
            for c in range(DC):
                r.op("tensor", "matmul", bank[0:1, :], seb_t[:, c:c + 1],
                     xb[:, c * T + h * HF: c * T + (h + 1) * HF],
                     start=(c == 0), stop=(c == DC - 1), inc="PE" if c == DC - 1 else None)
        v_se = r.cnt["PE"]
        r.wait("scalar", "PE", v_se)
        r.op("scalar", "activation", srow_t[0:1, 0:HF], pb[5][0:1, :], AF.Exp, scale=-1.0)
        v_sex = r.op("scalar", "activation", srow_t[0:1, HF:2 * HF], pb[2][0:1, :], AF.Exp,
                     scale=-1.0, inc="ACT")
        r.consume("pb5", "ACT", v_sex)
        r.consume("pb2", "ACT", v_sex)
        r.wait("vector", "ACT", v_sex)
        v_sa = r.op("vector", "tensor_scalar_add", srow_t[0:1, :], srow_t[0:1, :], 1.0, inc="DVE")
        r.wait("vector", "DVE", v_sa)
        v_sig = r.op("vector", "reciprocal", srow_t[0:1, :], srow_t[0:1, :], inc="DVE")
        r.wait("vector", "DVE", v_sig)
        r.wait("tensor", "DVE", v_sig)
        r.acquire("tensor", "pb6")
        r.acquire("tensor", "pb7")
        r.op("tensor", "matmul", pb[6][:, :], ones1_t[:, :], srow_t[0:1, 0:HF], start=True, stop=True)
        v_sbc_pe = r.op("tensor", "matmul", pb[7][:, :], ones1_t[:, :], srow_t[0:1, HF:2 * HF],
                        start=True, stop=True, inc="PE")
        r.wait("scalar", "PE", v_sbc_pe)
        r.waitv("scalar", v_spill_all)
        r.op("scalar", "copy", s_bc[:, 0:HF], pb[6][:, :])
        v_sbc = r.op("scalar", "copy", s_bc[:, HF:2 * HF], pb[7][:, :], inc="ACT")
        r.consume("pb6", "ACT", v_sbc)
        r.consume("pb7", "ACT", v_sbc)
        TAP("s_bc", s_bc)

        if upto == "router":
            return _finish(r, sems, tap_specs, nc, block), tap_specs

        he = lambda c, h: heb[:, c * T + h * HF: c * T + (h + 1) * HF]
        dwb = lambda q: heb[:, 11264 + q * 1408: 11264 + (q + 1) * 1408]
        gwb = lambda q: wdtb_t[:, q * 2048:(q + 1) * 2048]
        uwb = lambda q: wbb_t[:, q * 2048:(q + 1) * 2048]

        jobs = [("sg", g) for g in range(SGN)] + [("ex", e) for e in range(NE)]
        gu_load, dw_load = {}, {}

        r.waitv("gpsimd", v_spill_all)
        r.waitv("vector", v_spill_all)

        def gu_dma(jkind, jid, m):
            q = m % 2
            src_g = d_sgw[jid, m] if jkind == "sg" else d_gw[jid, m]
            src_u = d_suw[jid, m] if jkind == "sg" else d_uw[jid, m]
            r.acquire("gpsimd", f"wdtb{q}")
            gu_load[("g", jkind, jid, m)] = r.dma("gpsimd", gwb(q), src_g, f"GW{q}")
            r.acquire("gpsimd", f"wbb{q}")
            gu_load[("u", jkind, jid, m)] = r.dma("gpsimd", uwb(q), src_u, f"UW{q}")

        def dw_dma(jkind, jid, m2):
            q = m2 % 2
            src = d_sdw[m2][:, jid * 1408:(jid + 1) * 1408] if jkind == "sg" else d_dw[jid, m2]
            r.acquire("gpsimd", f"dwb{q}")
            dw_load[(jkind, jid, m2)] = r.dma("gpsimd", dwb(q), src, f"DW{q}")

        gu_dma(*jobs[0], 0)
        gu_dma(*jobs[0], 1)

        v_acc = {}
        for ji, (jkind, jid) in enumerate(jobs):
            if jkind == "ex":
                e = jid
                sslot = e % 2
                r.acquire("sync", f"rows8_{sslot}")
                r.wait("sync", "ACT", v_wfm)
                v_wrow = r.dma("sync", rows8_t[0:1, sslot * 1024:sslot * 1024 + T],
                               wfm[e:e + 1, 0:T], f"WR{sslot}")
                r.waitv("tensor", v_wrow)
                r.acquire("tensor", "pb6")
                r.acquire("tensor", "pb7")
                r.op("tensor", "matmul", pb[6][:, :], ones1_t[:, :],
                     rows8_t[0:1, sslot * 1024:sslot * 1024 + HF], start=True, stop=True)
                v_pe_bc = r.op("tensor", "matmul", pb[7][:, :], ones1_t[:, :],
                               rows8_t[0:1, sslot * 1024 + HF:sslot * 1024 + T],
                               start=True, stop=True, inc="PE")
                r.consume(f"rows8_{sslot}", "PE", v_pe_bc)
                r.wait("scalar", "PE", v_pe_bc)
                r.acquire("scalar", "w_bc")
                r.op("scalar", "copy", w_bc[:, 0:HF], pb[6][:, :])
                v_wbc = r.op("scalar", "copy", w_bc[:, HF:2 * HF], pb[7][:, :], inc="ACT")
                r.consume("pb6", "ACT", v_wbc)
                r.consume("pb7", "ACT", v_wbc)
                scale_tile, v_scale = w_bc, ("ACT", v_wbc)
            else:
                scale_tile, v_scale = s_bc, ("ACT", v_sbc)

            v_gu_last = None
            v_gu = {}
            for m in range(IC):
                q = m % 2
                r.waitv("tensor", gu_load[("g", jkind, jid, m)])
                r.waitv("tensor", gu_load[("u", jkind, jid, m)])
                for h in range(HFN):
                    r.acquire("tensor", f"pb{h}")
                    for c in range(DC):
                        r.op("tensor", "matmul", pb[h][:, :], gwb(q)[:, c * 128:(c + 1) * 128],
                             xb[:, c * T + h * HF: c * T + (h + 1) * HF],
                             start=(c == 0), stop=(c == DC - 1), inc="PE" if c == DC - 1 else None)
                    v_g = r.cnt["PE"]
                    r.acquire("tensor", f"pb{2 + h}")
                    for c in range(DC):
                        r.op("tensor", "matmul", pb[2 + h][:, :], uwb(q)[:, c * 128:(c + 1) * 128],
                             xb[:, c * T + h * HF: c * T + (h + 1) * HF],
                             start=(c == 0), stop=(c == DC - 1), inc="PE" if c == DC - 1 else None)
                    v_upe = r.cnt["PE"]
                    if h == 1:
                        r.consume(f"wdtb{q}", "PE", v_upe)
                        r.consume(f"wbb{q}", "PE", v_upe)
                    r.acquire("scalar", f"sil{h}")
                    r.wait("scalar", "PE", v_g)
                    v_sil = r.op("scalar", "activation", sil_t[:, h * HF:(h + 1) * HF],
                                 pb[h][:, :], AF.Silu, inc="ACT")
                    r.consume(f"pb{h}", "ACT", v_sil)
                    r.wait("vector", "ACT", v_sil)
                    r.wait("vector", "PE", v_upe)
                    v_h = r.op("vector", "tensor_mul", he(m, h), sil_t[:, h * HF:(h + 1) * HF],
                               pb[2 + h][:, :], inc="DVE")
                    r.consume(f"pb{2 + h}", "DVE", v_h)
                    r.consume(f"sil{h}", "DVE", v_h)
                    v_gu[(m, h)] = v_h
                    v_gu_last = v_h
                if m + 2 < IC:
                    gu_dma(jkind, jid, m + 2)
            dw_dma(jkind, jid, 0)
            dw_dma(jkind, jid, 1)
            if ji + 1 < len(jobs):
                njk, njid = jobs[ji + 1]
                gu_dma(njk, njid, 0)
                gu_dma(njk, njid, 1)
            for m2 in range(DC):
                q = m2 % 2
                r.waitv("tensor", dw_load[(jkind, jid, m2)])
                for h in range(HFN):
                    bk = 4 + h
                    r.acquire("tensor", f"pb{bk}")
                    for c in range(IC):
                        r.wait("tensor", "DVE", v_gu[(c, h)])
                        r.op("tensor", "matmul", pb[bk][:, :], dwb(q)[:, c * 128:(c + 1) * 128],
                             he(c, h), start=(c == 0), stop=(c == IC - 1),
                             inc="PE" if c == IC - 1 else None)
                    v_d = r.cnt["PE"]
                    if h == 1:
                        r.consume(f"dwb{q}", "PE", v_d)
                    r.wait("vector", "PE", v_d)
                    r.wait("vector", v_scale[0], v_scale[1])
                    acc_sl = accv[:, m2 * T + h * HF: m2 * T + (h + 1) * HF]
                    ssl = scale_tile[:, h * HF:(h + 1) * HF]
                    if ji == 0:
                        v_a = r.op("vector", "tensor_mul", acc_sl, ssl, pb[bk][:, :], inc="DVE")
                    else:
                        r.acquire("vector", f"tscr{h}")
                        tb = tscr[:, h * HF:(h + 1) * HF]
                        r.op("vector", "tensor_mul", tb, ssl, pb[bk][:, :])
                        v_a = r.op("vector", "tensor_add", acc_sl, acc_sl, tb, inc="DVE")
                        r.consume(f"tscr{h}", "DVE", v_a)
                    r.consume(f"pb{bk}", "DVE", v_a)
                    v_acc[(m2, h)] = v_a
                if m2 + 2 < DC:
                    dw_dma(jkind, jid, m2 + 2)
            if jkind == "ex":
                r.consume("w_bc", "DVE", v_acc[(DC - 1, 1)])

        TAP("acc", accv[:, :])

        if upto == "moe":
            return _finish(r, sems, tap_specs, nc, block), tap_specs

        v_r2 = {}
        for m in range(DC):
            q = m % 2
            rbs = xs_t[:, q * T:(q + 1) * T]
            r.acquire("sync", f"rb{q}")
            v_back = r.dma("sync", rbs, d_r1d[:, m * T:(m + 1) * T], f"XS{q}")
            r.waitv("vector", v_back)
            r.wait("vector", "DVE", v_acc[(m, 1)])
            v = r.op("vector", "tensor_add", accv[:, m * T:(m + 1) * T],
                     accv[:, m * T:(m + 1) * T], rbs, inc="DVE")
            r.consume(f"rb{q}", "DVE", v)
            v_r2[m] = v

        v_bc2 = emit_ln(lambda m, h: accv[:, m * T + h * HF: m * T + (h + 1) * HF],
                        lambda m: v_r2[m])
        TAP("r2", accv[:, :])
        TAP("rows2", rows8_t[:, :])
        if upto == "ln2a":
            return _finish(r, sems, tap_specs, nc, block), tap_specs
        for m in range(DC):
            v = None
            for h in range(HFN):
                sl = accv[:, m * T + h * HF: m * T + (h + 1) * HF]
                r.wait("vector", "PE", v_bc2)
                r.acquire("vector", f"tscr{h}")
                tb = tscr[:, h * HF:(h + 1) * HF]
                r.op("vector", "tensor_sub", tb, sl, pb[4 + h][:, :])
                v = r.op("vector", "tensor_mul", sl, tb, pb[6 + h][:, :], inc="DVE")
                r.consume(f"tscr{h}", "DVE", v)
            r.wait("sync", "DVE", v)
            r.dma("sync", d_y[m * 128:(m + 1) * 128, :], accv[:, m * T:(m + 1) * T], "Y")

        return _finish(r, sems, tap_specs, nc, block), tap_specs


def _finish(r, sems, tap_specs, nc, block):
    if tap_specs:
        for name in r.cnt:
            if name != "Y":
                r.wait("sync", name, r.cnt[name])
        for name in list(tap_specs):
            ap = tap_specs[name]
            d = nc.dram_tensor(f"tap_{name}", list(ap.shape), ap.dtype, kind="ExternalOutput").ap()
            r.dma("sync", d, ap, "Y")
    r.wait("sync", "Y", r.cnt["Y"])
    r.wait("gpsimd", "Y", r.cnt["Y"])

    @block.sync
    def _(eng):
        r.replay(eng, "sync", sems)

    @block.tensor
    def _(eng):
        r.replay(eng, "tensor", sems)

    @block.scalar
    def _(eng):
        r.replay(eng, "scalar", sems)

    @block.vector
    def _(eng):
        r.replay(eng, "vector", sems)

    @block.gpsimd
    def _(eng):
        r.replay(eng, "gpsimd", sems)

    return nc


def _pack_weights(i):
    f32 = lambda a: np.ascontiguousarray(np.asarray(a, dtype=np.float32))
    C = np.ascontiguousarray
    W_dt = f32(i["L0_attn_W_dt"]); W_B = f32(i["L0_attn_W_B"]); W_C = f32(i["L0_attn_W_C"])
    gw = f32(i["L0_ffn_gate_w"]); uw = f32(i["L0_ffn_up_w"]); dwt = f32(i["L0_ffn_down_w"])
    sgw = f32(i["L0_ffn_shared_gate_w"]); suw = f32(i["L0_ffn_shared_up_w"])
    sdw = f32(i["L0_ffn_shared_down_w"]); rw = f32(i["L0_ffn_router_w"]); se = f32(i["L0_ffn_se_gate_w"])

    d = {}
    d["wdt"] = C(W_dt.reshape(16, 128, 16, 128).transpose(0, 3, 2, 1).reshape(16, 128, 2048))
    d["wb"] = C(W_B.reshape(16, 128, 16, 128).transpose(0, 3, 2, 1).reshape(16, 128, 2048))
    d["wc"] = C(W_C.reshape(16, 128, 16, 128).transpose(0, 3, 2, 1).reshape(16, 128, 2048))
    d["gw"] = C(gw.reshape(16, 11, 128, 16, 128).transpose(0, 1, 4, 3, 2).reshape(16, 11, 128, 2048))
    d["uw"] = C(uw.reshape(16, 11, 128, 16, 128).transpose(0, 1, 4, 3, 2).reshape(16, 11, 128, 2048))
    d["dw"] = C(dwt.reshape(16, 16, 128, 11, 128).transpose(0, 1, 4, 3, 2).reshape(16, 16, 128, 1408))
    d["sgw"] = C(sgw.reshape(44, 128, 16, 128).transpose(0, 3, 2, 1).reshape(4, 11, 128, 2048))
    d["suw"] = C(suw.reshape(44, 128, 16, 128).transpose(0, 3, 2, 1).reshape(4, 11, 128, 2048))
    d["sdw"] = C(sdw.reshape(16, 128, 44, 128).transpose(0, 3, 2, 1).reshape(16, 128, 5632))
    d["rw"] = C(rw.reshape(16, 16, 128).transpose(2, 1, 0).reshape(128, 256))
    d["sew"] = C(se.reshape(16, 128).T)
    v16 = lambda v: C(np.asarray(v, np.float32).reshape(16, 128).T)
    d["bdt"] = v16(i["L0_attn_b_dt"])
    d["alog"] = v16(i["L0_attn_A_log"])
    d["dskip"] = v16(i["L0_attn_D_skip"])
    d["idn"] = np.eye(128, dtype=np.float32)
    d["ones1"] = np.ones((1, 128), np.float32)
    d["onesc"] = np.ones((128, 1), np.float32)
    cst = np.zeros((128, 4), np.float32); cst[:, 0] = EPS; cst[:, 1] = -0.5
    d["cst"] = cst
    return d


def make_in_maps(inputs, shared=None):
    x = np.asarray(inputs["L0_attn_x"], dtype=np.float32)
    if shared is None:
        shared = _pack_weights(inputs)
    in_maps = []
    for k in range(NCORE):
        m = dict(shared)
        m["xfm"] = np.ascontiguousarray(x[0, k * T:(k + 1) * T, :].T)
        m["sel"] = np.ascontiguousarray(
            np.tile((np.arange(8) < k).astype(np.float32), (128, 1)))
        in_maps.append(m)
    return in_maps


def kernel(**inputs):
    nc, _ = build_program(upto="full", taps=())
    in_maps = make_in_maps(inputs)
    res = run_bass_kernel_spmd(nc, in_maps, core_ids=list(range(NCORE)))
    out = np.empty((1, 8192, 2048), dtype=np.float32)
    for k in range(NCORE):
        out[0, k * T:(k + 1) * T, :] = res.results[k]["y"].T
    return out
